# version 10
# baseline (speedup 1.0000x reference)
import contextlib
import numpy as np
import concourse.bass as bass
import concourse.mybir as mybir
from concourse.bass_utils import run_bass_kernel_spmd

F32 = mybir.dt.float32
BF16 = mybir.dt.bfloat16
I32 = mybir.dt.int32
AF = mybir.ActivationFunctionType
OP = mybir.AluOpType
AX = mybir.AxisListType

DM = 1024
NDB = 8
NST = 16
RK = 64
NEX = 8
NGT = 10
ERK = 128
Lc = 512
TOK = 4096
NCH = TOK // Lc
NQ = 4
NSUB = NQ * NDB + 6
NPC = Lc

P1_CUT = 8
P1_RUN = NCH
P2_RUN = NSUB
PHASE3 = True
AG_ON = True
DYN_ON = True
NPieces = TOK // NPC


class Prog:
    ENGINES = ("sync", "tensor", "scalar", "vector", "gpsimd")

    def __init__(self, nc):
        self.nc = nc
        self.ops = {e: [] for e in self.ENGINES}
        self.count = {}
        self.ecount = {e: 0 for e in self.ENGINES}
        self.sems = {}
        self.extra_sems = []

    def make_sems(self, E):
        for e in self.ENGINES:
            self.sems[e] = E(self.nc.semaphore(f"sem_{e}"))
            self.count[e] = 0
        for s in self.extra_sems:
            self.sems[s] = E(self.nc.semaphore(f"sem_{s}"))
            self.count[s] = 0

    def add(self, engine, fn, deps=(), inc=1, sem=None):
        sem = sem or engine
        self.count[sem] += inc
        self.ecount[engine] += 1
        ev = (sem, self.count[sem])
        self.ops[engine].append((fn, tuple(deps), inc, sem, self.ecount[engine]))
        if not hasattr(self, "evmap"):
            self.evmap = {}
        self.evmap[ev] = (engine, self.ecount[engine])
        return ev

    def eev(self, engine):
        return (engine, self.count[engine])

    def emit(self, block):
        sems = self.sems

        def run(eng_name):
            def body(eng):
                waited = {s: 0 for s in self.count}
                drained = 0
                dma_engine = eng_name == "sync"
                for fn, deps, inc, sem, eidx in self.ops[eng_name]:
                    need = {}
                    se_need = 0
                    for (de, dc) in deps:
                        if de == eng_name and not dma_engine:
                            se_need = max(se_need, self.evmap.get((de, dc), (de, 0))[1])
                            continue
                        need[de] = max(need.get(de, 0), dc)
                    for de, dc in need.items():
                        if waited[de] < dc:
                            eng.wait_ge(sems[de], dc)
                            waited[de] = dc
                    if se_need > drained:
                        eng.drain()
                        drained = self.ecount_at(eng_name, eidx)
                    instr = fn(eng)
                    if inc:
                        assert instr is not None
                        instr.then_inc(sems[sem], inc)
            return body

        block.sync(run("sync"))
        block.tensor(run("tensor"))
        block.scalar(run("scalar"))
        block.vector(run("vector"))
        block.gpsimd(run("gpsimd"))

    def ecount_at(self, eng, eidx):
        return eidx - 1


def build_program():
    nc = bass.Bass("TRN2", target_bir_lowering=False)

    dr = lambda n, s, k="ExternalInput", dt=F32: nc.dram_tensor(n, s, dt, kind=k).ap()
    x_h = dr("x", [NDB, 128, TOK])
    out_h = dr("out", [NDB, 128, TOK], "ExternalOutput")
    win_h = dr("w_in", [DM, 2 * NST])
    wdt1_h = dr("w_dt1", [DM, RK])
    wdt2_h = dr("w_dt2", [RK, DM])
    basew_h = dr("base_wT", [DM, DM])
    gate_h = dr("gate_wT", [DM, NGT])
    acat_h = dr("a_cat", [DM, ERK])
    bcat_h = dr("b_cat", [ERK, DM])
    a2_h = dr("a2_sb", [128, NSUB * NQ])
    d_h = dr("D_sb", [128, NDB])
    bin_h = dr("b_in", [1, 2 * NST])
    bdt1_h = dr("b_dt1", [1, RK])
    bdt2_h = dr("b_dt2", [1, DM])
    baseb_h = dr("base_b", [1, DM])
    onescol_h = dr("ones_col", [128, 1])
    onesrow_h = dr("ones_row", [1, 128])
    ident_h = dr("ident", [128, 128])
    expand_h = dr("expand16", [NEX, 128])
    sel16_h = dr("sel16", [48, NST * 128])
    fc_h = dr("fconsts", [128, 2])
    seltab_h = dr("seltab", [128, NSUB * 8])
    slottab_h = dr("slot_tbl", [1, NSUB + 1], dt=I32)
    ytab_h = dr("y_tbl", [1, NSUB], dt=I32)
    s_dtxn = dr("s_dtxn", [NDB + 1, 128, 2 * TOK], "Internal")
    s_xmid = dr("s_xmid", [NDB, 128, TOK], "Internal")
    s_y = dr("s_y", [NQ * NDB, 128, TOK], "Internal", dt=BF16)
    cc_in = dr("cc_in", [128, NQ], "Internal")
    cc_out = dr("cc_out", [8 * 128, NQ], "Internal")

    with contextlib.ExitStack() as stack:
        E = stack.enter_context
        sb = lambda n, s, dt=F32: E(nc.sbuf_tensor(n, s, dt))

        bb0 = sb("bb0", [128, 2 * TOK])
        bb1 = sb("bb1", [128, 2 * TOK])
        b_x = bb0[:, 0:TOK]
        b_xn = bb0[:, TOK:2 * TOK]
        b_dt = bb1[:, 0:TOK]
        b_y = bb1[:, TOK:2 * TOK]
        b_t1 = sb("b_t1", [128, TOK])
        b_t2 = sb("b_t2", [128, TOK])
        b_t3 = sb("b_t3", [128, TOK])
        b_t4 = sb("b_t4", [128, TOK], BF16)
        ybuf16 = sb("ybuf16", [128, TOK], BF16)
        wstage = sb("wstage", [128, 2048])
        xn16 = sb("xn16", [128, TOK], BF16)
        xsq16 = xn16
        x16 = b_t4
        bcpack = sb("bcpack", [32 + NST, TOK], BF16)
        bmf = bcpack[0:NST, :]
        cmf = bcpack[32:32 + NST, :]
        r_mean = sb("r_mean", [1, Lc])
        r_msq = sb("r_msq", [1, Lc])
        r_rstd = sb("r_rstd", [1, Lc])
        r_mean16 = sb("r_mean16", [1, Lc], BF16)
        r_rstd16 = sb("r_rstd16", [1, Lc], BF16)
        gm1 = sb("gm1", [128, 4])
        gnm1 = sb("gnm1", [128, 4])
        gm2 = sb("gm2", [128, 4])
        ge = sb("ge", [128, 64])
        ge2 = None
        goh1 = sb("goh1", [128, 64])
        goh1n = None
        ge2 = b_t2[:, 0:64]
        goh1n = b_t2[:, 64:128]
        gtw = sb("gtw", [128, 64])
        gden = sb("gden", [128, 4])
        giz = sb("giz", [128, 4])
        gdf = sb("gdf", [128, 4])
        ew8 = sb("ew8", [128, 32])
        logit_sb = None
        ewfm16 = sb("ewfm16", [NEX, Lc], BF16)
        ewexp16 = sb("ewexp16", [128, Lc], BF16)
        zw16 = sb("zw16", [128, Lc], BF16)
        p116 = sb("p116", [RK, Lc], BF16)
        win16 = sb("win16", [128, NDB * 2 * NST], BF16)
        wdt1_16 = sb("wdt1_16", [128, NDB * RK], BF16)
        wdt2_16 = sb("wdt2_16", [RK, DM], BF16)
        basew16 = sb("basew16", [128, NDB * DM], BF16)
        gate16 = sb("gate16", [128, NDB * NGT], BF16)
        acat16 = sb("acat16", [128, NDB * ERK], BF16)
        bcat16 = sb("bcat16", [128, DM], BF16)
        a_sb = sb("a_sb", [128, NSUB * NQ])
        ainv = sb("ainv", [128, NSUB * NQ])
        d_sb = sb("d_sb", [128, NDB])
        seltab = sb("seltab_s", [128, NSUB * 8], BF16)
        tabs = sb("tabs", [1, 2 * NSUB + 1], I32)
        stcols = sb("stcols", [128, NSUB * NQ])
        gathbuf = sb("gathbuf", [128, 2 * 8 * NQ])
        init4 = sb("init4", [128, NQ])
        nbias = 2 * NST + RK + DM + DM
        braws = b_t1[0:1, 0:nbias]
        blo_f = b_t3[0:1, 0:nbias]
        bhi = sb("bhi16", [1, nbias], BF16)
        blo = sb("blo16", [1, nbias], BF16)
        onescol16 = sb("onescol16", [128, 1], BF16)
        onesrow16 = sb("onesrow16", [1, 128], BF16)
        onesL16_t = sb("onesL16s", [1, Lc], BF16)
        onesL16 = onesL16_t[0:1, :]
        onesL32 = onesL16
        ident_sb = sb("ident_sb", [128, 128])
        expand16 = sb("expand16sb", [NEX, 128], BF16)
        sel16sb = sb("sel16sb", [48, NST * 128], BF16)
        fc_sb = sb("fc_sb", [128, 2])

        ps8 = E(nc.psum_tensor("ps8", [128, TOK], F32))
        ps_st = ps8[0:33, 0 * Lc:1 * Lc]
        ps_a = ps8[:, 1 * Lc:2 * Lc]
        ps_b = ps8[:, 2 * Lc:3 * Lc]
        ps_c = ps8[:, 3 * Lc:4 * Lc]
        ps_d = ps8[:, 4 * Lc:5 * Lc]

        pg = Prog(nc)
        pg.extra_sems = ["slot0", "slot1", "yst", "pub", "gath", "spill", "ost",
                         "ysw0", "ag", "spx", "spn", "spd"]
        pg.make_sems(E)

        OB_IN, OB_DT1, OB_DT2, OB_BB = 0, 2 * NST, 2 * NST + RK, 2 * NST + RK + DM

        def ld_dma(dst, src, deps=()):
            return pg.add("sync", lambda e, d=dst, s=src: e.dma_start(out=d, in_=s),
                          deps, inc=16)

        def vcast(dst, src, deps):
            return pg.add("vector", lambda e, d=dst, s=src: e.tensor_copy(d, s), deps)

        ld_dma(a_sb[:], a2_h)
        ld_dma(tabs[0:1, 0:NSUB + 1], slottab_h)
        ld_dma(tabs[0:1, NSUB + 1:2 * NSUB + 1], ytab_h)
        ld_dma(d_sb[:], d_h)
        ld_dma(ident_sb[:], ident_h)
        ld_dma(fc_sb[:], fc_h)
        ld_dma(braws[0:1, OB_IN:OB_IN + 2 * NST], bin_h)
        ld_dma(braws[0:1, OB_DT1:OB_DT1 + RK], bdt1_h)
        ld_dma(braws[0:1, OB_DT2:OB_DT2 + DM], bdt2_h)
        evA = ld_dma(braws[0:1, OB_BB:OB_BB + DM], baseb_h)
        pg.add("vector", lambda e: e.reciprocal(ainv[:], a_sb[:]), [evA])
        pg.add("vector", lambda e: e.memset(onesL16, 1.0), [])
        pg.add("vector", lambda e: e.memset(ge[:], 0.0), [])
        pg.add("vector", lambda e: e.memset(goh1[:], 0.0), [])
        pg.add("vector", lambda e: e.memset(stcols[:], 0.0), [])
        pg.add("vector", lambda e: e.memset(gathbuf[:], 0.0), [])
        ev = vcast(bhi[0:1, :], braws, [evA])
        ev = pg.add("vector", lambda e: e.tensor_sub(blo_f, braws, bhi[0:1, :]), [ev])
        ev_blo = vcast(blo[0:1, :], blo_f, [ev])
        pc = [ev_blo]

        def staged(dst, src, p_rows, p_cols):
            ev_l = ld_dma(wstage[0:p_rows, 0:p_cols], src, [pc[0]])
            pc[0] = vcast(dst, wstage[0:p_rows, 0:p_cols], [ev_l])

        staged(seltab[:], seltab_h, 128, NSUB * 8)
        staged(expand16[:], expand_h, NEX, 128)
        staged(sel16sb[0:48, :], sel16_h, 48, NST * 128)
        staged(onescol16[:], onescol_h, 128, 1)
        staged(onesrow16[:], onesrow_h, 1, 128)
        for k in range(NDB):
            staged(win16[:, k * 2 * NST:(k + 1) * 2 * NST],
                   win_h[k * 128:(k + 1) * 128, :], 128, 2 * NST)
        for k in range(NDB):
            staged(wdt1_16[:, k * RK:(k + 1) * RK],
                   wdt1_h[k * 128:(k + 1) * 128, :], 128, RK)
        for k in range(NDB):
            staged(basew16[:, k * DM:(k + 1) * DM],
                   basew_h[k * 128:(k + 1) * 128, :], 128, DM)
        for k in range(NDB):
            staged(gate16[:, k * NGT:(k + 1) * NGT],
                   gate_h[k * 128:(k + 1) * 128, :], 128, NGT)
        for k in range(NDB):
            staged(acat16[:, k * ERK:(k + 1) * ERK],
                   acat_h[k * 128:(k + 1) * 128, :], 128, ERK)
        staged(wdt2_16[:], wdt2_h, RK, DM)
        staged(bcat16[:], bcat_h, 128, DM)
        setup_done = ("vector", pg.count["vector"])

        def dbs(buf, db):
            return buf[:, db * Lc:(db + 1) * Lc]

        def layernorm(src, dst, dst16, in_evs, ps_guard, x16_free=(), sq_free=()):
            ev_c = pg.add("vector", lambda e: e.tensor_copy(x16[:], src[:]),
                          list(in_evs) + list(x16_free))
            ev_sq = pg.add("scalar", lambda e: e.activation(xsq16[:], x16[:], AF.Square),
                           [ev_c] + list(sq_free))

            def stats(eng):
                for k in range(NDB):
                    eng.matmul(ps_st[0:1, :], onescol16[:], dbs(x16, k),
                               start=(k == 0), stop=(k == NDB - 1))
                ii = None
                for k in range(NDB):
                    ii = eng.matmul(ps_st[32:33, :], onescol16[:], dbs(xsq16, k),
                                    start=(k == 0), stop=(k == NDB - 1))
                return ii
            ev_mm = pg.add("tensor", stats, [ev_c, ev_sq, setup_done])
            ev_r0a = pg.add("scalar", lambda e: e.copy(r_mean[:], ps_st[0:1, :]), [ev_mm])
            ev_r0 = pg.add("scalar", lambda e: e.copy(r_msq[:], ps_st[32:33, :]), [ev_r0a])
            ev_v1 = pg.add("vector", lambda e: e.tensor_mul(r_rstd[:], r_mean[:], r_mean[:]), [ev_r0])
            ev_v2 = pg.add("vector", lambda e: e.tensor_sub(r_msq[:], r_msq[:], r_rstd[:]), [ev_v1])
            ev_s1 = pg.add("scalar", lambda e: e.activation(r_msq[:], r_msq[:], AF.Ln, bias=fc_sb[0:1, 0:1]), [ev_v2])
            ev_s2 = pg.add("scalar", lambda e: e.activation(r_rstd[:], r_msq[:], AF.Exp, scale=fc_sb[0:1, 1:2]), [ev_s1])
            ev_m16 = pg.add("vector", lambda e: e.tensor_copy(r_mean16[:], r_mean[:]), [ev_r0])
            ev_m16b = pg.add("vector", lambda e: e.tensor_copy(r_rstd16[:], r_rstd[:]), [ev_s2])
            ev_rep = pg.add("tensor", lambda e: (
                e.matmul(ps_a[:, :], onesrow16[:], r_mean16[:], start=True, stop=True),
                e.matmul(ps_b[:, :], onesrow16[:], r_rstd16[:], start=True, stop=True))[-1],
                [ev_m16, ev_m16b] + list(ps_guard))
            evs = []
            for db in range(NDB):
                e1 = pg.add("vector", lambda e, db=db: e.tensor_sub(dbs(dst, db), dbs(src, db), ps_a[:, :]), [ev_rep])
                e2 = pg.add("vector", lambda e, db=db: e.tensor_mul(dbs(dst, db), dbs(dst, db), ps_b[:, :]), [e1])
                e3 = pg.add("vector", lambda e, db=db: e.tensor_copy(dbs(dst16, db), dbs(dst, db)), [e2])
                evs.append(e3)
            return evs, ev_mm

        def k1_into(eng, ps, rows_n, col_off, is_last):
            eng.matmul(ps[0:rows_n, :], bhi[0:1, col_off:col_off + rows_n],
                       onesL16, start=False, stop=False)
            return eng.matmul(ps[0:rows_n, :], blo[0:1, col_off:col_off + rows_n],
                              onesL32, start=False, stop=is_last)

        def ffn(xn_evs, x_buf, x_in_evs, ps_guard, psc_free=()):
            def gate_mm(eng):
                ii = None
                for k in range(NDB):
                    ii = eng.matmul(ps_c[0:NGT, :], gate16[:, k * NGT:(k + 1) * NGT],
                                    dbs(xn16, k), start=(k == 0), stop=(k == NDB - 1))
                return ii
            ev_g = pg.add("tensor", gate_mm, list(xn_evs) + list(psc_free))
            ev_ge = pg.add("scalar", lambda e: e.copy(b_y[0:NGT, 0:Lc], ps_c[0:NGT, :]), [ev_g])
            nblk = Lc // 128

            def transp(eng):
                ii = None
                for j in range(nblk):
                    ii = eng.transpose(ps_d[:, j * 16:j * 16 + NGT],
                                       b_y[0:NGT, j * 128:(j + 1) * 128],
                                       ident_sb[0:NGT, 0:NGT])
                return ii
            ev_t = pg.add("tensor", transp, [ev_ge, ev_g])

            def blkcols(buf, j, w=NGT):
                return buf[:, j * 16:j * 16 + w]
            for j in range(nblk):
                ev = pg.add("vector", lambda e, j=j: e.reduce_max(gm1[:, j:j + 1], blkcols(ps_d, j), axis=AX.X), [ev_t])
            for j in range(nblk):
                ev = pg.add("vector", lambda e, j=j: e.reduce_max(gnm1[:, j:j + 1], blkcols(ps_d, j), axis=AX.X, negate=True), [ev_t])
            ev_mx = ev
            evs_e = []
            for j in range(nblk):
                evs_e.append(pg.add("scalar", lambda e, j=j: e.activation(
                    blkcols(ge, j), blkcols(ps_d, j), AF.Exp, bias=gnm1[:, j:j + 1]), [ev_mx]))
            for j in range(nblk):
                ev = pg.add("vector", lambda e, j=j: e.tensor_scalar(
                    blkcols(goh1, j), blkcols(ps_d, j), gm1[:, j:j + 1], None, OP.is_equal),
                    [ev_mx] + evs_e)
            ev = pg.add("vector", lambda e: e.tensor_scalar(
                goh1n[:], goh1[:], -1.0, 1.0, OP.mult, OP.add), [ev])
            ev = pg.add("vector", lambda e: e.tensor_mul(ge2[:], ge[:], goh1n[:]), [ev] + evs_e)
            ev_e2 = ev
            for j in range(nblk):
                ev = pg.add("vector", lambda e, j=j: e.reduce_max(gm2[:, j:j + 1], blkcols(ge2, j), axis=AX.X), [ev_e2])
            ev_sp = pg.add("vector", lambda e: e.tensor_copy(giz[:], gm2[:]), [ev])
            for j in range(nblk):
                ev = pg.add("vector", lambda e, j=j: e.tensor_scalar(
                    blkcols(ge2, j), blkcols(ge2, j), gm2[:, j:j + 1], None, OP.is_equal), [ev_sp])
            for j in range(nblk):
                ev = pg.add("vector", lambda e, j=j: e.scalar_tensor_tensor(
                    blkcols(gtw, j), blkcols(ge2, j), gm2[:, j:j + 1], blkcols(goh1, j),
                    OP.mult, OP.add), [ev])
            for j in range(nblk):
                ev = pg.add("vector", lambda e, j=j: e.reduce_sum(
                    gden[:, j:j + 1], blkcols(gtw, j, NEX), axis=AX.X), [ev])
            ev = pg.add("vector", lambda e: e.tensor_single_scalar(giz[:], gden[:], 0.0, OP.is_equal), [ev])
            ev = pg.add("vector", lambda e: e.tensor_add(gdf[:], gden[:], giz[:]), [ev])
            ev = pg.add("vector", lambda e: e.reciprocal(gm2[:], gdf[:]), [ev])
            ev_sp2 = pg.add("vector", lambda e: e.tensor_copy(gdf[:], gm2[:]), [ev])
            evw = []
            for j in range(nblk):
                evw.append(pg.add("vector", lambda e, j=j: e.tensor_scalar(
                    ew8[:, j * 8:(j + 1) * 8], blkcols(gtw, j, NEX), gm2[:, j:j + 1], None,
                    OP.mult), [ev_sp2]))

            def transb(eng):
                ii = None
                for j in range(nblk):
                    ii = eng.transpose(ps_d[0:NEX, j * 128:(j + 1) * 128],
                                       ew8[:, j * 8:(j + 1) * 8], ident_sb[:, :])
                return ii
            ev_tb = pg.add("tensor", transb, evw)
            ev_efm = pg.add("scalar", lambda e: e.copy(ewfm16[:], ps_d[0:NEX, :]), [ev_tb])
            ev_ex = pg.add("tensor", lambda e: e.matmul(
                ps_d[:, :], expand16[:], ewfm16[:], start=True, stop=True), [ev_efm])
            ev_ex16 = pg.add("scalar", lambda e: e.copy(ewexp16[:], ps_d[:, :]), [ev_ex])

            def zmm(eng):
                ii = None
                for k in range(NDB):
                    ii = eng.matmul(ps_c[:, :], acat16[:, k * ERK:(k + 1) * ERK],
                                    dbs(xn16, k), start=(k == 0), stop=(k == NDB - 1))
                return ii
            ev_z = pg.add("tensor", zmm, list(xn_evs) + [ev_ge])
            ev_zw = pg.add("vector", lambda e: e.tensor_mul(zw16[:], ps_c[:, :], ewexp16[:]),
                           [ev_z, ev_ex16])
            x_evs = []
            prev_ep = {0: None, 1: None}
            for m in range(NDB):
                ps = ps_a if (m % 2 == 0) else ps_b

                def bmm(eng, m=m, ps=ps):
                    for k in range(NDB):
                        eng.matmul(ps[:, :], basew16[:, k * DM + m * 128:k * DM + (m + 1) * 128],
                                   dbs(xn16, k), start=(k == 0), stop=False)
                    eng.matmul(ps[:, :], bcat16[:, m * 128:(m + 1) * 128], zw16[:],
                               start=False, stop=False)
                    return k1_into(eng, ps, 128, OB_BB + m * 128, True)
                deps = list(xn_evs) + [ev_zw]
                if prev_ep[m % 2] is not None:
                    deps.append(prev_ep[m % 2])
                if m < 2:
                    deps += list(ps_guard)
                ev_b = pg.add("tensor", bmm, deps)
                ev_ep = pg.add("vector", lambda e, m=m, ps=ps: e.scalar_tensor_tensor(
                    dbs(x_buf, m), ps[:, :], 0.5, dbs(x_buf, m),
                    OP.mult, OP.add), [ev_b] + list(x_in_evs))
                prev_ep[m % 2] = ev_ep
                x_evs.append(ev_ep)
            return x_evs

        prev_reuse = [setup_done]
        free_bx = [setup_done] * NDB
        free_x16 = [setup_done]
        free_xn16 = [setup_done]
        free_psc = [setup_done]
        free_bdt = [setup_done]
        free_bxn = [setup_done]
        guard1 = [setup_done]
        for c in range(P1_RUN):
            c0 = c * Lc
            ld_evs = []
            for db in range(NDB):
                ld_evs.append(pg.add(
                    "sync", lambda e, db=db, c0=c0: e.dma_start(
                        out=dbs(b_x, db), in_=x_h[db, :, c0:c0 + Lc]),
                    [free_bx[db]], inc=16))
            x_ev = ld_evs[-1]
            if P1_CUT < 2:
                prev_reuse = ld_evs
                continue
            xn_evs, ev_mm1 = layernorm(b_x, b_xn, xn16, ld_evs, guard1,
                                       x16_free=free_x16, sq_free=free_xn16)
            if P1_CUT < 3:
                prev_reuse = xn_evs
                continue
            x_evs = ffn(xn_evs, b_x, ld_evs, [xn_evs[-1]], psc_free=free_psc)
            if P1_CUT < 4:
                prev_reuse = x_evs
                continue
            xn_evs2, ev_mm2 = layernorm(b_x, b_xn, xn16, [x_evs[-1]] + x_evs, x_evs)
            if P1_CUT < 5:
                prev_reuse = xn_evs2
                continue
            sp_evs = []
            sp_xmid = []
            for db in range(NDB):
                sp_xmid.append(pg.add(
                    "sync", lambda e, db=db, c0=c0: e.dma_start(
                        out=s_xmid[db, :, c0:c0 + Lc], in_=dbs(b_x, db)),
                    [x_evs[db], xn_evs2[-1]], inc=16, sem="spx"))
            sp_xn = []
            for db in range(NDB):
                sp_xn.append(pg.add(
                    "sync", lambda e, db=db, c0=c0: e.dma_start(
                        out=s_dtxn[db, :, TOK + c0:TOK + c0 + Lc], in_=dbs(b_xn, db)),
                    [xn_evs2[db]], inc=16, sem="spn"))
            sp_evs = sp_xmid + sp_xn

            def bcmm(eng):
                for k in range(NDB):
                    eng.matmul(ps_c[0:NST, :], win16[:, k * 2 * NST:k * 2 * NST + NST],
                               dbs(xn16, k), start=(k == 0), stop=False)
                eng.matmul(ps_c[0:NST, :], bhi[0:1, OB_IN:OB_IN + NST],
                           onesL16, start=False, stop=False)
                eng.matmul(ps_c[0:NST, :], blo[0:1, OB_IN:OB_IN + NST],
                           onesL32, start=False, stop=True)
                for k in range(NDB):
                    eng.matmul(ps_c[32:32 + NST, :],
                               win16[:, k * 2 * NST + NST:(k + 1) * 2 * NST],
                               dbs(xn16, k), start=(k == 0), stop=False)
                eng.matmul(ps_c[32:32 + NST, :], bhi[0:1, OB_IN + NST:OB_IN + 2 * NST],
                           onesL16, start=False, stop=False)
                return eng.matmul(ps_c[32:32 + NST, :], blo[0:1, OB_IN + NST:OB_IN + 2 * NST],
                                  onesL32, start=False, stop=True)
            if P1_CUT < 6:
                prev_reuse = sp_evs + [xn_evs2[-1]]
                continue
            ev_bc = pg.add("tensor", bcmm, list(xn_evs2))
            ev_bm = pg.add("scalar", lambda e, c0=c0: e.copy(bmf[:, c0:c0 + Lc], ps_c[0:NST, :]), [ev_bc])
            ev_cm = pg.add("scalar", lambda e, c0=c0: e.copy(cmf[:, c0:c0 + Lc], ps_c[32:32 + NST, :]), [ev_bc])

            def p1mm(eng):
                for k in range(NDB):
                    eng.matmul(ps_c[0:RK, :], wdt1_16[:, k * RK:(k + 1) * RK],
                               dbs(xn16, k), start=(k == 0), stop=False)
                return k1_into(eng, ps_c, RK, OB_DT1, True)
            if P1_CUT < 7:
                prev_reuse = sp_evs + [xn_evs2[-1], ev_bm, ev_cm]
                continue
            ev_p1 = pg.add("tensor", p1mm, list(xn_evs2) + [ev_bm, ev_cm])
            ev_p116 = pg.add("scalar", lambda e: e.copy(p116[:], ps_c[0:RK, :]), [ev_p1])
            exp_evs = []
            prev = ev_p116
            for db in range(NDB):
                def dtmm(eng, db=db):
                    eng.matmul(ps_c[:, :], wdt2_16[:, db * 128:(db + 1) * 128], p116[:],
                               start=True, stop=False)
                    return k1_into(eng, ps_c, 128, OB_DT2 + db * 128, True)
                ev_m = pg.add("tensor", dtmm, [ev_p116, prev])
                ev_e = pg.add("scalar", lambda e, db=db: e.activation(
                    dbs(b_y, db), ps_c[:, :], AF.Exp), [ev_m])
                prev = ev_e
                exp_evs.append(ev_e)
            if P1_CUT < 8:
                prev_reuse = sp_evs + [xn_evs2[-1], exp_evs[-1]]
                continue
            dt_sp = []
            cl_evs = []
            for db in range(NDB):
                ev_l = pg.add("scalar", lambda e, db=db: e.activation(
                    dbs(b_dt, db), dbs(b_y, db), AF.Ln, bias=1.0),
                    [exp_evs[db]] + free_bdt)
                ev_cl = pg.add("vector", lambda e, db=db: e.tensor_scalar(
                    dbs(b_dt, db), dbs(b_dt, db), 1e-7, 1e6, OP.max, OP.min), [ev_l])
                cl_evs.append(ev_cl)
                dt_sp.append(pg.add(
                    "sync", lambda e, db=db, c0=c0: e.dma_start(
                        out=s_dtxn[db, :, c0:c0 + Lc], in_=dbs(b_dt, db)),
                    [ev_cl], inc=16, sem="spd"))
            free_bx = [sp_xmid[-1]] * NDB
            free_x16 = [ev_mm2]
            free_xn16 = [ev_p1]
            free_psc = [exp_evs[-1]]
            free_bdt = [dt_sp[-1]]
            free_bxn = sp_xn
            guard1 = [sp_xn[-1], xn_evs2[-1]]
            prev_reuse = sp_evs + dt_sp + [xn_evs2[-1], cl_evs[-1], exp_evs[-1]]
        p1_done_s = ("sync", pg.count["sync"])
        p1_done_sp = ("spd", pg.count["spd"])
        p1_done_spx = ("spx", pg.count["spx"])
        p1_done_spn = ("spn", pg.count["spn"])
        p1_done_v = ("vector", pg.count["vector"])
        p1_done_sc = ("scalar", pg.count["scalar"])
        p1_done_t = ("tensor", pg.count["tensor"])

        abuf = [b_t1, b_t2]
        bbxb = b_t4
        htb = b_t3

        setup_g = [p1_done_s, p1_done_v, p1_done_sc, p1_done_t, p1_done_sp,
                   p1_done_spx, p1_done_spn]
        bbp = [bb0, bb1]

        _dyn_ctr = [0]

        def dyn_load(eng, tab_idx, dst, scratch):
            _dyn_ctr[0] += 1
            i = _dyn_ctr[0]
            with eng.register(f"ds{i}") as slot, eng.register(f"do{i}") as off:
                eng.reg_load(slot, tabs[0:1, tab_idx:tab_idx + 1])
                eng.reg_mul(off, slot, 128 * 2 * TOK)
                srcap = bass.AP(tensor=scratch.tensor, offset=off,
                                ap=[[2 * TOK, 128], [1, 2 * TOK]])
                return eng.dma_start(out=dst, in_=srcap)

        def dyn_store(eng, tab_idx, src_buf, scratch):
            _dyn_ctr[0] += 1
            i = _dyn_ctr[0]
            with eng.register(f"ds{i}") as slot, eng.register(f"do{i}") as off:
                eng.reg_load(slot, tabs[0:1, tab_idx:tab_idx + 1])
                eng.reg_mul(off, slot, 128 * TOK)
                dstap = bass.AP(tensor=scratch.tensor, offset=off,
                                ap=[[TOK, 128], [1, TOK]])
                return eng.dma_start(out=dstap, in_=src_buf)

        def load_engine(T):
            return "sync" if T < 27 else "scalar"

        def store_engine(R):
            return "scalar" if R < 16 else "gpsimd"

        buf_loaded = {0: [], 1: []}
        buf_loaded[0] = [pg.add("sync", lambda e: e.dma_start(
            out=bbp[0][:], in_=s_dtxn[0, :, :]), setup_g, inc=16, sem="slot0")]
        buf_cons = {0: list(setup_g), 1: list(setup_g)}
        ab_free = {0: None, 1: None}
        t4_free = []
        xn16_free = {0: [], 1: []}
        ht_free = {0: [], 1: []}
        ps_free = {0: [p1_done_v, p1_done_sc], 1: [p1_done_v, p1_done_sc]}
        ystore_prev = None
        gather_load_evs = {}
        cmb_evs = {}
        ag_ev_prev = None

        for R in range(P2_RUN):
            par = R % 2
            npar = 1 - par
            dtq = bbp[par][:, 0:TOK]
            xnq = bbp[par][:, TOK:2 * TOK]
            if R + 1 < P2_RUN:
                T = R + 1
                if T <= 3 or not DYN_ON:
                    lds = pg.add(load_engine(T) if DYN_ON else "sync",
                                 lambda e, npar=npar: e.dma_start(
                                     out=bbp[npar][:], in_=s_dtxn[0, :, :]),
                                 buf_cons[npar], inc=16, sem=f"slot{npar}")
                else:
                    lds = pg.add(load_engine(T),
                                 lambda e, T=T, npar=npar: dyn_load(
                                     e, T, bbp[npar][:], s_dtxn),
                                 buf_cons[npar], inc=16, sem=f"slot{npar}")
                buf_loaded[npar] = [lds]
                buf_cons[npar] = []
            if R < 2:
                ev_cmb = pg.add("vector", lambda e: e.memset(init4[:], 0.0), [])
            else:
                glev = gather_load_evs[R - 2]
                goff = ((R - 2) % 2) * 8 * NQ
                ev_cmb = pg.add("vector", lambda e: e.memset(init4[:], 0.0),
                                [glev])
                for k in range(8):
                    scol = R * 8 + k
                    ev_cmb = pg.add("vector", lambda e, k=k, scol=scol, goff=goff:
                                    e.scalar_tensor_tensor(
                        init4[:], gathbuf[:, goff + k * NQ:goff + (k + 1) * NQ],
                        seltab[:, scol:scol + 1],
                        init4[:], OP.mult, OP.add), [ev_cmb, glev])
            ev_cmb = pg.add("vector", lambda e: e.tensor_copy(giz[:], gm2[:]), [ev_cmb])
            cmb_evs[R] = ev_cmb

            st_evs = []
            last_y = ystore_prev
            last_ev_y = None
            HW2 = TOK // 2
            for j in range(NQ):
                gcol = R * NQ + j
                rho = NQ * (R % NQ) + j
                ab = abuf[j % 2]
                ab_deps = buf_loaded[par] + [ev_cmb]
                if ab_free[j % 2] is not None:
                    ab_deps.append(ab_free[j % 2])
                ev_ab = pg.add("scalar", lambda e, ab=ab, gcol=gcol, dtq=dtq: e.activation(
                    ab[:], dtq, AF.Exp, scale=a_sb[:, gcol:gcol + 1]), ab_deps)
                buf_cons[par].append(ev_ab)

                def mmh(eng, rho, base, h):
                    ii = None
                    for pc in range(4):
                        c0 = h * HW2 + pc * NPC
                        ii = eng.matmul(ps8[:, (h * 4 + pc) * NPC:(h * 4 + pc + 1) * NPC],
                                        sel16sb[base:base + NST, rho * 128:(rho + 1) * 128],
                                        (bmf if base == 0 else cmf)[:, c0:c0 + NPC],
                                        start=True, stop=True)
                    return ii

                def H(buf, h):
                    return buf[:, h * HW2:(h + 1) * HW2]

                psH = [ps8[:, 0:HW2], ps8[:, HW2:TOK]]
                ev_mmB = {}
                ev_bm16 = {}
                for h in (0, 1):
                    ev_mmB[h] = pg.add("tensor", lambda e, rho=rho, h=h: mmh(
                        e, rho, 0, h), ps_free[h] + [setup_done])
                    ev_bm16[h] = pg.add("scalar", lambda e, h=h: e.copy(
                        H(xn16, h), psH[h]), [ev_mmB[h]] + xn16_free[h])
                ev_t1 = pg.add("vector", lambda e, ab=ab, xnq=xnq: e.scalar_tensor_tensor(
                    bbxb[:], ab[:], -1.0, xnq, OP.add, OP.mult),
                    [ev_ab] + buf_loaded[par] + t4_free)
                buf_cons[par].append(ev_t1)
                ev_bx0 = pg.add("vector", lambda e, gcol=gcol: e.scalar_tensor_tensor(
                    H(bbxb, 0), H(bbxb, 0), ainv[:, gcol:gcol + 1], H(xn16, 0),
                    OP.mult, OP.mult), [ev_t1, ev_bm16[0]])
                xn16_free[0] = [ev_bx0]
                ev_h0 = pg.add("vector", lambda e, ab=ab, j=j: e.tensor_tensor_scan(
                    H(htb, 0), H(ab, 0), H(bbxb, 0), init4[:, j:j + 1],
                    OP.mult, OP.add), [ev_bx0, ev_cmb] + ht_free[0])
                ev_bx1 = pg.add("vector", lambda e, gcol=gcol: e.scalar_tensor_tensor(
                    H(bbxb, 1), H(bbxb, 1), ainv[:, gcol:gcol + 1], H(xn16, 1),
                    OP.mult, OP.mult), [ev_t1, ev_bm16[1]])
                xn16_free[1] = [ev_bx1]
                ev_mmC = {}
                ev_mmC[0] = pg.add("tensor", lambda e, rho=rho: mmh(e, rho, 32, 0),
                                   [ev_bm16[0]])
                cy_deps0 = ([last_y] if last_y is not None else [])
                if j == 0:
                    tgt0 = lambda: H(ybuf16, 0)
                    ev_cm0 = pg.add("scalar", lambda e: e.copy(
                        H(ybuf16, 0), psH[0]), [ev_mmC[0]] + cy_deps0)
                    ev_cy0 = pg.add("gpsimd", lambda e: e.tensor_mul(
                        H(ybuf16, 0), H(htb, 0), H(ybuf16, 0)), [ev_cm0, ev_h0])
                    ev_y0 = ev_cy0
                else:
                    ev_cm0 = pg.add("scalar", lambda e: e.copy(
                        H(bbxb, 0), psH[0]), [ev_mmC[0], ev_h0])
                    ev_cy0 = pg.add("gpsimd", lambda e: e.tensor_mul(
                        H(bbxb, 0), H(htb, 0), H(bbxb, 0)), [ev_cm0, ev_h0])
                    ev_y0 = pg.add("gpsimd", lambda e: e.tensor_add(
                        H(ybuf16, 0), H(ybuf16, 0), H(bbxb, 0)), [ev_cy0] + cy_deps0)
                ev_h1 = pg.add("vector", lambda e, ab=ab: e.tensor_tensor_scan(
                    H(htb, 1), H(ab, 1), H(bbxb, 1), htb[:, HW2 - 1:HW2],
                    OP.mult, OP.add), [ev_bx1, ev_h0] + ht_free[1])
                ab_free[j % 2] = ev_h1
                ev_st = pg.add("gpsimd", lambda e, gcol=gcol: e.tensor_copy(
                    stcols[:, gcol:gcol + 1], htb[:, TOK - 1:TOK]), [ev_h1])
                st_evs.append(ev_st)
                ev_mmC[1] = pg.add("tensor", lambda e, rho=rho: mmh(e, rho, 32, 1),
                                   [ev_bm16[1]])
                if j == 0:
                    ev_cm1 = pg.add("scalar", lambda e: e.copy(
                        H(ybuf16, 1), psH[1]), [ev_mmC[1]] + cy_deps0)
                    ev_cy1 = pg.add("gpsimd", lambda e: e.tensor_mul(
                        H(ybuf16, 1), H(htb, 1), H(ybuf16, 1)), [ev_cm1, ev_h1])
                    ev_y1 = ev_cy1
                    t4_free = [ev_h1]
                else:
                    ev_cm1 = pg.add("scalar", lambda e: e.copy(
                        H(bbxb, 1), psH[1]), [ev_mmC[1], ev_h1])
                    ev_cy1 = pg.add("gpsimd", lambda e: e.tensor_mul(
                        H(bbxb, 1), H(htb, 1), H(bbxb, 1)), [ev_cm1, ev_h1])
                    ev_y1 = pg.add("gpsimd", lambda e: e.tensor_add(
                        H(ybuf16, 1), H(ybuf16, 1), H(bbxb, 1)), [ev_cy1])
                    t4_free = [ev_y0, ev_y1]
                ht_free = {0: [ev_cy0], 1: [ev_cy1, ev_st]}
                ps_free = {0: [ev_cm0], 1: [ev_cm1]}
                last_ev_y = ev_y1
            if R <= 3 or not DYN_ON:
                ystore_prev = pg.add("sync", lambda e, R=R: e.dma_start(
                    out=s_y[R % 32, :, :], in_=ybuf16[:]), [last_ev_y], inc=16,
                    sem="yst")
            else:
                eng_nm = store_engine(R)
                sem_nm = "ysw0" if eng_nm == "gpsimd" else "yst"
                ystore_prev = pg.add(eng_nm, lambda e, R=R: dyn_store(
                    e, NSUB + 1 + R, ybuf16[:], s_y), [last_ev_y],
                    inc=16, sem=sem_nm)
            if AG_ON:
                pub_deps = list(st_evs)
                if ag_ev_prev is not None:
                    pub_deps.append(ag_ev_prev)
                ev_pub = pg.add("sync", lambda e, R=R: e.dma_start(
                    out=cc_in, in_=stcols[:, R * NQ:(R + 1) * NQ]), pub_deps, inc=16,
                    sem="pub")
                ag_deps = [ev_pub]
                if R - 1 in gather_load_evs:
                    ag_deps.append(gather_load_evs[R - 1])
                ev_ag = pg.add("gpsimd", lambda e: e.collective_compute(
                    "AllGather", OP.bypass, replica_groups=[list(range(8))],
                    ins=[cc_in], outs=[cc_out]), ag_deps, sem="ag")
                goff_w = (R % 2) * 8 * NQ
                gl_deps = [ev_ag]
                if R in cmb_evs:
                    pass
                if R - 2 >= 0 and R in cmb_evs:
                    pass
                gl_deps.append(cmb_evs[R])
                gather_load_evs[R] = pg.add("sync", lambda e, goff_w=goff_w: e.dma_start(
                    out=gathbuf[:, goff_w:goff_w + 8 * NQ].rearrange(
                        "p (k q) -> p k q", k=8),
                    in_=cc_out.rearrange("(k p) q -> p k q", p=128)), gl_deps, inc=16,
                    sem="gath")
                ag_ev_prev = ev_ag
            else:
                gather_load_evs[R] = st_evs[-1]

        p2_done = [("vector", pg.count["vector"]), ("gpsimd", pg.count["gpsimd"]),
                   ("sync", pg.count["sync"]), ("scalar", pg.count["scalar"]),
                   ("tensor", pg.count["tensor"]), ("yst", pg.count["yst"]),
                   ("ysw0", pg.count["ysw0"]), ("ag", pg.count["ag"]),
                   ("slot0", pg.count["slot0"]), ("slot1", pg.count["slot1"]),
                   ("pub", pg.count["pub"]), ("gath", pg.count["gath"])]

        prev_reuse = p2_done
        p3_free_x16 = p2_done
        p3_free_xn16 = p2_done
        p3_free_psc = p2_done
        p3_free_bx = p2_done
        p3_ln_tail = p2_done
        for c in range(NCH if PHASE3 else 0):
            c0 = c * Lc
            ld_evs = []
            for db in range(NDB):
                ld_deps = (list(p2_done) if c == 0 else
                           [p3_free_bx[db], p3_ln_tail[0]])
                ld_evs.append(pg.add(
                    "sync", lambda e, db=db, c0=c0: e.dma_start(
                        out=dbs(b_x, db), in_=s_xmid[db, :, c0:c0 + Lc]),
                    ld_deps, inc=16))
            ysum_evs = []
            for db in range(NDB):
                q_evs = []
                for q in range(NQ):
                    q_evs.append(pg.add(
                        "sync", lambda e, db=db, q=q, c0=c0: e.dma_start(
                            out=ybuf16[:, q * Lc:(q + 1) * Lc],
                            in_=s_y[db * NQ + q, :, c0:c0 + Lc]),
                        (prev_reuse if c == 0 else p3_yst_guard) + ysum_evs[-2:],
                        inc=16))
                ev_xn = pg.add(
                    "sync", lambda e, db=db, c0=c0: e.dma_start(
                        out=dbs(b_t3, db), in_=s_dtxn[db, :, TOK + c0:TOK + c0 + Lc]),
                    (prev_reuse if c == 0 else [ysum_evs[-1] if ysum_evs else p3_ln_tail[0]]),
                    inc=16)
                ev_a1 = pg.add("vector", lambda e: e.tensor_add(
                    ybuf16[:, 4 * Lc:5 * Lc], ybuf16[:, 0:Lc], ybuf16[:, Lc:2 * Lc]),
                    q_evs + [ev_xn] + ysum_evs[-2:])
                ev_a2 = pg.add("vector", lambda e: e.tensor_add(
                    ybuf16[:, 5 * Lc:6 * Lc], ybuf16[:, 2 * Lc:3 * Lc],
                    ybuf16[:, 3 * Lc:4 * Lc]), q_evs)
                ev_s = pg.add("vector", lambda e, db=db: e.tensor_add(
                    dbs(b_y, db), ybuf16[:, 4 * Lc:5 * Lc], ybuf16[:, 5 * Lc:6 * Lc]),
                    [ev_a1, ev_a2])
                ev_s = pg.add("vector", lambda e, db=db: e.scalar_tensor_tensor(
                    dbs(b_y, db), dbs(b_t3, db), d_sb[:, db:db + 1], dbs(b_y, db),
                    OP.mult, OP.add), [ev_s, ev_xn])
                ev_s = pg.add("vector", lambda e, db=db: e.tensor_add(
                    dbs(b_x, db), dbs(b_x, db), dbs(b_y, db)), [ev_s, ld_evs[db]])
                ysum_evs.append(ev_s)
            xn_evs3, ev_mm3 = layernorm(b_x, b_xn, xn16, ysum_evs, ysum_evs,
                                        x16_free=p3_free_x16, sq_free=p3_free_xn16)
            x_evs3 = ffn(xn_evs3, b_x, ysum_evs, [xn_evs3[-1]],
                         psc_free=p3_free_psc)
            ost_evs = []
            for db in range(NDB):
                ost_evs.append(pg.add(
                    "sync", lambda e, db=db, c0=c0: e.dma_start(
                        out=out_h[db, :, c0:c0 + Lc], in_=dbs(b_x, db)),
                    [x_evs3[db], xn_evs3[-1]], inc=16, sem="ost"))
            p3_yst_guard = [ysum_evs[-1]]
            p3_free_x16 = [ev_mm3]
            p3_free_xn16 = [x_evs3[-1]]
            p3_free_psc = [x_evs3[-1]]
            p3_free_bx = [ost_evs[-1]] * NDB
            p3_ln_tail = [xn_evs3[-1]]
            prev_reuse = []

        for semk, eng in [("sync", "sync"), ("gpsimd", "gpsimd"), ("yst", "sync"),
                          ("ysw0", "gpsimd"), ("ag", "gpsimd"),
                          ("slot0", "sync"), ("slot1", "sync"), ("pub", "sync"),
                          ("gath", "sync"), ("spill", "sync"), ("ost", "sync"),
                          ("spx", "sync"), ("spn", "sync"), ("spd", "sync")]:
            lastv = pg.count[semk]
            if lastv:
                pg.add(eng, lambda e, semk=semk, lastv=lastv: e.wait_ge(
                    pg.sems[semk], lastv) and None, [], inc=0)

        block = E(nc.Block())
        pg.emit(block)
    return nc


def _prep_inputs(x_full, inp):
    B, Lb, _ = x_full.shape
    T = B * Lb
    x_fm = np.ascontiguousarray(x_full.reshape(T, DM).T)
    A_dm_n = inp["L0_attn_A"].reshape(DM, NST).astype(np.float32)
    W_in = inp["L0_attn_W_in"].astype(np.float32)
    b_in = inp["L0_attn_b_in"].astype(np.float32)
    W_dt1 = inp["L0_attn_W_dt1"].astype(np.float32)
    W_dt2 = inp["L0_attn_W_dt2"].astype(np.float32)
    base_w = inp["L0_ffn_base_w"].astype(np.float32)
    gate_w = inp["L0_ffn_gate_w"].astype(np.float32)
    lora_A = inp["L0_ffn_lora_A"].astype(np.float32)
    lora_B = inp["L0_ffn_lora_B"].astype(np.float32)
    Dv = inp["L0_attn_D"].astype(np.float32)
    b_dt2 = inp["L0_attn_b_dt2"].astype(np.float32).reshape(-1)
    base_b = inp["L0_ffn_base_b"].astype(np.float32).reshape(-1)
    a_cat = lora_A.transpose(2, 0, 1).reshape(DM, ERK)
    b_cat = lora_B.transpose(0, 2, 1).reshape(ERK, DM)
    sel_one = np.concatenate(
        [np.eye(NST, dtype=np.float32)[:, n:n + 1] * np.ones((1, 128), np.float32)
         for n in range(NST)], axis=1)
    shared = {
        "ones_col": np.full((128, 1), 1.0 / DM, np.float32),
        "ones_row": np.ones((1, 128), np.float32),
        "ident": np.eye(128, dtype=np.float32),
        "expand16": np.kron(np.eye(NEX, dtype=np.float32), np.ones((1, 16), np.float32)) * 2.0,
        "sel16": np.concatenate([sel_one, np.zeros((NST, NST * 128), np.float32),
                                 sel_one], axis=0),
        "fconsts": np.tile(np.array([[1e-5, -0.5]], np.float32), (128, 1)),
        "b_dt1": inp["L0_attn_b_dt1"].reshape(1, -1).astype(np.float32),
    }
    maps = []
    for core in range(8):
        p = core % 4
        perm_n = np.empty(NST, np.int64)
        for a in range(4):
            for j in range(4):
                perm_n[4 * a + j] = 4 * ((a - 2 * p) % 4) + j
        m = dict(shared)
        m["x"] = np.ascontiguousarray(
            x_fm[:, core * TOK:(core + 1) * TOK]).reshape(NDB, 128, TOK)
        m["w_in"] = np.concatenate(
            [W_in[:, perm_n], W_in[:, NST + perm_n]], axis=1)
        m["b_in"] = np.concatenate([b_in[perm_n], b_in[NST + perm_n]]).reshape(1, -1)
        m["w_dt1"] = W_dt1
        m["w_dt2"] = W_dt2
        m["base_wT"] = base_w.T
        m["gate_wT"] = gate_w.T
        m["a_cat"] = a_cat
        m["b_cat"] = b_cat
        m["b_dt2"] = b_dt2.reshape(1, -1)
        m["base_b"] = base_b.reshape(1, -1)
        m["D_sb"] = Dv.reshape(NDB, 128).T
        a2 = np.empty((128, NSUB * NQ), np.float32)
        sel = np.zeros((128, NSUB * 8), np.float32)
        slot_tbl = np.zeros((1, NSUB + 1), np.int32)
        y_tbl = np.zeros((1, NSUB), np.int32)
        for R in range(NSUB):
            g = (R - 2 * p) % 32
            phys_db = g // NQ
            phys_n0 = NQ * (g % NQ)
            slot_tbl[0, R] = phys_db
            y_tbl[0, R] = g
            for j in range(NQ):
                a2[:, R * NQ + j] = A_dm_n[phys_db * 128:(phys_db + 1) * 128,
                                           phys_n0 + j]
            if p > 0:
                sel[:, R * 8 + (core - 1)] = 1.0
        m["a2_sb"] = np.ascontiguousarray(a2)
        m["seltab"] = np.ascontiguousarray(sel)
        m["slot_tbl"] = slot_tbl
        m["y_tbl"] = y_tbl
        maps.append({k: (np.ascontiguousarray(v) if v.dtype == np.int32
                         else np.ascontiguousarray(v, np.float32))
                     for k, v in m.items()})
    return maps, None


def _unperm(core):
    return np.arange(DM, dtype=np.int64)


def kernel(**inputs):
    x_full = np.asarray(inputs["L0_attn_x"], np.float32)
    B, Lb, _ = x_full.shape
    maps, _ = _prep_inputs(x_full, {k: np.asarray(v, np.float32) for k, v in inputs.items()})
    nc = build_program()
    res = run_bass_kernel_spmd(nc, maps, core_ids=list(range(8)))
    T = B * Lb
    out_fm = np.empty((DM, T), np.float32)
    for core in range(8):
        perm_d = _unperm(core)
        out_fm[perm_d, core * TOK:(core + 1) * TOK] = \
            res.results[core]["out"].reshape(DM, TOK)
    return np.ascontiguousarray(out_fm.T).reshape(B, Lb, DM)


# version 12
# speedup vs baseline: 1.1743x; 1.1743x over previous
import contextlib
import numpy as np
import concourse.bass as bass
import concourse.mybir as mybir
from concourse.bass_utils import run_bass_kernel_spmd

F32 = mybir.dt.float32
BF16 = mybir.dt.bfloat16
I32 = mybir.dt.int32
AF = mybir.ActivationFunctionType
OP = mybir.AluOpType
AX = mybir.AxisListType

DM = 1024
NDB = 8
NST = 16
RK = 64
NEX = 8
NGT = 10
ERK = 128
Lc = 512
TOK = 4096
NCH = TOK // Lc
NQ = 4
NSUB = NQ * NDB + 6
NPC = Lc

P1_CUT = 8
P1_RUN = NCH
P2_RUN = NSUB
PHASE3 = True
AG_ON = True
DYN_ON = True
NPieces = TOK // NPC


class Prog:
    ENGINES = ("sync", "tensor", "scalar", "vector", "gpsimd")

    def __init__(self, nc):
        self.nc = nc
        self.ops = {e: [] for e in self.ENGINES}
        self.count = {}
        self.ecount = {e: 0 for e in self.ENGINES}
        self.sems = {}
        self.extra_sems = []

    def make_sems(self, E):
        for e in self.ENGINES:
            self.sems[e] = E(self.nc.semaphore(f"sem_{e}"))
            self.count[e] = 0
        for s in self.extra_sems:
            self.sems[s] = E(self.nc.semaphore(f"sem_{s}"))
            self.count[s] = 0

    def add(self, engine, fn, deps=(), inc=1, sem=None):
        sem = sem or engine
        self.count[sem] += inc
        self.ecount[engine] += 1
        ev = (sem, self.count[sem])
        self.ops[engine].append((fn, tuple(deps), inc, sem, self.ecount[engine]))
        if not hasattr(self, "evmap"):
            self.evmap = {}
        self.evmap[ev] = (engine, self.ecount[engine])
        return ev

    def eev(self, engine):
        return (engine, self.count[engine])

    def emit(self, block):
        sems = self.sems

        def run(eng_name):
            def body(eng):
                waited = {s: 0 for s in self.count}
                drained = 0
                dma_engine = eng_name == "sync"
                for fn, deps, inc, sem, eidx in self.ops[eng_name]:
                    need = {}
                    se_need = 0
                    for (de, dc) in deps:
                        if de == eng_name and not dma_engine:
                            se_need = max(se_need, self.evmap.get((de, dc), (de, 0))[1])
                            continue
                        need[de] = max(need.get(de, 0), dc)
                    for de, dc in need.items():
                        if waited[de] < dc:
                            eng.wait_ge(sems[de], dc)
                            waited[de] = dc
                    if se_need > drained:
                        eng.drain()
                        drained = self.ecount_at(eng_name, eidx)
                    instr = fn(eng)
                    if inc:
                        assert instr is not None
                        instr.then_inc(sems[sem], inc)
            return body

        block.sync(run("sync"))
        block.tensor(run("tensor"))
        block.scalar(run("scalar"))
        block.vector(run("vector"))
        block.gpsimd(run("gpsimd"))

    def ecount_at(self, eng, eidx):
        return eidx - 1


def build_program():
    nc = bass.Bass("TRN2", target_bir_lowering=False)

    dr = lambda n, s, k="ExternalInput", dt=F32: nc.dram_tensor(n, s, dt, kind=k).ap()
    x_h = dr("x", [NDB, 128, TOK])
    out_h = dr("out", [NDB, 128, TOK], "ExternalOutput")
    win_h = dr("w_in", [DM, 2 * NST])
    wdt1_h = dr("w_dt1", [DM, RK])
    wdt2_h = dr("w_dt2", [RK, DM])
    basew_h = dr("base_wT", [DM, DM])
    gate_h = dr("gate_wT", [DM, NGT])
    acat_h = dr("a_cat", [DM, ERK])
    bcat_h = dr("b_cat", [ERK, DM])
    a2_h = dr("a2_sb", [128, NSUB * NQ])
    d_h = dr("D_sb", [128, NDB])
    bin_h = dr("b_in", [1, 2 * NST])
    bdt1_h = dr("b_dt1", [1, RK])
    bdt2_h = dr("b_dt2", [1, DM])
    baseb_h = dr("base_b", [1, DM])
    onescol_h = dr("ones_col", [128, 1])
    onesrow_h = dr("ones_row", [1, 128])
    ident_h = dr("ident", [128, 128])
    expand_h = dr("expand16", [NEX, 128])
    sel16_h = dr("sel16", [48, NST * 128])
    fc_h = dr("fconsts", [128, 2])
    seltab_h = dr("seltab", [128, NSUB * 8])
    slottab_h = dr("slot_tbl", [1, NSUB + 1], dt=I32)
    ytab_h = dr("y_tbl", [1, NSUB], dt=I32)
    s_dtxn = dr("s_dtxn", [NDB + 1, 128, 2 * TOK], "Internal")
    s_xmid = dr("s_xmid", [NDB, 128, TOK], "Internal")
    s_y = dr("s_y", [NQ * NDB, 128, TOK], "Internal", dt=BF16)
    cc_in = dr("cc_in", [128, NQ], "Internal")
    cc_out = dr("cc_out", [8 * 128, NQ], "Internal")

    with contextlib.ExitStack() as stack:
        E = stack.enter_context
        sb = lambda n, s, dt=F32: E(nc.sbuf_tensor(n, s, dt))

        bb0 = sb("bb0", [128, 2 * TOK])
        bb1 = sb("bb1", [128, 2 * TOK])
        b_x = bb0[:, 0:TOK]
        b_xn = bb0[:, TOK:2 * TOK]
        b_dt = bb1[:, 0:TOK]
        b_y = bb1[:, TOK:2 * TOK]
        b_t1 = sb("b_t1", [128, TOK])
        b_t2 = sb("b_t2", [128, TOK])
        b_t3 = sb("b_t3", [128, TOK])
        b_t4 = sb("b_t4", [128, TOK], BF16)
        ybuf16 = sb("ybuf16", [128, TOK], BF16)
        xn16 = sb("xn16", [128, TOK], BF16)
        x16b = sb("x16b", [128, TOK], BF16)
        xsq16 = xn16
        x16 = b_t4
        bcpack = sb("bcpack", [32 + NST, TOK], BF16)
        bmf = bcpack[0:NST, :]
        cmf = bcpack[32:32 + NST, :]
        r_mean = sb("r_mean", [1, Lc])
        r_msq = sb("r_msq", [1, Lc])
        r_rstd = sb("r_rstd", [1, Lc])
        r_mean16 = sb("r_mean16", [1, Lc], BF16)
        r_rstd16 = sb("r_rstd16", [1, Lc], BF16)
        gm1 = sb("gm1", [128, 4])
        gnm1 = sb("gnm1", [128, 4])
        gm2 = sb("gm2", [128, 4])
        ge = sb("ge", [128, 64])
        ge2 = None
        goh1 = sb("goh1", [128, 64])
        goh1n = None
        ge2 = b_t2[:, 0:64]
        goh1n = b_t2[:, 64:128]
        gtw = sb("gtw", [128, 64])
        gden = sb("gden", [128, 4])
        giz = sb("giz", [128, 4])
        gdf = sb("gdf", [128, 4])
        ew8 = sb("ew8", [128, 32])
        logit_sb = None
        ewfm16 = sb("ewfm16", [NEX, Lc], BF16)
        ewexp16 = sb("ewexp16", [128, Lc], BF16)
        zw16 = sb("zw16", [128, Lc], BF16)
        p116 = sb("p116", [RK, Lc], BF16)
        win16 = sb("win16", [128, NDB * 2 * NST], BF16)
        wdt1_16 = sb("wdt1_16", [128, NDB * RK], BF16)
        wdt2_16 = sb("wdt2_16", [RK, DM], BF16)
        basew16 = sb("basew16", [128, NDB * DM], BF16)
        gate16 = sb("gate16", [128, NDB * NGT], BF16)
        acat16 = sb("acat16", [128, NDB * ERK], BF16)
        bcat16 = sb("bcat16", [128, DM], BF16)
        a_sb = sb("a_sb", [128, NSUB * NQ])
        ainv = sb("ainv", [128, NSUB * NQ])
        d_sb = sb("d_sb", [128, NDB])
        seltab = sb("seltab_s", [128, NSUB * 8], BF16)
        tabs = sb("tabs", [1, 2 * NSUB + 1], I32)
        stcols = sb("stcols", [128, NSUB * NQ])
        gathbuf = sb("gathbuf", [128, 2 * 8 * NQ])
        init4 = sb("init4", [128, NQ])
        nbias = 2 * NST + RK + DM + DM
        braws = b_t1[0:1, 0:nbias]
        blo_f = b_t3[0:1, 0:nbias]
        bhi = sb("bhi16", [1, nbias], BF16)
        blo = sb("blo16", [1, nbias], BF16)
        onescol16 = sb("onescol16", [128, 1], BF16)
        onesrow16 = sb("onesrow16", [1, 128], BF16)
        onesL16_t = sb("onesL16s", [1, Lc], BF16)
        onesL16 = onesL16_t[0:1, :]
        onesL32 = onesL16
        ident_sb = sb("ident_sb", [128, 128])
        expand16 = sb("expand16sb", [NEX, 128], BF16)
        sel16sb = sb("sel16sb", [48, NST * 128], BF16)
        fc_sb = sb("fc_sb", [128, 2])

        ps8 = E(nc.psum_tensor("ps8", [128, TOK], F32))
        ps_st = ps8[0:33, 0 * Lc:1 * Lc]
        ps_a = ps8[:, 1 * Lc:2 * Lc]
        ps_b = ps8[:, 2 * Lc:3 * Lc]
        ps_c = ps8[:, 3 * Lc:4 * Lc]
        ps_d = ps8[:, 4 * Lc:5 * Lc]

        wstage = b_t2[:, 0:2048]
        pg = Prog(nc)
        pg.extra_sems = ["slot0", "slot1", "yst", "pub", "gath", "spill", "ost",
                         "ysw0", "ag", "spx", "spn", "spd"]
        pg.make_sems(E)

        OB_IN, OB_DT1, OB_DT2, OB_BB = 0, 2 * NST, 2 * NST + RK, 2 * NST + RK + DM

        def ld_dma(dst, src, deps=()):
            return pg.add("sync", lambda e, d=dst, s=src: e.dma_start(out=d, in_=s),
                          deps, inc=16)

        def vcast(dst, src, deps):
            return pg.add("vector", lambda e, d=dst, s=src: e.tensor_copy(d, s), deps)

        ld_dma(a_sb[:], a2_h)
        ld_dma(tabs[0:1, 0:NSUB + 1], slottab_h)
        ld_dma(tabs[0:1, NSUB + 1:2 * NSUB + 1], ytab_h)
        ld_dma(d_sb[:], d_h)
        ld_dma(ident_sb[:], ident_h)
        ld_dma(fc_sb[:], fc_h)
        ld_dma(braws[0:1, OB_IN:OB_IN + 2 * NST], bin_h)
        ld_dma(braws[0:1, OB_DT1:OB_DT1 + RK], bdt1_h)
        ld_dma(braws[0:1, OB_DT2:OB_DT2 + DM], bdt2_h)
        evA = ld_dma(braws[0:1, OB_BB:OB_BB + DM], baseb_h)
        pg.add("vector", lambda e: e.reciprocal(ainv[:], a_sb[:]), [evA])
        pg.add("vector", lambda e: e.memset(onesL16, 1.0), [])
        pg.add("vector", lambda e: e.memset(ge[:], 0.0), [])
        pg.add("vector", lambda e: e.memset(goh1[:], 0.0), [])
        pg.add("vector", lambda e: e.memset(stcols[:], 0.0), [])
        pg.add("vector", lambda e: e.memset(gathbuf[:], 0.0), [])
        ev = vcast(bhi[0:1, :], braws, [evA])
        ev = pg.add("vector", lambda e: e.tensor_sub(blo_f, braws, bhi[0:1, :]), [ev])
        ev_blo = vcast(blo[0:1, :], blo_f, [ev])
        pc = [ev_blo]

        def staged(dst, src, p_rows, p_cols):
            ev_l = ld_dma(wstage[0:p_rows, 0:p_cols], src, [pc[0]])
            pc[0] = vcast(dst, wstage[0:p_rows, 0:p_cols], [ev_l])

        staged(seltab[:], seltab_h, 128, NSUB * 8)
        staged(expand16[:], expand_h, NEX, 128)
        staged(sel16sb[0:48, :], sel16_h, 48, NST * 128)
        staged(onescol16[:], onescol_h, 128, 1)
        staged(onesrow16[:], onesrow_h, 1, 128)
        for k in range(NDB):
            staged(win16[:, k * 2 * NST:(k + 1) * 2 * NST],
                   win_h[k * 128:(k + 1) * 128, :], 128, 2 * NST)
        for k in range(NDB):
            staged(wdt1_16[:, k * RK:(k + 1) * RK],
                   wdt1_h[k * 128:(k + 1) * 128, :], 128, RK)
        for k in range(NDB):
            staged(basew16[:, k * DM:(k + 1) * DM],
                   basew_h[k * 128:(k + 1) * 128, :], 128, DM)
        for k in range(NDB):
            staged(gate16[:, k * NGT:(k + 1) * NGT],
                   gate_h[k * 128:(k + 1) * 128, :], 128, NGT)
        for k in range(NDB):
            staged(acat16[:, k * ERK:(k + 1) * ERK],
                   acat_h[k * 128:(k + 1) * 128, :], 128, ERK)
        staged(wdt2_16[:], wdt2_h, RK, DM)
        staged(bcat16[:], bcat_h, 128, DM)
        setup_done = ("vector", pg.count["vector"])

        def dbs(buf, db):
            return buf[:, db * Lc:(db + 1) * Lc]

        def layernorm(src, dst, dst16, in_evs, ps_guard, x16_free=(), sq_free=(),
                      sqbuf=None, cast_free=(), xbuf=None):
            sq = sqbuf if sqbuf is not None else xsq16
            xb = xbuf if xbuf is not None else x16
            ev_c = pg.add("vector", lambda e, xb=xb: e.tensor_copy(xb[:], src[:]),
                          list(in_evs) + list(x16_free))
            ev_sq = pg.add("scalar", lambda e, sq=sq, xb=xb: e.activation(sq[:], xb[:], AF.Square),
                           [ev_c] + list(sq_free))

            def stats(eng):
                for k in range(NDB):
                    eng.matmul(ps_st[0:1, :], onescol16[:], dbs(xb, k),
                               start=(k == 0), stop=(k == NDB - 1))
                ii = None
                for k in range(NDB):
                    ii = eng.matmul(ps_st[32:33, :], onescol16[:], dbs(sq, k),
                                    start=(k == 0), stop=(k == NDB - 1))
                return ii
            ev_mm = pg.add("tensor", stats, [ev_c, ev_sq, setup_done])
            ev_r0a = pg.add("scalar", lambda e: e.copy(r_mean[:], ps_st[0:1, :]), [ev_mm])
            ev_r0 = pg.add("scalar", lambda e: e.copy(r_msq[:], ps_st[32:33, :]), [ev_r0a])
            ev_v1 = pg.add("vector", lambda e: e.tensor_mul(r_rstd[:], r_mean[:], r_mean[:]), [ev_r0])
            ev_v2 = pg.add("vector", lambda e: e.tensor_sub(r_msq[:], r_msq[:], r_rstd[:]), [ev_v1])
            ev_s1 = pg.add("scalar", lambda e: e.activation(r_msq[:], r_msq[:], AF.Ln, bias=fc_sb[0:1, 0:1]), [ev_v2])
            ev_s2 = pg.add("scalar", lambda e: e.activation(r_rstd[:], r_msq[:], AF.Exp, scale=fc_sb[0:1, 1:2]), [ev_s1])
            ev_m16 = pg.add("vector", lambda e: e.tensor_copy(r_mean16[:], r_mean[:]), [ev_r0])
            ev_m16b = pg.add("vector", lambda e: e.tensor_copy(r_rstd16[:], r_rstd[:]), [ev_s2])
            ev_rep = pg.add("tensor", lambda e: (
                e.matmul(ps_a[:, :], onesrow16[:], r_mean16[:], start=True, stop=True),
                e.matmul(ps_b[:, :], onesrow16[:], r_rstd16[:], start=True, stop=True))[-1],
                [ev_m16, ev_m16b] + list(ps_guard))
            evs = []
            for db in range(NDB):
                e1 = pg.add("vector", lambda e, db=db: e.tensor_sub(dbs(dst, db), dbs(src, db), ps_a[:, :]), [ev_rep])
                e2 = pg.add("vector", lambda e, db=db: e.tensor_mul(dbs(dst, db), dbs(dst, db), ps_b[:, :]), [e1])
                e3 = pg.add("vector", lambda e, db=db: e.tensor_copy(dbs(dst16, db), dbs(dst, db)),
                            [e2] + list(cast_free))
                evs.append(e3)
            return evs, ev_mm

        def k1_into(eng, ps, rows_n, col_off, is_last):
            eng.matmul(ps[0:rows_n, :], bhi[0:1, col_off:col_off + rows_n],
                       onesL16, start=False, stop=False)
            return eng.matmul(ps[0:rows_n, :], blo[0:1, col_off:col_off + rows_n],
                              onesL32, start=False, stop=is_last)

        def ffn(xn_evs, x_buf, x_in_evs, ps_guard, psc_free=()):
            def gate_mm(eng):
                ii = None
                for k in range(NDB):
                    ii = eng.matmul(ps_c[0:NGT, :], gate16[:, k * NGT:(k + 1) * NGT],
                                    dbs(xn16, k), start=(k == 0), stop=(k == NDB - 1))
                return ii
            ev_g = pg.add("tensor", gate_mm, list(xn_evs) + list(psc_free))
            ev_ge = pg.add("scalar", lambda e: e.copy(b_y[0:NGT, 0:Lc], ps_c[0:NGT, :]), [ev_g])
            nblk = Lc // 128

            def transp(eng):
                ii = None
                for j in range(nblk):
                    ii = eng.transpose(ps_d[:, j * 16:j * 16 + NGT],
                                       b_y[0:NGT, j * 128:(j + 1) * 128],
                                       ident_sb[0:NGT, 0:NGT])
                return ii
            ev_t = pg.add("tensor", transp, [ev_ge, ev_g])

            def blkcols(buf, j, w=NGT):
                return buf[:, j * 16:j * 16 + w]
            for j in range(nblk):
                ev = pg.add("vector", lambda e, j=j: e.reduce_max(gm1[:, j:j + 1], blkcols(ps_d, j), axis=AX.X), [ev_t])
            for j in range(nblk):
                ev = pg.add("vector", lambda e, j=j: e.reduce_max(gnm1[:, j:j + 1], blkcols(ps_d, j), axis=AX.X, negate=True), [ev_t])
            ev_mx = ev
            evs_e = []
            for j in range(nblk):
                evs_e.append(pg.add("scalar", lambda e, j=j: e.activation(
                    blkcols(ge, j), blkcols(ps_d, j), AF.Exp, bias=gnm1[:, j:j + 1]), [ev_mx]))
            for j in range(nblk):
                ev = pg.add("vector", lambda e, j=j: e.tensor_scalar(
                    blkcols(goh1, j), blkcols(ps_d, j), gm1[:, j:j + 1], None, OP.is_equal),
                    [ev_mx] + evs_e)
            ev = pg.add("vector", lambda e: e.tensor_scalar(
                goh1n[:], goh1[:], -1.0, 1.0, OP.mult, OP.add), [ev])
            ev = pg.add("vector", lambda e: e.tensor_mul(ge2[:], ge[:], goh1n[:]), [ev] + evs_e)
            ev_e2 = ev
            for j in range(nblk):
                ev = pg.add("vector", lambda e, j=j: e.reduce_max(gm2[:, j:j + 1], blkcols(ge2, j), axis=AX.X), [ev_e2])
            ev_sp = pg.add("vector", lambda e: e.tensor_copy(giz[:], gm2[:]), [ev])
            for j in range(nblk):
                ev = pg.add("vector", lambda e, j=j: e.tensor_scalar(
                    blkcols(ge2, j), blkcols(ge2, j), gm2[:, j:j + 1], None, OP.is_equal), [ev_sp])
            for j in range(nblk):
                ev = pg.add("vector", lambda e, j=j: e.scalar_tensor_tensor(
                    blkcols(gtw, j), blkcols(ge2, j), gm2[:, j:j + 1], blkcols(goh1, j),
                    OP.mult, OP.add), [ev])
            for j in range(nblk):
                ev = pg.add("vector", lambda e, j=j: e.reduce_sum(
                    gden[:, j:j + 1], blkcols(gtw, j, NEX), axis=AX.X), [ev])
            ev = pg.add("vector", lambda e: e.tensor_single_scalar(giz[:], gden[:], 0.0, OP.is_equal), [ev])
            ev = pg.add("vector", lambda e: e.tensor_add(gdf[:], gden[:], giz[:]), [ev])
            ev = pg.add("vector", lambda e: e.reciprocal(gm2[:], gdf[:]), [ev])
            ev_sp2 = pg.add("vector", lambda e: e.tensor_copy(gdf[:], gm2[:]), [ev])
            evw = []
            for j in range(nblk):
                evw.append(pg.add("vector", lambda e, j=j: e.tensor_scalar(
                    ew8[:, j * 8:(j + 1) * 8], blkcols(gtw, j, NEX), gm2[:, j:j + 1], None,
                    OP.mult), [ev_sp2]))

            def transb(eng):
                ii = None
                for j in range(nblk):
                    ii = eng.transpose(ps_d[0:NEX, j * 128:(j + 1) * 128],
                                       ew8[:, j * 8:(j + 1) * 8], ident_sb[:, :])
                return ii
            ev_tb = pg.add("tensor", transb, evw)
            ev_efm = pg.add("scalar", lambda e: e.copy(ewfm16[:], ps_d[0:NEX, :]), [ev_tb])
            ev_ex = pg.add("tensor", lambda e: e.matmul(
                ps_d[:, :], expand16[:], ewfm16[:], start=True, stop=True), [ev_efm])
            ev_ex16 = pg.add("scalar", lambda e: e.copy(ewexp16[:], ps_d[:, :]), [ev_ex])

            def zmm(eng):
                ii = None
                for k in range(NDB):
                    ii = eng.matmul(ps_c[:, :], acat16[:, k * ERK:(k + 1) * ERK],
                                    dbs(xn16, k), start=(k == 0), stop=(k == NDB - 1))
                return ii
            ev_z = pg.add("tensor", zmm, list(xn_evs) + [ev_ge])
            ev_zw = pg.add("vector", lambda e: e.tensor_mul(zw16[:], ps_c[:, :], ewexp16[:]),
                           [ev_z, ev_ex16])
            x_evs = []
            prev_ep = {0: None, 1: None}
            for m in range(NDB):
                ps = ps_a if (m % 2 == 0) else ps_b

                def bmm(eng, m=m, ps=ps):
                    for k in range(NDB):
                        eng.matmul(ps[:, :], basew16[:, k * DM + m * 128:k * DM + (m + 1) * 128],
                                   dbs(xn16, k), start=(k == 0), stop=False)
                    eng.matmul(ps[:, :], bcat16[:, m * 128:(m + 1) * 128], zw16[:],
                               start=False, stop=False)
                    return k1_into(eng, ps, 128, OB_BB + m * 128, True)
                deps = list(xn_evs) + [ev_zw]
                if prev_ep[m % 2] is not None:
                    deps.append(prev_ep[m % 2])
                if m < 2:
                    deps += list(ps_guard)
                ev_b = pg.add("tensor", bmm, deps)
                ev_ep = pg.add("vector", lambda e, m=m, ps=ps: e.scalar_tensor_tensor(
                    dbs(x_buf, m), ps[:, :], 0.5, dbs(x_buf, m),
                    OP.mult, OP.add), [ev_b] + list(x_in_evs))
                prev_ep[m % 2] = ev_ep
                x_evs.append(ev_ep)
            return x_evs

        prev_reuse = [setup_done]
        free_bx = [setup_done] * NDB
        free_x16 = [setup_done]
        free_xn16 = [setup_done]
        free_sq1 = [setup_done]
        free_x16b = [setup_done]
        free_psc = [setup_done]
        free_bdt = [setup_done]
        free_bxn = [setup_done]
        guard1 = [setup_done]
        for c in range(P1_RUN):
            c0 = c * Lc
            ld_evs = []
            for db in range(NDB):
                ld_evs.append(pg.add(
                    "sync", lambda e, db=db, c0=c0: e.dma_start(
                        out=dbs(b_x, db), in_=x_h[db, :, c0:c0 + Lc]),
                    [free_bx[db]], inc=16))
            x_ev = ld_evs[-1]
            if P1_CUT < 2:
                prev_reuse = ld_evs
                continue
            xn_evs, ev_mm1 = layernorm(b_x, b_xn, xn16, ld_evs, guard1,
                                       x16_free=free_x16b, sq_free=free_sq1,
                                       sqbuf=ybuf16, cast_free=free_xn16, xbuf=x16b)
            if P1_CUT < 3:
                prev_reuse = xn_evs
                continue
            x_evs = ffn(xn_evs, b_x, ld_evs, [xn_evs[-1]], psc_free=free_psc)
            if P1_CUT < 4:
                prev_reuse = x_evs
                continue
            xn_evs2, ev_mm2 = layernorm(b_x, b_xn, xn16, [x_evs[-1]] + x_evs, x_evs)
            if P1_CUT < 5:
                prev_reuse = xn_evs2
                continue
            sp_evs = []
            sp_xmid = []
            for db in range(NDB):
                sp_xmid.append(pg.add(
                    "sync", lambda e, db=db, c0=c0: e.dma_start(
                        out=s_xmid[db, :, c0:c0 + Lc], in_=dbs(b_x, db)),
                    [x_evs[db], xn_evs2[-1]], inc=16, sem="spx"))
            sp_xn = []
            for db in range(NDB):
                sp_xn.append(pg.add(
                    "sync", lambda e, db=db, c0=c0: e.dma_start(
                        out=s_dtxn[db, :, TOK + c0:TOK + c0 + Lc], in_=dbs(b_xn, db)),
                    [xn_evs2[db]], inc=16, sem="spn"))
            sp_evs = sp_xmid + sp_xn

            def bcmm(eng):
                for k in range(NDB):
                    eng.matmul(ps_c[0:NST, :], win16[:, k * 2 * NST:k * 2 * NST + NST],
                               dbs(xn16, k), start=(k == 0), stop=False)
                eng.matmul(ps_c[0:NST, :], bhi[0:1, OB_IN:OB_IN + NST],
                           onesL16, start=False, stop=False)
                eng.matmul(ps_c[0:NST, :], blo[0:1, OB_IN:OB_IN + NST],
                           onesL32, start=False, stop=True)
                for k in range(NDB):
                    eng.matmul(ps_c[32:32 + NST, :],
                               win16[:, k * 2 * NST + NST:(k + 1) * 2 * NST],
                               dbs(xn16, k), start=(k == 0), stop=False)
                eng.matmul(ps_c[32:32 + NST, :], bhi[0:1, OB_IN + NST:OB_IN + 2 * NST],
                           onesL16, start=False, stop=False)
                return eng.matmul(ps_c[32:32 + NST, :], blo[0:1, OB_IN + NST:OB_IN + 2 * NST],
                                  onesL32, start=False, stop=True)
            if P1_CUT < 6:
                prev_reuse = sp_evs + [xn_evs2[-1]]
                continue
            ev_bc = pg.add("tensor", bcmm, list(xn_evs2))
            ev_bm = pg.add("scalar", lambda e, c0=c0: e.copy(bmf[:, c0:c0 + Lc], ps_c[0:NST, :]), [ev_bc])
            ev_cm = pg.add("scalar", lambda e, c0=c0: e.copy(cmf[:, c0:c0 + Lc], ps_c[32:32 + NST, :]), [ev_bc])

            def p1mm(eng):
                for k in range(NDB):
                    eng.matmul(ps_c[0:RK, :], wdt1_16[:, k * RK:(k + 1) * RK],
                               dbs(xn16, k), start=(k == 0), stop=False)
                return k1_into(eng, ps_c, RK, OB_DT1, True)
            if P1_CUT < 7:
                prev_reuse = sp_evs + [xn_evs2[-1], ev_bm, ev_cm]
                continue
            ev_p1 = pg.add("tensor", p1mm, list(xn_evs2) + [ev_bm, ev_cm])
            ev_p116 = pg.add("scalar", lambda e: e.copy(p116[:], ps_c[0:RK, :]), [ev_p1])
            exp_evs = []
            prev = ev_p116
            for db in range(NDB):
                def dtmm(eng, db=db):
                    eng.matmul(ps_c[:, :], wdt2_16[:, db * 128:(db + 1) * 128], p116[:],
                               start=True, stop=False)
                    return k1_into(eng, ps_c, 128, OB_DT2 + db * 128, True)
                ev_m = pg.add("tensor", dtmm, [ev_p116, prev])
                ev_e = pg.add("scalar", lambda e, db=db: e.activation(
                    dbs(b_y, db), ps_c[:, :], AF.Exp), [ev_m])
                prev = ev_e
                exp_evs.append(ev_e)
            if P1_CUT < 8:
                prev_reuse = sp_evs + [xn_evs2[-1], exp_evs[-1]]
                continue
            dt_sp = []
            cl_evs = []
            for db in range(NDB):
                ev_l = pg.add("scalar", lambda e, db=db: e.activation(
                    dbs(b_dt, db), dbs(b_y, db), AF.Ln, bias=1.0),
                    [exp_evs[db]] + free_bdt)
                ev_cl = pg.add("vector", lambda e, db=db: e.tensor_scalar(
                    dbs(b_dt, db), dbs(b_dt, db), 1e-7, 1e6, OP.max, OP.min), [ev_l])
                cl_evs.append(ev_cl)
                dt_sp.append(pg.add(
                    "sync", lambda e, db=db, c0=c0: e.dma_start(
                        out=s_dtxn[db, :, c0:c0 + Lc], in_=dbs(b_dt, db)),
                    [ev_cl], inc=16, sem="spd"))
            free_bx = [sp_xmid[-1]] * NDB
            free_x16 = [ev_mm2]
            free_xn16 = [ev_p1]
            free_sq1 = [ev_mm1]
            free_x16b = [ev_mm1]
            free_psc = [exp_evs[-1]]
            free_bdt = [dt_sp[-1]]
            free_bxn = sp_xn
            guard1 = [sp_xn[-1], xn_evs2[-1]]
            prev_reuse = sp_evs + dt_sp + [xn_evs2[-1], cl_evs[-1], exp_evs[-1]]
        p1_done_s = ("sync", pg.count["sync"])
        p1_done_sp = ("spd", pg.count["spd"])
        p1_done_spx = ("spx", pg.count["spx"])
        p1_done_spn = ("spn", pg.count["spn"])
        p1_done_v = ("vector", pg.count["vector"])
        p1_done_sc = ("scalar", pg.count["scalar"])
        p1_done_t = ("tensor", pg.count["tensor"])

        abuf = [b_t1, b_t2]
        bbxb = b_t4
        htb = b_t3

        setup_g = [p1_done_s, p1_done_v, p1_done_sc, p1_done_t, p1_done_sp,
                   p1_done_spx, p1_done_spn]
        bbp = [bb0, bb1]

        _dyn_ctr = [0]

        def dyn_load(eng, tab_idx, dst, scratch):
            _dyn_ctr[0] += 1
            i = _dyn_ctr[0]
            with eng.register(f"ds{i}") as slot, eng.register(f"do{i}") as off:
                eng.reg_load(slot, tabs[0:1, tab_idx:tab_idx + 1])
                eng.reg_mul(off, slot, 128 * 2 * TOK)
                srcap = bass.AP(tensor=scratch.tensor, offset=off,
                                ap=[[2 * TOK, 128], [1, 2 * TOK]])
                return eng.dma_start(out=dst, in_=srcap)

        def dyn_store(eng, tab_idx, src_buf, scratch):
            _dyn_ctr[0] += 1
            i = _dyn_ctr[0]
            with eng.register(f"ds{i}") as slot, eng.register(f"do{i}") as off:
                eng.reg_load(slot, tabs[0:1, tab_idx:tab_idx + 1])
                eng.reg_mul(off, slot, 128 * TOK)
                dstap = bass.AP(tensor=scratch.tensor, offset=off,
                                ap=[[TOK, 128], [1, TOK]])
                return eng.dma_start(out=dstap, in_=src_buf)

        def load_engine(T):
            return "sync" if T < 27 else "scalar"

        def store_engine(R):
            return "scalar" if R < 16 else "gpsimd"

        buf_loaded = {0: [], 1: []}
        buf_loaded[0] = [pg.add("sync", lambda e: e.dma_start(
            out=bbp[0][:], in_=s_dtxn[0, :, :]), setup_g, inc=16, sem="slot0")]
        buf_cons = {0: list(setup_g), 1: list(setup_g)}
        ab_free = {0: None, 1: None}
        t4_free = []
        xn16_free = {0: [], 1: []}
        ht_free = {0: [], 1: []}
        ps_free = {0: [p1_done_v, p1_done_sc], 1: [p1_done_v, p1_done_sc]}
        ystore_prev = None
        gather_load_evs = {}
        cmb_evs = {}
        ag_ev_prev = None

        for R in range(P2_RUN):
            par = R % 2
            npar = 1 - par
            dtq = bbp[par][:, 0:TOK]
            xnq = bbp[par][:, TOK:2 * TOK]
            if R + 1 < P2_RUN:
                T = R + 1
                if T <= 3 or not DYN_ON:
                    lds = pg.add(load_engine(T) if DYN_ON else "sync",
                                 lambda e, npar=npar: e.dma_start(
                                     out=bbp[npar][:], in_=s_dtxn[0, :, :]),
                                 buf_cons[npar], inc=16, sem=f"slot{npar}")
                else:
                    lds = pg.add(load_engine(T),
                                 lambda e, T=T, npar=npar: dyn_load(
                                     e, T, bbp[npar][:], s_dtxn),
                                 buf_cons[npar], inc=16, sem=f"slot{npar}")
                buf_loaded[npar] = [lds]
                buf_cons[npar] = []
            if R < 2:
                ev_cmb = pg.add("vector", lambda e: e.memset(init4[:], 0.0), [])
            else:
                glev = gather_load_evs[R - 2]
                goff = ((R - 2) % 2) * 8 * NQ
                ev_cmb = pg.add("vector", lambda e: e.memset(init4[:], 0.0),
                                [glev])
                for k in range(8):
                    scol = R * 8 + k
                    ev_cmb = pg.add("vector", lambda e, k=k, scol=scol, goff=goff:
                                    e.scalar_tensor_tensor(
                        init4[:], gathbuf[:, goff + k * NQ:goff + (k + 1) * NQ],
                        seltab[:, scol:scol + 1],
                        init4[:], OP.mult, OP.add), [ev_cmb, glev])
            ev_cmb = pg.add("vector", lambda e: e.tensor_copy(giz[:], gm2[:]), [ev_cmb])
            cmb_evs[R] = ev_cmb

            st_evs = []
            last_y = ystore_prev
            last_ev_y = None
            HW2 = TOK // 2
            for j in range(NQ):
                gcol = R * NQ + j
                rho = NQ * (R % NQ) + j
                ab = abuf[j % 2]
                ab_deps = buf_loaded[par] + [ev_cmb]
                if ab_free[j % 2] is not None:
                    ab_deps.append(ab_free[j % 2])
                ev_ab = pg.add("scalar", lambda e, ab=ab, gcol=gcol, dtq=dtq: e.activation(
                    ab[:], dtq, AF.Exp, scale=a_sb[:, gcol:gcol + 1]), ab_deps)
                buf_cons[par].append(ev_ab)

                def mmh(eng, rho, base, h):
                    ii = None
                    for pc in range(4):
                        c0 = h * HW2 + pc * NPC
                        ii = eng.matmul(ps8[:, (h * 4 + pc) * NPC:(h * 4 + pc + 1) * NPC],
                                        sel16sb[base:base + NST, rho * 128:(rho + 1) * 128],
                                        (bmf if base == 0 else cmf)[:, c0:c0 + NPC],
                                        start=True, stop=True)
                    return ii

                def H(buf, h):
                    return buf[:, h * HW2:(h + 1) * HW2]

                psH = [ps8[:, 0:HW2], ps8[:, HW2:TOK]]
                ev_mmB = {}
                ev_bm16 = {}
                for h in (0, 1):
                    ev_mmB[h] = pg.add("tensor", lambda e, rho=rho, h=h: mmh(
                        e, rho, 0, h), ps_free[h] + [setup_done])
                    ev_bm16[h] = pg.add("scalar", lambda e, h=h: e.copy(
                        H(xn16, h), psH[h]), [ev_mmB[h]] + xn16_free[h])
                ev_t1 = pg.add("vector", lambda e, ab=ab, xnq=xnq: e.scalar_tensor_tensor(
                    bbxb[:], ab[:], -1.0, xnq, OP.add, OP.mult),
                    [ev_ab] + buf_loaded[par] + t4_free)
                buf_cons[par].append(ev_t1)
                ev_bx0 = pg.add("vector", lambda e: e.tensor_mul(
                    H(bbxb, 0), H(bbxb, 0), H(xn16, 0)), [ev_t1, ev_bm16[0]])
                xn16_free[0] = [ev_bx0]
                ev_h0 = pg.add("vector", lambda e, ab=ab, j=j: e.tensor_tensor_scan(
                    H(htb, 0), H(ab, 0), H(bbxb, 0), init4[:, j:j + 1],
                    OP.mult, OP.add), [ev_bx0, ev_cmb] + ht_free[0])
                ev_bx1 = pg.add("vector", lambda e: e.tensor_mul(
                    H(bbxb, 1), H(bbxb, 1), H(xn16, 1)), [ev_t1, ev_bm16[1]])
                xn16_free[1] = [ev_bx1]
                ev_mmC = {}
                ev_mmC[0] = pg.add("tensor", lambda e, rho=rho: mmh(e, rho, 32, 0),
                                   [ev_bm16[0]])
                if j == 0:
                    cy_deps0 = ([last_y] if last_y is not None else [])
                    ev_cy0 = pg.add("vector", lambda e, gcol=gcol: e.scalar_tensor_tensor(
                        H(ybuf16, 0), H(htb, 0), ainv[:, gcol:gcol + 1], psH[0],
                        OP.mult, OP.mult), [ev_h0, ev_mmC[0]] + cy_deps0)
                    ev_y0 = ev_cy0
                else:
                    ev_cy0 = pg.add("vector", lambda e, gcol=gcol: e.scalar_tensor_tensor(
                        H(bbxb, 0), H(htb, 0), ainv[:, gcol:gcol + 1], psH[0],
                        OP.mult, OP.mult), [ev_h0, ev_mmC[0]])
                    ev_y0 = pg.add("gpsimd", lambda e: e.tensor_add(
                        H(ybuf16, 0), H(ybuf16, 0), H(bbxb, 0)), [ev_cy0])
                ev_h1 = pg.add("vector", lambda e, ab=ab: e.tensor_tensor_scan(
                    H(htb, 1), H(ab, 1), H(bbxb, 1), htb[:, HW2 - 1:HW2],
                    OP.mult, OP.add), [ev_bx1, ev_h0] + ht_free[1])
                ab_free[j % 2] = ev_h1
                ev_st = pg.add("gpsimd", lambda e, gcol=gcol: e.tensor_copy(
                    stcols[:, gcol:gcol + 1], htb[:, TOK - 1:TOK]), [ev_h1])
                st_evs.append(ev_st)
                ev_mmC[1] = pg.add("tensor", lambda e, rho=rho: mmh(e, rho, 32, 1),
                                   [ev_bm16[1]])
                if j == 0:
                    ev_cy1 = pg.add("vector", lambda e, gcol=gcol: e.scalar_tensor_tensor(
                        H(ybuf16, 1), H(htb, 1), ainv[:, gcol:gcol + 1], psH[1],
                        OP.mult, OP.mult), [ev_h1, ev_mmC[1]] + cy_deps0)
                    ev_y1 = ev_cy1
                    t4_free = [ev_h1]
                else:
                    ev_cy1 = pg.add("vector", lambda e, gcol=gcol: e.scalar_tensor_tensor(
                        H(bbxb, 1), H(htb, 1), ainv[:, gcol:gcol + 1], psH[1],
                        OP.mult, OP.mult), [ev_h1, ev_mmC[1]])
                    ev_y1 = pg.add("gpsimd", lambda e: e.tensor_add(
                        H(ybuf16, 1), H(ybuf16, 1), H(bbxb, 1)), [ev_cy1])
                    t4_free = [ev_y0, ev_y1]
                ht_free = {0: [ev_cy0], 1: [ev_cy1, ev_st]}
                ps_free = {0: [ev_cy0], 1: [ev_cy1]}
                last_ev_y = ev_y1
            if R <= 3 or not DYN_ON:
                ystore_prev = pg.add("sync", lambda e, R=R: e.dma_start(
                    out=s_y[R % 32, :, :], in_=ybuf16[:]), [last_ev_y], inc=16,
                    sem="yst")
            else:
                eng_nm = store_engine(R)
                sem_nm = "ysw0" if eng_nm == "gpsimd" else "yst"
                ystore_prev = pg.add(eng_nm, lambda e, R=R: dyn_store(
                    e, NSUB + 1 + R, ybuf16[:], s_y), [last_ev_y],
                    inc=16, sem=sem_nm)
            if AG_ON:
                pub_deps = list(st_evs)
                if ag_ev_prev is not None:
                    pub_deps.append(ag_ev_prev)
                ev_pub = pg.add("sync", lambda e, R=R: e.dma_start(
                    out=cc_in, in_=stcols[:, R * NQ:(R + 1) * NQ]), pub_deps, inc=16,
                    sem="pub")
                ag_deps = [ev_pub]
                if R - 1 in gather_load_evs:
                    ag_deps.append(gather_load_evs[R - 1])
                ev_ag = pg.add("gpsimd", lambda e: e.collective_compute(
                    "AllGather", OP.bypass, replica_groups=[list(range(8))],
                    ins=[cc_in], outs=[cc_out]), ag_deps, sem="ag")
                goff_w = (R % 2) * 8 * NQ
                gl_deps = [ev_ag]
                if R in cmb_evs:
                    pass
                if R - 2 >= 0 and R in cmb_evs:
                    pass
                gl_deps.append(cmb_evs[R])
                gather_load_evs[R] = pg.add("sync", lambda e, goff_w=goff_w: e.dma_start(
                    out=gathbuf[:, goff_w:goff_w + 8 * NQ].rearrange(
                        "p (k q) -> p k q", k=8),
                    in_=cc_out.rearrange("(k p) q -> p k q", p=128)), gl_deps, inc=16,
                    sem="gath")
                ag_ev_prev = ev_ag
            else:
                gather_load_evs[R] = st_evs[-1]

        p2_done = [("vector", pg.count["vector"]), ("gpsimd", pg.count["gpsimd"]),
                   ("sync", pg.count["sync"]), ("scalar", pg.count["scalar"]),
                   ("tensor", pg.count["tensor"]), ("yst", pg.count["yst"]),
                   ("ysw0", pg.count["ysw0"]), ("ag", pg.count["ag"]),
                   ("slot0", pg.count["slot0"]), ("slot1", pg.count["slot1"]),
                   ("pub", pg.count["pub"]), ("gath", pg.count["gath"])]

        prev_reuse = p2_done
        p3_free_x16 = p2_done
        p3_free_xn16 = p2_done
        p3_free_psc = p2_done
        p3_free_bx = p2_done
        p3_ln_tail = p2_done
        for c in range(NCH if PHASE3 else 0):
            c0 = c * Lc
            ld_evs = []
            for db in range(NDB):
                ld_deps = (list(p2_done) if c == 0 else
                           [p3_free_bx[db], p3_ln_tail[0]])
                ld_evs.append(pg.add(
                    "sync", lambda e, db=db, c0=c0: e.dma_start(
                        out=dbs(b_x, db), in_=s_xmid[db, :, c0:c0 + Lc]),
                    ld_deps, inc=16))
            ysum_evs = []
            for db in range(NDB):
                q_evs = []
                for q in range(NQ):
                    q_evs.append(pg.add(
                        "sync", lambda e, db=db, q=q, c0=c0: e.dma_start(
                            out=ybuf16[:, q * Lc:(q + 1) * Lc],
                            in_=s_y[db * NQ + q, :, c0:c0 + Lc]),
                        (prev_reuse if c == 0 else p3_yst_guard) + ysum_evs[-2:],
                        inc=16))
                ev_xn = pg.add(
                    "sync", lambda e, db=db, c0=c0: e.dma_start(
                        out=dbs(b_t3, db), in_=s_dtxn[db, :, TOK + c0:TOK + c0 + Lc]),
                    (prev_reuse if c == 0 else [ysum_evs[-1] if ysum_evs else p3_ln_tail[0]]),
                    inc=16)
                ev_a1 = pg.add("vector", lambda e: e.tensor_add(
                    ybuf16[:, 4 * Lc:5 * Lc], ybuf16[:, 0:Lc], ybuf16[:, Lc:2 * Lc]),
                    q_evs + [ev_xn] + ysum_evs[-2:])
                ev_a2 = pg.add("vector", lambda e: e.tensor_add(
                    ybuf16[:, 5 * Lc:6 * Lc], ybuf16[:, 2 * Lc:3 * Lc],
                    ybuf16[:, 3 * Lc:4 * Lc]), q_evs)
                ev_s = pg.add("vector", lambda e, db=db: e.tensor_add(
                    dbs(b_y, db), ybuf16[:, 4 * Lc:5 * Lc], ybuf16[:, 5 * Lc:6 * Lc]),
                    [ev_a1, ev_a2])
                ev_s = pg.add("vector", lambda e, db=db: e.scalar_tensor_tensor(
                    dbs(b_y, db), dbs(b_t3, db), d_sb[:, db:db + 1], dbs(b_y, db),
                    OP.mult, OP.add), [ev_s, ev_xn])
                ev_s = pg.add("vector", lambda e, db=db: e.tensor_add(
                    dbs(b_x, db), dbs(b_x, db), dbs(b_y, db)), [ev_s, ld_evs[db]])
                ysum_evs.append(ev_s)
            xn_evs3, ev_mm3 = layernorm(b_x, b_xn, xn16, ysum_evs, ysum_evs,
                                        x16_free=p3_free_x16, sq_free=p3_free_xn16)
            x_evs3 = ffn(xn_evs3, b_x, ysum_evs, [xn_evs3[-1]],
                         psc_free=p3_free_psc)
            ost_evs = []
            for db in range(NDB):
                ost_evs.append(pg.add(
                    "sync", lambda e, db=db, c0=c0: e.dma_start(
                        out=out_h[db, :, c0:c0 + Lc], in_=dbs(b_x, db)),
                    [x_evs3[db], xn_evs3[-1]], inc=16, sem="ost"))
            p3_yst_guard = [ysum_evs[-1]]
            p3_free_x16 = [ev_mm3]
            p3_free_xn16 = [x_evs3[-1]]
            p3_free_psc = [x_evs3[-1]]
            p3_free_bx = [ost_evs[-1]] * NDB
            p3_ln_tail = [xn_evs3[-1]]
            prev_reuse = []

        for semk, eng in [("sync", "sync"), ("gpsimd", "gpsimd"), ("yst", "sync"),
                          ("ysw0", "gpsimd"), ("ag", "gpsimd"),
                          ("slot0", "sync"), ("slot1", "sync"), ("pub", "sync"),
                          ("gath", "sync"), ("spill", "sync"), ("ost", "sync"),
                          ("spx", "sync"), ("spn", "sync"), ("spd", "sync")]:
            lastv = pg.count[semk]
            if lastv:
                pg.add(eng, lambda e, semk=semk, lastv=lastv: e.wait_ge(
                    pg.sems[semk], lastv) and None, [], inc=0)

        block = E(nc.Block())
        pg.emit(block)
    return nc


def _prep_inputs(x_full, inp):
    B, Lb, _ = x_full.shape
    T = B * Lb
    x_fm = np.ascontiguousarray(x_full.reshape(T, DM).T)
    A_dm_n = inp["L0_attn_A"].reshape(DM, NST).astype(np.float32)
    W_in = inp["L0_attn_W_in"].astype(np.float32)
    b_in = inp["L0_attn_b_in"].astype(np.float32)
    W_dt1 = inp["L0_attn_W_dt1"].astype(np.float32)
    W_dt2 = inp["L0_attn_W_dt2"].astype(np.float32)
    base_w = inp["L0_ffn_base_w"].astype(np.float32)
    gate_w = inp["L0_ffn_gate_w"].astype(np.float32)
    lora_A = inp["L0_ffn_lora_A"].astype(np.float32)
    lora_B = inp["L0_ffn_lora_B"].astype(np.float32)
    Dv = inp["L0_attn_D"].astype(np.float32)
    b_dt2 = inp["L0_attn_b_dt2"].astype(np.float32).reshape(-1)
    base_b = inp["L0_ffn_base_b"].astype(np.float32).reshape(-1)
    a_cat = lora_A.transpose(2, 0, 1).reshape(DM, ERK)
    b_cat = lora_B.transpose(0, 2, 1).reshape(ERK, DM)
    sel_one = np.concatenate(
        [np.eye(NST, dtype=np.float32)[:, n:n + 1] * np.ones((1, 128), np.float32)
         for n in range(NST)], axis=1)
    shared = {
        "ones_col": np.full((128, 1), 1.0 / DM, np.float32),
        "ones_row": np.ones((1, 128), np.float32),
        "ident": np.eye(128, dtype=np.float32),
        "expand16": np.kron(np.eye(NEX, dtype=np.float32), np.ones((1, 16), np.float32)) * 2.0,
        "sel16": np.concatenate([sel_one, np.zeros((NST, NST * 128), np.float32),
                                 sel_one], axis=0),
        "fconsts": np.tile(np.array([[1e-5, -0.5]], np.float32), (128, 1)),
        "b_dt1": inp["L0_attn_b_dt1"].reshape(1, -1).astype(np.float32),
    }
    maps = []
    for core in range(8):
        p = core % 4
        perm_n = np.empty(NST, np.int64)
        for a in range(4):
            for j in range(4):
                perm_n[4 * a + j] = 4 * ((a - 2 * p) % 4) + j
        m = dict(shared)
        m["x"] = np.ascontiguousarray(
            x_fm[:, core * TOK:(core + 1) * TOK]).reshape(NDB, 128, TOK)
        m["w_in"] = np.concatenate(
            [W_in[:, perm_n], W_in[:, NST + perm_n]], axis=1)
        m["b_in"] = np.concatenate([b_in[perm_n], b_in[NST + perm_n]]).reshape(1, -1)
        m["w_dt1"] = W_dt1
        m["w_dt2"] = W_dt2
        m["base_wT"] = base_w.T
        m["gate_wT"] = gate_w.T
        m["a_cat"] = a_cat
        m["b_cat"] = b_cat
        m["b_dt2"] = b_dt2.reshape(1, -1)
        m["base_b"] = base_b.reshape(1, -1)
        m["D_sb"] = Dv.reshape(NDB, 128).T
        a2 = np.empty((128, NSUB * NQ), np.float32)
        sel = np.zeros((128, NSUB * 8), np.float32)
        slot_tbl = np.zeros((1, NSUB + 1), np.int32)
        y_tbl = np.zeros((1, NSUB), np.int32)
        for R in range(NSUB):
            g = (R - 2 * p) % 32
            phys_db = g // NQ
            phys_n0 = NQ * (g % NQ)
            slot_tbl[0, R] = phys_db
            y_tbl[0, R] = g
            for j in range(NQ):
                a2[:, R * NQ + j] = A_dm_n[phys_db * 128:(phys_db + 1) * 128,
                                           phys_n0 + j]
            if p > 0:
                sel[:, R * 8 + (core - 1)] = 1.0
        m["a2_sb"] = np.ascontiguousarray(a2)
        m["seltab"] = np.ascontiguousarray(sel)
        m["slot_tbl"] = slot_tbl
        m["y_tbl"] = y_tbl
        maps.append({k: (np.ascontiguousarray(v) if v.dtype == np.int32
                         else np.ascontiguousarray(v, np.float32))
                     for k, v in m.items()})
    return maps, None


def _unperm(core):
    return np.arange(DM, dtype=np.int64)


def kernel(**inputs):
    x_full = np.asarray(inputs["L0_attn_x"], np.float32)
    B, Lb, _ = x_full.shape
    maps, _ = _prep_inputs(x_full, {k: np.asarray(v, np.float32) for k, v in inputs.items()})
    nc = build_program()
    res = run_bass_kernel_spmd(nc, maps, core_ids=list(range(8)))
    T = B * Lb
    out_fm = np.empty((DM, T), np.float32)
    for core in range(8):
        perm_d = _unperm(core)
        out_fm[perm_d, core * TOK:(core + 1) * TOK] = \
            res.results[core]["out"].reshape(DM, TOK)
    return np.ascontiguousarray(out_fm.T).reshape(B, Lb, DM)


# version 13 (final)
# speedup vs baseline: 1.1780x; 1.0031x over previous
import contextlib
import numpy as np
import concourse.bass as bass
import concourse.mybir as mybir
from concourse.bass_utils import run_bass_kernel_spmd

F32 = mybir.dt.float32
BF16 = mybir.dt.bfloat16
I32 = mybir.dt.int32
AF = mybir.ActivationFunctionType
OP = mybir.AluOpType
AX = mybir.AxisListType

DM = 1024
NDB = 8
NST = 16
RK = 64
NEX = 8
NGT = 10
ERK = 128
Lc = 512
TOK = 4096
NCH = TOK // Lc
NQ = 4
NSUB = NQ * NDB + 6
NPC = Lc

P1_CUT = 8
P1_RUN = NCH
P2_RUN = NSUB
PHASE3 = True
AG_ON = True
DYN_ON = True
NPieces = TOK // NPC


class Prog:
    ENGINES = ("sync", "tensor", "scalar", "vector", "gpsimd")

    def __init__(self, nc):
        self.nc = nc
        self.ops = {e: [] for e in self.ENGINES}
        self.count = {}
        self.ecount = {e: 0 for e in self.ENGINES}
        self.sems = {}
        self.extra_sems = []

    def make_sems(self, E):
        for e in self.ENGINES:
            self.sems[e] = E(self.nc.semaphore(f"sem_{e}"))
            self.count[e] = 0
        for s in self.extra_sems:
            self.sems[s] = E(self.nc.semaphore(f"sem_{s}"))
            self.count[s] = 0

    def add(self, engine, fn, deps=(), inc=1, sem=None):
        sem = sem or engine
        self.count[sem] += inc
        self.ecount[engine] += 1
        ev = (sem, self.count[sem])
        self.ops[engine].append((fn, tuple(deps), inc, sem, self.ecount[engine]))
        if not hasattr(self, "evmap"):
            self.evmap = {}
        self.evmap[ev] = (engine, self.ecount[engine])
        return ev

    def eev(self, engine):
        return (engine, self.count[engine])

    def emit(self, block):
        sems = self.sems

        def run(eng_name):
            def body(eng):
                waited = {s: 0 for s in self.count}
                drained = 0
                dma_engine = eng_name == "sync"
                for fn, deps, inc, sem, eidx in self.ops[eng_name]:
                    need = {}
                    se_need = 0
                    for (de, dc) in deps:
                        if de == eng_name and not dma_engine:
                            se_need = max(se_need, self.evmap.get((de, dc), (de, 0))[1])
                            continue
                        need[de] = max(need.get(de, 0), dc)
                    for de, dc in need.items():
                        if waited[de] < dc:
                            eng.wait_ge(sems[de], dc)
                            waited[de] = dc
                    if se_need > drained:
                        eng.drain()
                        drained = self.ecount_at(eng_name, eidx)
                    instr = fn(eng)
                    if inc:
                        assert instr is not None
                        instr.then_inc(sems[sem], inc)
            return body

        block.sync(run("sync"))
        block.tensor(run("tensor"))
        block.scalar(run("scalar"))
        block.vector(run("vector"))
        block.gpsimd(run("gpsimd"))

    def ecount_at(self, eng, eidx):
        return eidx - 1


def build_program():
    nc = bass.Bass("TRN2", target_bir_lowering=False)

    dr = lambda n, s, k="ExternalInput", dt=F32: nc.dram_tensor(n, s, dt, kind=k).ap()
    x_h = dr("x", [NDB, 128, TOK])
    out_h = dr("out", [NDB, 128, TOK], "ExternalOutput")
    win_h = dr("w_in", [DM, 2 * NST])
    wdt1_h = dr("w_dt1", [DM, RK])
    wdt2_h = dr("w_dt2", [RK, DM])
    basew_h = dr("base_wT", [DM, DM])
    gate_h = dr("gate_wT", [DM, NGT])
    acat_h = dr("a_cat", [DM, ERK])
    bcat_h = dr("b_cat", [ERK, DM])
    a2_h = dr("a2_sb", [128, NSUB * NQ])
    d_h = dr("D_sb", [128, NDB])
    bin_h = dr("b_in", [1, 2 * NST])
    bdt1_h = dr("b_dt1", [1, RK])
    bdt2_h = dr("b_dt2", [1, DM])
    baseb_h = dr("base_b", [1, DM])
    onescol_h = dr("ones_col", [128, 1])
    onesrow_h = dr("ones_row", [1, 128])
    ident_h = dr("ident", [128, 128])
    expand_h = dr("expand16", [NEX, 128])
    sel16_h = dr("sel16", [48, NST * 128])
    fc_h = dr("fconsts", [128, 2])
    seltab_h = dr("seltab", [128, NSUB * 8])
    slottab_h = dr("slot_tbl", [1, NSUB + 1], dt=I32)
    ytab_h = dr("y_tbl", [1, NSUB], dt=I32)
    s_dtxn = dr("s_dtxn", [NDB + 1, 128, 2 * TOK], "Internal")
    s_xmid = dr("s_xmid", [NDB, 128, TOK], "Internal")
    s_y = dr("s_y", [NQ * NDB, 128, TOK], "Internal", dt=BF16)
    cc_in = dr("cc_in", [128, NQ], "Internal")
    cc_out = dr("cc_out", [8 * 128, NQ], "Internal")

    with contextlib.ExitStack() as stack:
        E = stack.enter_context
        sb = lambda n, s, dt=F32: E(nc.sbuf_tensor(n, s, dt))

        bb0 = sb("bb0", [128, 2 * TOK])
        bb1 = sb("bb1", [128, 2 * TOK])
        b_x = bb0[:, 0:TOK]
        b_xn = bb0[:, TOK:2 * TOK]
        b_dt = bb1[:, 0:TOK]
        b_y = bb1[:, TOK:2 * TOK]
        b_t1 = sb("b_t1", [128, TOK])
        b_t2 = sb("b_t2", [128, TOK])
        b_t3 = sb("b_t3", [128, TOK])
        b_t4 = sb("b_t4", [128, TOK], BF16)
        ybuf16 = sb("ybuf16", [128, TOK], BF16)
        xn16 = sb("xn16", [128, TOK], BF16)
        x16b = sb("x16b", [128, TOK], BF16)
        xsq16 = xn16
        x16 = b_t4
        bcpack = sb("bcpack", [32 + NST, TOK], BF16)
        bmf = bcpack[0:NST, :]
        cmf = bcpack[32:32 + NST, :]
        r_mean = sb("r_mean", [1, Lc])
        r_msq = sb("r_msq", [1, Lc])
        r_rstd = sb("r_rstd", [1, Lc])
        r_mean16 = sb("r_mean16", [1, Lc], BF16)
        r_rstd16 = sb("r_rstd16", [1, Lc], BF16)
        gm1 = sb("gm1", [128, 4])
        gnm1 = sb("gnm1", [128, 4])
        gm2 = sb("gm2", [128, 4])
        ge = sb("ge", [128, 64])
        ge2 = None
        goh1 = sb("goh1", [128, 64])
        goh1n = None
        ge2 = b_t2[:, 0:64]
        goh1n = b_t2[:, 64:128]
        gtw = sb("gtw", [128, 64])
        gden = sb("gden", [128, 4])
        giz = sb("giz", [128, 4])
        gdf = sb("gdf", [128, 4])
        ew8 = sb("ew8", [128, 32])
        logit_sb = None
        ewfm16 = sb("ewfm16", [NEX, Lc], BF16)
        ewexp16 = sb("ewexp16", [128, Lc], BF16)
        zw16 = sb("zw16", [128, Lc], BF16)
        p116 = sb("p116", [RK, Lc], BF16)
        win16 = sb("win16", [128, NDB * 2 * NST], BF16)
        wdt1_16 = sb("wdt1_16", [128, NDB * RK], BF16)
        wdt2_16 = sb("wdt2_16", [RK, DM], BF16)
        basew16 = sb("basew16", [128, NDB * DM], BF16)
        gate16 = sb("gate16", [128, NDB * NGT], BF16)
        acat16 = sb("acat16", [128, NDB * ERK], BF16)
        bcat16 = sb("bcat16", [128, DM], BF16)
        a_sb = sb("a_sb", [128, NSUB * NQ])
        ainv = sb("ainv", [128, NSUB * NQ])
        d_sb = sb("d_sb", [128, NDB])
        seltab = sb("seltab_s", [128, NSUB * 8], BF16)
        tabs = sb("tabs", [1, 2 * NSUB + 1], I32)
        stcols = sb("stcols", [128, NSUB * NQ])
        gathbuf = sb("gathbuf", [128, 2 * 8 * NQ])
        init4 = sb("init4", [128, NQ])
        nbias = 2 * NST + RK + DM + DM
        braws = b_t1[0:1, 0:nbias]
        blo_f = b_t3[0:1, 0:nbias]
        bhi = sb("bhi16", [1, nbias], BF16)
        blo = sb("blo16", [1, nbias], BF16)
        onescol16 = sb("onescol16", [128, 1], BF16)
        onesrow16 = sb("onesrow16", [1, 128], BF16)
        onesL16_t = sb("onesL16s", [1, Lc], BF16)
        onesL16 = onesL16_t[0:1, :]
        onesL32 = onesL16
        ident_sb = sb("ident_sb", [128, 128])
        expand16 = sb("expand16sb", [NEX, 128], BF16)
        sel16sb = sb("sel16sb", [48, NST * 128], BF16)
        fc_sb = sb("fc_sb", [128, 2])

        ps8 = E(nc.psum_tensor("ps8", [128, TOK], F32))
        ps_st = ps8[0:33, 0 * Lc:1 * Lc]
        ps_a = ps8[:, 1 * Lc:2 * Lc]
        ps_b = ps8[:, 2 * Lc:3 * Lc]
        ps_c = ps8[:, 3 * Lc:4 * Lc]
        ps_d = ps8[:, 4 * Lc:5 * Lc]

        wstage = b_t2[:, 0:2048]
        pg = Prog(nc)
        pg.extra_sems = ["slot0", "slot1", "yst", "pub", "gath", "spill", "ost",
                         "ysw0", "ag", "spx", "spn", "spd"]
        pg.make_sems(E)

        OB_IN, OB_DT1, OB_DT2, OB_BB = 0, 2 * NST, 2 * NST + RK, 2 * NST + RK + DM

        def ld_dma(dst, src, deps=()):
            return pg.add("sync", lambda e, d=dst, s=src: e.dma_start(out=d, in_=s),
                          deps, inc=16)

        def vcast(dst, src, deps):
            return pg.add("vector", lambda e, d=dst, s=src: e.tensor_copy(d, s), deps)

        ld_dma(a_sb[:], a2_h)
        ld_dma(tabs[0:1, 0:NSUB + 1], slottab_h)
        ld_dma(tabs[0:1, NSUB + 1:2 * NSUB + 1], ytab_h)
        ld_dma(d_sb[:], d_h)
        ld_dma(ident_sb[:], ident_h)
        ld_dma(fc_sb[:], fc_h)
        ld_dma(braws[0:1, OB_IN:OB_IN + 2 * NST], bin_h)
        ld_dma(braws[0:1, OB_DT1:OB_DT1 + RK], bdt1_h)
        ld_dma(braws[0:1, OB_DT2:OB_DT2 + DM], bdt2_h)
        evA = ld_dma(braws[0:1, OB_BB:OB_BB + DM], baseb_h)
        pg.add("vector", lambda e: e.reciprocal(ainv[:], a_sb[:]), [evA])
        pg.add("vector", lambda e: e.memset(onesL16, 1.0), [])
        pg.add("vector", lambda e: e.memset(ge[:], 0.0), [])
        pg.add("vector", lambda e: e.memset(goh1[:], 0.0), [])
        pg.add("vector", lambda e: e.memset(stcols[:], 0.0), [])
        pg.add("vector", lambda e: e.memset(gathbuf[:], 0.0), [])
        ev = vcast(bhi[0:1, :], braws, [evA])
        ev = pg.add("vector", lambda e: e.tensor_sub(blo_f, braws, bhi[0:1, :]), [ev])
        ev_blo = vcast(blo[0:1, :], blo_f, [ev])
        pc = [ev_blo]

        def staged(dst, src, p_rows, p_cols):
            ev_l = ld_dma(wstage[0:p_rows, 0:p_cols], src, [pc[0]])
            pc[0] = vcast(dst, wstage[0:p_rows, 0:p_cols], [ev_l])

        staged(seltab[:], seltab_h, 128, NSUB * 8)
        staged(expand16[:], expand_h, NEX, 128)
        staged(sel16sb[0:48, :], sel16_h, 48, NST * 128)
        staged(onescol16[:], onescol_h, 128, 1)
        staged(onesrow16[:], onesrow_h, 1, 128)
        for k in range(NDB):
            staged(win16[:, k * 2 * NST:(k + 1) * 2 * NST],
                   win_h[k * 128:(k + 1) * 128, :], 128, 2 * NST)
        for k in range(NDB):
            staged(wdt1_16[:, k * RK:(k + 1) * RK],
                   wdt1_h[k * 128:(k + 1) * 128, :], 128, RK)
        for k in range(NDB):
            staged(basew16[:, k * DM:(k + 1) * DM],
                   basew_h[k * 128:(k + 1) * 128, :], 128, DM)
        for k in range(NDB):
            staged(gate16[:, k * NGT:(k + 1) * NGT],
                   gate_h[k * 128:(k + 1) * 128, :], 128, NGT)
        for k in range(NDB):
            staged(acat16[:, k * ERK:(k + 1) * ERK],
                   acat_h[k * 128:(k + 1) * 128, :], 128, ERK)
        staged(wdt2_16[:], wdt2_h, RK, DM)
        staged(bcat16[:], bcat_h, 128, DM)
        setup_done = ("vector", pg.count["vector"])

        def dbs(buf, db):
            return buf[:, db * Lc:(db + 1) * Lc]

        def layernorm(src, dst, dst16, in_evs, ps_guard, x16_free=(), sq_free=(),
                      sqbuf=None, cast_free=(), xbuf=None):
            sq = sqbuf if sqbuf is not None else xsq16
            xb = xbuf if xbuf is not None else x16
            ev_c = pg.add("vector", lambda e, xb=xb: e.tensor_copy(xb[:], src[:]),
                          list(in_evs) + list(x16_free))
            ev_sq = pg.add("scalar", lambda e, sq=sq, xb=xb: e.activation(sq[:], xb[:], AF.Square),
                           [ev_c] + list(sq_free))

            def stats(eng):
                for k in range(NDB):
                    eng.matmul(ps_st[0:1, :], onescol16[:], dbs(xb, k),
                               start=(k == 0), stop=(k == NDB - 1))
                ii = None
                for k in range(NDB):
                    ii = eng.matmul(ps_st[32:33, :], onescol16[:], dbs(sq, k),
                                    start=(k == 0), stop=(k == NDB - 1))
                return ii
            ev_mm = pg.add("tensor", stats, [ev_c, ev_sq, setup_done])
            ev_r0a = pg.add("scalar", lambda e: e.copy(r_mean[:], ps_st[0:1, :]), [ev_mm])
            ev_r0 = pg.add("scalar", lambda e: e.copy(r_msq[:], ps_st[32:33, :]), [ev_r0a])
            ev_v1 = pg.add("vector", lambda e: e.tensor_mul(r_rstd[:], r_mean[:], r_mean[:]), [ev_r0])
            ev_v2 = pg.add("vector", lambda e: e.tensor_sub(r_msq[:], r_msq[:], r_rstd[:]), [ev_v1])
            ev_s1 = pg.add("scalar", lambda e: e.activation(r_msq[:], r_msq[:], AF.Ln, bias=fc_sb[0:1, 0:1]), [ev_v2])
            ev_s2 = pg.add("scalar", lambda e: e.activation(r_rstd[:], r_msq[:], AF.Exp, scale=fc_sb[0:1, 1:2]), [ev_s1])
            ev_m16 = pg.add("vector", lambda e: e.tensor_copy(r_mean16[:], r_mean[:]), [ev_r0])
            ev_m16b = pg.add("vector", lambda e: e.tensor_copy(r_rstd16[:], r_rstd[:]), [ev_s2])
            ev_rep = pg.add("tensor", lambda e: (
                e.matmul(ps_a[:, :], onesrow16[:], r_mean16[:], start=True, stop=True),
                e.matmul(ps_b[:, :], onesrow16[:], r_rstd16[:], start=True, stop=True))[-1],
                [ev_m16, ev_m16b] + list(ps_guard))
            evs = []
            for db in range(NDB):
                e1 = pg.add("vector", lambda e, db=db: e.tensor_sub(dbs(dst, db), dbs(src, db), ps_a[:, :]), [ev_rep])
                e2 = pg.add("vector", lambda e, db=db: e.tensor_mul(dbs(dst, db), dbs(dst, db), ps_b[:, :]), [e1])
                e3 = pg.add("vector", lambda e, db=db: e.tensor_copy(dbs(dst16, db), dbs(dst, db)),
                            [e2] + list(cast_free))
                evs.append(e3)
            return evs, ev_mm

        def k1_into(eng, ps, rows_n, col_off, is_last):
            eng.matmul(ps[0:rows_n, :], bhi[0:1, col_off:col_off + rows_n],
                       onesL16, start=False, stop=False)
            return eng.matmul(ps[0:rows_n, :], blo[0:1, col_off:col_off + rows_n],
                              onesL32, start=False, stop=is_last)

        def ffn(xn_evs, x_buf, x_in_evs, ps_guard, psc_free=()):
            def gate_mm(eng):
                ii = None
                for k in range(NDB):
                    ii = eng.matmul(ps_c[0:NGT, :], gate16[:, k * NGT:(k + 1) * NGT],
                                    dbs(xn16, k), start=(k == 0), stop=(k == NDB - 1))
                return ii
            ev_g = pg.add("tensor", gate_mm, list(xn_evs) + list(psc_free))
            ev_ge = pg.add("scalar", lambda e: e.copy(b_y[0:NGT, 0:Lc], ps_c[0:NGT, :]), [ev_g])
            nblk = Lc // 128

            def transp(eng):
                ii = None
                for j in range(nblk):
                    ii = eng.transpose(ps_d[:, j * 16:j * 16 + NGT],
                                       b_y[0:NGT, j * 128:(j + 1) * 128],
                                       ident_sb[0:NGT, 0:NGT])
                return ii
            ev_t = pg.add("tensor", transp, [ev_ge, ev_g])

            def blkcols(buf, j, w=NGT):
                return buf[:, j * 16:j * 16 + w]
            for j in range(nblk):
                ev = pg.add("vector", lambda e, j=j: e.reduce_max(gm1[:, j:j + 1], blkcols(ps_d, j), axis=AX.X), [ev_t])
            for j in range(nblk):
                ev = pg.add("vector", lambda e, j=j: e.reduce_max(gnm1[:, j:j + 1], blkcols(ps_d, j), axis=AX.X, negate=True), [ev_t])
            ev_mx = ev
            evs_e = []
            for j in range(nblk):
                evs_e.append(pg.add("scalar", lambda e, j=j: e.activation(
                    blkcols(ge, j), blkcols(ps_d, j), AF.Exp, bias=gnm1[:, j:j + 1]), [ev_mx]))
            for j in range(nblk):
                ev = pg.add("vector", lambda e, j=j: e.tensor_scalar(
                    blkcols(goh1, j), blkcols(ps_d, j), gm1[:, j:j + 1], None, OP.is_equal),
                    [ev_mx] + evs_e)
            ev = pg.add("vector", lambda e: e.tensor_scalar(
                goh1n[:], goh1[:], -1.0, 1.0, OP.mult, OP.add), [ev])
            ev = pg.add("vector", lambda e: e.tensor_mul(ge2[:], ge[:], goh1n[:]), [ev] + evs_e)
            ev_e2 = ev
            for j in range(nblk):
                ev = pg.add("vector", lambda e, j=j: e.reduce_max(gm2[:, j:j + 1], blkcols(ge2, j), axis=AX.X), [ev_e2])
            ev_sp = pg.add("vector", lambda e: e.tensor_copy(giz[:], gm2[:]), [ev])
            for j in range(nblk):
                ev = pg.add("vector", lambda e, j=j: e.tensor_scalar(
                    blkcols(ge2, j), blkcols(ge2, j), gm2[:, j:j + 1], None, OP.is_equal), [ev_sp])
            for j in range(nblk):
                ev = pg.add("vector", lambda e, j=j: e.scalar_tensor_tensor(
                    blkcols(gtw, j), blkcols(ge2, j), gm2[:, j:j + 1], blkcols(goh1, j),
                    OP.mult, OP.add), [ev])
            for j in range(nblk):
                ev = pg.add("vector", lambda e, j=j: e.reduce_sum(
                    gden[:, j:j + 1], blkcols(gtw, j, NEX), axis=AX.X), [ev])
            ev = pg.add("vector", lambda e: e.tensor_single_scalar(giz[:], gden[:], 0.0, OP.is_equal), [ev])
            ev = pg.add("vector", lambda e: e.tensor_add(gdf[:], gden[:], giz[:]), [ev])
            ev = pg.add("vector", lambda e: e.reciprocal(gm2[:], gdf[:]), [ev])
            ev_sp2 = pg.add("vector", lambda e: e.tensor_copy(gdf[:], gm2[:]), [ev])
            evw = []
            for j in range(nblk):
                evw.append(pg.add("vector", lambda e, j=j: e.tensor_scalar(
                    ew8[:, j * 8:(j + 1) * 8], blkcols(gtw, j, NEX), gm2[:, j:j + 1], None,
                    OP.mult), [ev_sp2]))

            def transb(eng):
                ii = None
                for j in range(nblk):
                    ii = eng.transpose(ps_d[0:NEX, j * 128:(j + 1) * 128],
                                       ew8[:, j * 8:(j + 1) * 8], ident_sb[:, :])
                return ii
            ev_tb = pg.add("tensor", transb, evw)
            ev_efm = pg.add("scalar", lambda e: e.copy(ewfm16[:], ps_d[0:NEX, :]), [ev_tb])
            ev_ex = pg.add("tensor", lambda e: e.matmul(
                ps_d[:, :], expand16[:], ewfm16[:], start=True, stop=True), [ev_efm])
            ev_ex16 = pg.add("scalar", lambda e: e.copy(ewexp16[:], ps_d[:, :]), [ev_ex])

            def zmm(eng):
                ii = None
                for k in range(NDB):
                    ii = eng.matmul(ps_c[:, :], acat16[:, k * ERK:(k + 1) * ERK],
                                    dbs(xn16, k), start=(k == 0), stop=(k == NDB - 1))
                return ii
            ev_z = pg.add("tensor", zmm, list(xn_evs) + [ev_ge])
            ev_zw = pg.add("vector", lambda e: e.tensor_mul(zw16[:], ps_c[:, :], ewexp16[:]),
                           [ev_z, ev_ex16])
            x_evs = []
            prev_ep = {0: None, 1: None}
            for m in range(NDB):
                ps = ps_a if (m % 2 == 0) else ps_b

                def bmm(eng, m=m, ps=ps):
                    for k in range(NDB):
                        eng.matmul(ps[:, :], basew16[:, k * DM + m * 128:k * DM + (m + 1) * 128],
                                   dbs(xn16, k), start=(k == 0), stop=False)
                    eng.matmul(ps[:, :], bcat16[:, m * 128:(m + 1) * 128], zw16[:],
                               start=False, stop=False)
                    return k1_into(eng, ps, 128, OB_BB + m * 128, True)
                deps = list(xn_evs) + [ev_zw]
                if prev_ep[m % 2] is not None:
                    deps.append(prev_ep[m % 2])
                if m < 2:
                    deps += list(ps_guard)
                ev_b = pg.add("tensor", bmm, deps)
                ev_ep = pg.add("vector", lambda e, m=m, ps=ps: e.scalar_tensor_tensor(
                    dbs(x_buf, m), ps[:, :], 0.5, dbs(x_buf, m),
                    OP.mult, OP.add), [ev_b] + list(x_in_evs))
                prev_ep[m % 2] = ev_ep
                x_evs.append(ev_ep)
            return x_evs

        prev_reuse = [setup_done]
        free_bx = [setup_done] * NDB
        free_x16 = [setup_done]
        free_xn16 = [setup_done]
        free_sq1 = [setup_done]
        free_x16b = [setup_done]
        free_psc = [setup_done]
        free_bdt = [setup_done]
        free_bxn = [setup_done]
        guard1 = [setup_done]
        for c in range(P1_RUN):
            c0 = c * Lc
            ld_evs = []
            for db in range(NDB):
                ld_evs.append(pg.add(
                    "sync", lambda e, db=db, c0=c0: e.dma_start(
                        out=dbs(b_x, db), in_=x_h[db, :, c0:c0 + Lc]),
                    [free_bx[db]], inc=16))
            x_ev = ld_evs[-1]
            if P1_CUT < 2:
                prev_reuse = ld_evs
                continue
            xn_evs, ev_mm1 = layernorm(b_x, b_xn, xn16, ld_evs, guard1,
                                       x16_free=free_x16b, sq_free=free_sq1,
                                       sqbuf=ybuf16, cast_free=free_xn16, xbuf=x16b)
            if P1_CUT < 3:
                prev_reuse = xn_evs
                continue
            x_evs = ffn(xn_evs, b_x, ld_evs, [xn_evs[-1]], psc_free=free_psc)
            if P1_CUT < 4:
                prev_reuse = x_evs
                continue
            xn_evs2, ev_mm2 = layernorm(b_x, b_xn, xn16, [x_evs[-1]] + x_evs, x_evs)
            if P1_CUT < 5:
                prev_reuse = xn_evs2
                continue
            sp_evs = []
            sp_xmid = []
            for db in range(NDB):
                sp_xmid.append(pg.add(
                    "sync", lambda e, db=db, c0=c0: e.dma_start(
                        out=s_xmid[db, :, c0:c0 + Lc], in_=dbs(b_x, db)),
                    [x_evs[db], xn_evs2[-1]], inc=16, sem="spx"))
            sp_xn = []
            for db in range(NDB):
                sp_xn.append(pg.add(
                    "sync", lambda e, db=db, c0=c0: e.dma_start(
                        out=s_dtxn[db, :, TOK + c0:TOK + c0 + Lc], in_=dbs(b_xn, db)),
                    [xn_evs2[db]], inc=16, sem="spn"))
            sp_evs = sp_xmid + sp_xn

            def bcmm(eng):
                for k in range(NDB):
                    eng.matmul(ps_c[0:NST, :], win16[:, k * 2 * NST:k * 2 * NST + NST],
                               dbs(xn16, k), start=(k == 0), stop=False)
                eng.matmul(ps_c[0:NST, :], bhi[0:1, OB_IN:OB_IN + NST],
                           onesL16, start=False, stop=False)
                eng.matmul(ps_c[0:NST, :], blo[0:1, OB_IN:OB_IN + NST],
                           onesL32, start=False, stop=True)
                for k in range(NDB):
                    eng.matmul(ps_c[32:32 + NST, :],
                               win16[:, k * 2 * NST + NST:(k + 1) * 2 * NST],
                               dbs(xn16, k), start=(k == 0), stop=False)
                eng.matmul(ps_c[32:32 + NST, :], bhi[0:1, OB_IN + NST:OB_IN + 2 * NST],
                           onesL16, start=False, stop=False)
                return eng.matmul(ps_c[32:32 + NST, :], blo[0:1, OB_IN + NST:OB_IN + 2 * NST],
                                  onesL32, start=False, stop=True)
            if P1_CUT < 6:
                prev_reuse = sp_evs + [xn_evs2[-1]]
                continue
            ev_bc = pg.add("tensor", bcmm, list(xn_evs2))
            ev_bm = pg.add("scalar", lambda e, c0=c0: e.copy(bmf[:, c0:c0 + Lc], ps_c[0:NST, :]), [ev_bc])
            ev_cm = pg.add("scalar", lambda e, c0=c0: e.copy(cmf[:, c0:c0 + Lc], ps_c[32:32 + NST, :]), [ev_bc])

            def p1mm(eng):
                for k in range(NDB):
                    eng.matmul(ps_c[0:RK, :], wdt1_16[:, k * RK:(k + 1) * RK],
                               dbs(xn16, k), start=(k == 0), stop=False)
                return k1_into(eng, ps_c, RK, OB_DT1, True)
            if P1_CUT < 7:
                prev_reuse = sp_evs + [xn_evs2[-1], ev_bm, ev_cm]
                continue
            ev_p1 = pg.add("tensor", p1mm, list(xn_evs2) + [ev_bm, ev_cm])
            ev_p116 = pg.add("scalar", lambda e: e.copy(p116[:], ps_c[0:RK, :]), [ev_p1])
            exp_evs = []
            prev = ev_p116
            for db in range(NDB):
                def dtmm(eng, db=db):
                    eng.matmul(ps_c[:, :], wdt2_16[:, db * 128:(db + 1) * 128], p116[:],
                               start=True, stop=False)
                    return k1_into(eng, ps_c, 128, OB_DT2 + db * 128, True)
                ev_m = pg.add("tensor", dtmm, [ev_p116, prev])
                ev_e = pg.add("scalar", lambda e, db=db: e.activation(
                    dbs(b_y, db), ps_c[:, :], AF.Exp), [ev_m])
                prev = ev_e
                exp_evs.append(ev_e)
            if P1_CUT < 8:
                prev_reuse = sp_evs + [xn_evs2[-1], exp_evs[-1]]
                continue
            dt_sp = []
            cl_evs = []
            for db in range(NDB):
                ev_l = pg.add("scalar", lambda e, db=db: e.activation(
                    dbs(b_dt, db), dbs(b_y, db), AF.Ln, bias=1.0),
                    [exp_evs[db]] + free_bdt)
                ev_cl = pg.add("vector", lambda e, db=db: e.tensor_scalar(
                    dbs(b_dt, db), dbs(b_dt, db), 1e-7, 1e6, OP.max, OP.min), [ev_l])
                cl_evs.append(ev_cl)
                dt_sp.append(pg.add(
                    "sync", lambda e, db=db, c0=c0: e.dma_start(
                        out=s_dtxn[db, :, c0:c0 + Lc], in_=dbs(b_dt, db)),
                    [ev_cl], inc=16, sem="spd"))
            free_bx = [sp_xmid[-1]] * NDB
            free_x16 = [ev_mm2]
            free_xn16 = [ev_p1]
            free_sq1 = [ev_mm1]
            free_x16b = [ev_mm1]
            free_psc = [exp_evs[-1]]
            free_bdt = [dt_sp[-1]]
            free_bxn = sp_xn
            guard1 = [sp_xn[-1], xn_evs2[-1]]
            prev_reuse = sp_evs + dt_sp + [xn_evs2[-1], cl_evs[-1], exp_evs[-1]]
        p1_done_s = ("sync", pg.count["sync"])
        p1_done_sp = ("spd", pg.count["spd"])
        p1_done_spx = ("spx", pg.count["spx"])
        p1_done_spn = ("spn", pg.count["spn"])
        p1_done_v = ("vector", pg.count["vector"])
        p1_done_sc = ("scalar", pg.count["scalar"])
        p1_done_t = ("tensor", pg.count["tensor"])

        abuf = [b_t1, b_t2]
        bbxb = b_t4
        htb = b_t3

        setup_g = [p1_done_s, p1_done_v, p1_done_sc, p1_done_t, p1_done_sp,
                   p1_done_spx, p1_done_spn]
        bbp = [bb0, bb1]

        _dyn_ctr = [0]

        def dyn_load(eng, tab_idx, dst, scratch):
            _dyn_ctr[0] += 1
            i = _dyn_ctr[0]
            with eng.register(f"ds{i}") as slot, eng.register(f"do{i}") as off:
                eng.reg_load(slot, tabs[0:1, tab_idx:tab_idx + 1])
                eng.reg_mul(off, slot, 128 * 2 * TOK)
                srcap = bass.AP(tensor=scratch.tensor, offset=off,
                                ap=[[2 * TOK, 128], [1, 2 * TOK]])
                return eng.dma_start(out=dst, in_=srcap)

        def dyn_store(eng, tab_idx, src_buf, scratch):
            _dyn_ctr[0] += 1
            i = _dyn_ctr[0]
            with eng.register(f"ds{i}") as slot, eng.register(f"do{i}") as off:
                eng.reg_load(slot, tabs[0:1, tab_idx:tab_idx + 1])
                eng.reg_mul(off, slot, 128 * TOK)
                dstap = bass.AP(tensor=scratch.tensor, offset=off,
                                ap=[[TOK, 128], [1, TOK]])
                return eng.dma_start(out=dstap, in_=src_buf)

        def load_engine(T):
            return "sync" if T < 27 else "scalar"

        def store_engine(R):
            return "scalar" if R < 16 else "gpsimd"

        buf_loaded = {0: [], 1: []}
        buf_loaded[0] = [pg.add("sync", lambda e: e.dma_start(
            out=bbp[0][:], in_=s_dtxn[0, :, :]), setup_g, inc=16, sem="slot0")]
        buf_cons = {0: list(setup_g), 1: list(setup_g)}
        ab_free = {0: None, 1: None}
        t4_free = []
        xn16_free = {0: [], 1: []}
        ht_free = {0: [], 1: []}
        ps_free = {0: [p1_done_v, p1_done_sc], 1: [p1_done_v, p1_done_sc]}
        ystore_prev = None
        gather_load_evs = {}
        cmb_evs = {}
        ag_ev_prev = None

        for R in range(P2_RUN):
            par = R % 2
            npar = 1 - par
            dtq = bbp[par][:, 0:TOK]
            xnq = bbp[par][:, TOK:2 * TOK]
            if R + 1 < P2_RUN:
                T = R + 1
                if T <= 3 or not DYN_ON:
                    lds = pg.add(load_engine(T) if DYN_ON else "sync",
                                 lambda e, npar=npar: e.dma_start(
                                     out=bbp[npar][:], in_=s_dtxn[0, :, :]),
                                 buf_cons[npar], inc=16, sem=f"slot{npar}")
                else:
                    lds = pg.add(load_engine(T),
                                 lambda e, T=T, npar=npar: dyn_load(
                                     e, T, bbp[npar][:], s_dtxn),
                                 buf_cons[npar], inc=16, sem=f"slot{npar}")
                buf_loaded[npar] = [lds]
                buf_cons[npar] = []
            if R < 2:
                ev_cmb = pg.add("vector", lambda e: e.memset(init4[:], 0.0), [])
            else:
                glev = gather_load_evs[R - 2]
                goff = ((R - 2) % 2) * 8 * NQ
                ev_cmb = pg.add("vector", lambda e: e.memset(init4[:], 0.0),
                                [glev])
                for k in range(8):
                    scol = R * 8 + k
                    ev_cmb = pg.add("vector", lambda e, k=k, scol=scol, goff=goff:
                                    e.scalar_tensor_tensor(
                        init4[:], gathbuf[:, goff + k * NQ:goff + (k + 1) * NQ],
                        seltab[:, scol:scol + 1],
                        init4[:], OP.mult, OP.add), [ev_cmb, glev])
            ev_cmb = pg.add("vector", lambda e: e.tensor_copy(giz[:], gm2[:]), [ev_cmb])
            cmb_evs[R] = ev_cmb

            st_evs = []
            last_y = ystore_prev
            last_ev_y = None
            HW2 = TOK // 2
            for j in range(NQ):
                gcol = R * NQ + j
                rho = NQ * (R % NQ) + j
                ab = abuf[j % 2]
                ab_deps = buf_loaded[par] + [ev_cmb]
                if ab_free[j % 2] is not None:
                    ab_deps.append(ab_free[j % 2])
                ev_ab = pg.add("scalar", lambda e, ab=ab, gcol=gcol, dtq=dtq: e.activation(
                    ab[:], dtq, AF.Exp, scale=a_sb[:, gcol:gcol + 1]), ab_deps)
                buf_cons[par].append(ev_ab)

                def mmh(eng, rho, base, h):
                    ii = None
                    for pc in range(4):
                        c0 = h * HW2 + pc * NPC
                        ii = eng.matmul(ps8[:, (h * 4 + pc) * NPC:(h * 4 + pc + 1) * NPC],
                                        sel16sb[base:base + NST, rho * 128:(rho + 1) * 128],
                                        (bmf if base == 0 else cmf)[:, c0:c0 + NPC],
                                        start=True, stop=True)
                    return ii

                def H(buf, h):
                    return buf[:, h * HW2:(h + 1) * HW2]

                psH = [ps8[:, 0:HW2], ps8[:, HW2:TOK]]
                ev_mmB = {}
                ev_bm16 = {}
                for h in (0, 1):
                    ev_mmB[h] = pg.add("tensor", lambda e, rho=rho, h=h: mmh(
                        e, rho, 0, h), ps_free[h] + [setup_done])
                    ev_bm16[h] = pg.add("scalar", lambda e, h=h: e.copy(
                        H(xn16, h), psH[h]), [ev_mmB[h]] + xn16_free[h])
                ev_t1 = pg.add("vector", lambda e, ab=ab, xnq=xnq: e.scalar_tensor_tensor(
                    bbxb[:], ab[:], -1.0, xnq, OP.add, OP.mult),
                    [ev_ab] + buf_loaded[par] + t4_free)
                buf_cons[par].append(ev_t1)
                ev_bx0 = pg.add("vector", lambda e: e.tensor_mul(
                    H(bbxb, 0), H(bbxb, 0), H(xn16, 0)), [ev_t1, ev_bm16[0]])
                xn16_free[0] = [ev_bx0]
                ev_h0 = pg.add("vector", lambda e, ab=ab, j=j: e.tensor_tensor_scan(
                    H(htb, 0), H(ab, 0), H(bbxb, 0), init4[:, j:j + 1],
                    OP.mult, OP.add), [ev_bx0, ev_cmb] + ht_free[0])
                ev_bx1 = pg.add("vector", lambda e: e.tensor_mul(
                    H(bbxb, 1), H(bbxb, 1), H(xn16, 1)), [ev_t1, ev_bm16[1]])
                xn16_free[1] = [ev_bx1]
                ev_mmC = {}
                ev_mmC[0] = pg.add("tensor", lambda e, rho=rho: mmh(e, rho, 32, 0),
                                   [ev_bm16[0]])
                if j == 0:
                    cy_deps0 = ([last_y] if last_y is not None else [])
                    ev_cy0 = pg.add("vector", lambda e, gcol=gcol: e.scalar_tensor_tensor(
                        H(ybuf16, 0), H(htb, 0), ainv[:, gcol:gcol + 1], psH[0],
                        OP.mult, OP.mult), [ev_h0, ev_mmC[0]] + cy_deps0)
                    ev_y0 = ev_cy0
                else:
                    ev_cy0 = pg.add("vector", lambda e, gcol=gcol: e.scalar_tensor_tensor(
                        H(bbxb, 0), H(htb, 0), ainv[:, gcol:gcol + 1], psH[0],
                        OP.mult, OP.mult), [ev_h0, ev_mmC[0]])
                    ev_y0 = pg.add("gpsimd", lambda e: e.tensor_add(
                        H(ybuf16, 0), H(ybuf16, 0), H(bbxb, 0)), [ev_cy0])
                ev_h1 = pg.add("vector", lambda e, ab=ab: e.tensor_tensor_scan(
                    H(htb, 1), H(ab, 1), H(bbxb, 1), htb[:, HW2 - 1:HW2],
                    OP.mult, OP.add), [ev_bx1, ev_h0] + ht_free[1])
                ab_free[j % 2] = ev_h1
                ev_st = pg.add("gpsimd", lambda e, gcol=gcol: e.tensor_copy(
                    stcols[:, gcol:gcol + 1], htb[:, TOK - 1:TOK]), [ev_h1])
                st_evs.append(ev_st)
                ev_mmC[1] = pg.add("tensor", lambda e, rho=rho: mmh(e, rho, 32, 1),
                                   [ev_bm16[1]])
                if j == 0:
                    ev_cy1 = pg.add("vector", lambda e, gcol=gcol: e.scalar_tensor_tensor(
                        H(ybuf16, 1), H(htb, 1), ainv[:, gcol:gcol + 1], psH[1],
                        OP.mult, OP.mult), [ev_h1, ev_mmC[1]] + cy_deps0)
                    ev_y1 = ev_cy1
                    t4_free = [ev_h1]
                else:
                    ev_cy1 = pg.add("vector", lambda e, gcol=gcol: e.scalar_tensor_tensor(
                        H(bbxb, 1), H(htb, 1), ainv[:, gcol:gcol + 1], psH[1],
                        OP.mult, OP.mult), [ev_h1, ev_mmC[1]])
                    ev_y1 = pg.add("gpsimd", lambda e: e.tensor_add(
                        H(ybuf16, 1), H(ybuf16, 1), H(bbxb, 1)), [ev_cy1])
                    t4_free = [ev_y0, ev_y1]
                ht_free = {0: [ev_cy0], 1: [ev_cy1, ev_st]}
                ps_free = {0: [ev_cy0], 1: [ev_cy1]}
                last_ev_y = ev_y1
            if R <= 3 or not DYN_ON:
                ystore_prev = pg.add("sync", lambda e, R=R: e.dma_start(
                    out=s_y[R % 32, :, :], in_=ybuf16[:]), [last_ev_y], inc=16,
                    sem="yst")
            else:
                eng_nm = store_engine(R)
                sem_nm = "ysw0" if eng_nm == "gpsimd" else "yst"
                ystore_prev = pg.add(eng_nm, lambda e, R=R: dyn_store(
                    e, NSUB + 1 + R, ybuf16[:], s_y), [last_ev_y],
                    inc=16, sem=sem_nm)
            if AG_ON:
                pub_deps = list(st_evs)
                if ag_ev_prev is not None:
                    pub_deps.append(ag_ev_prev)
                ev_pub = pg.add("sync", lambda e, R=R: e.dma_start(
                    out=cc_in, in_=stcols[:, R * NQ:(R + 1) * NQ]), pub_deps, inc=16,
                    sem="pub")
                ag_deps = [ev_pub]
                if R - 1 in gather_load_evs:
                    ag_deps.append(gather_load_evs[R - 1])
                ev_ag = pg.add("gpsimd", lambda e: e.collective_compute(
                    "AllGather", OP.bypass, replica_groups=[list(range(8))],
                    ins=[cc_in], outs=[cc_out]), ag_deps, sem="ag")
                goff_w = (R % 2) * 8 * NQ
                gl_deps = [ev_ag]
                if R in cmb_evs:
                    pass
                if R - 2 >= 0 and R in cmb_evs:
                    pass
                gl_deps.append(cmb_evs[R])
                gather_load_evs[R] = pg.add("sync", lambda e, goff_w=goff_w: e.dma_start(
                    out=gathbuf[:, goff_w:goff_w + 8 * NQ].rearrange(
                        "p (k q) -> p k q", k=8),
                    in_=cc_out.rearrange("(k p) q -> p k q", p=128)), gl_deps, inc=16,
                    sem="gath")
                ag_ev_prev = ev_ag
            else:
                gather_load_evs[R] = st_evs[-1]

        p2_done = [("vector", pg.count["vector"]), ("gpsimd", pg.count["gpsimd"]),
                   ("sync", pg.count["sync"]), ("scalar", pg.count["scalar"]),
                   ("tensor", pg.count["tensor"]), ("yst", pg.count["yst"]),
                   ("ysw0", pg.count["ysw0"]), ("ag", pg.count["ag"]),
                   ("slot0", pg.count["slot0"]), ("slot1", pg.count["slot1"]),
                   ("pub", pg.count["pub"]), ("gath", pg.count["gath"])]

        prev_reuse = p2_done
        p3_free_x16 = p2_done
        p3_free_xn16 = p2_done
        p3_free_psc = p2_done
        p3_free_bx = p2_done
        p3_ln_tail = p2_done
        for c in range(NCH if PHASE3 else 0):
            c0 = c * Lc
            ld_deps = (list(p2_done) if c == 0 else
                       [p3_free_bx[0], p3_ln_tail[0]])
            ev_ld = pg.add(
                "sync", lambda e, c0=c0: e.dma_start(
                    out=b_x[:, :],
                    in_=s_xmid[:, :, c0:c0 + Lc].rearrange("d p l -> p d l")),
                ld_deps, inc=16)
            ld_evs = [ev_ld] * NDB
            ysum_evs = []
            for db in range(NDB):
                q_evs = [pg.add(
                    "sync", lambda e, db=db, c0=c0: e.dma_start(
                        out=ybuf16[:, 0:NQ * Lc],
                        in_=s_y[db * NQ:(db + 1) * NQ, :, c0:c0 + Lc].rearrange(
                            "q p l -> p q l")),
                    (prev_reuse if c == 0 else p3_yst_guard) + ysum_evs[-2:],
                    inc=16)]
                ev_xn = pg.add(
                    "sync", lambda e, db=db, c0=c0: e.dma_start(
                        out=dbs(b_t3, db), in_=s_dtxn[db, :, TOK + c0:TOK + c0 + Lc]),
                    (prev_reuse if c == 0 else [ysum_evs[-1] if ysum_evs else p3_ln_tail[0]]),
                    inc=16)
                ev_a1 = pg.add("vector", lambda e: e.tensor_add(
                    ybuf16[:, 4 * Lc:5 * Lc], ybuf16[:, 0:Lc], ybuf16[:, Lc:2 * Lc]),
                    q_evs + [ev_xn] + ysum_evs[-2:])
                ev_a2 = pg.add("vector", lambda e: e.tensor_add(
                    ybuf16[:, 5 * Lc:6 * Lc], ybuf16[:, 2 * Lc:3 * Lc],
                    ybuf16[:, 3 * Lc:4 * Lc]), q_evs)
                ev_s = pg.add("vector", lambda e, db=db: e.tensor_add(
                    dbs(b_y, db), ybuf16[:, 4 * Lc:5 * Lc], ybuf16[:, 5 * Lc:6 * Lc]),
                    [ev_a1, ev_a2])
                ev_s = pg.add("vector", lambda e, db=db: e.scalar_tensor_tensor(
                    dbs(b_y, db), dbs(b_t3, db), d_sb[:, db:db + 1], dbs(b_y, db),
                    OP.mult, OP.add), [ev_s, ev_xn])
                ev_s = pg.add("vector", lambda e, db=db: e.tensor_add(
                    dbs(b_x, db), dbs(b_x, db), dbs(b_y, db)), [ev_s, ld_evs[db]])
                ysum_evs.append(ev_s)
            xn_evs3, ev_mm3 = layernorm(b_x, b_xn, xn16, ysum_evs, ysum_evs,
                                        x16_free=p3_free_x16, sq_free=p3_free_xn16)
            x_evs3 = ffn(xn_evs3, b_x, ysum_evs, [xn_evs3[-1]],
                         psc_free=p3_free_psc)
            ost_evs = []
            for db in range(NDB):
                ost_evs.append(pg.add(
                    "sync", lambda e, db=db, c0=c0: e.dma_start(
                        out=out_h[db, :, c0:c0 + Lc], in_=dbs(b_x, db)),
                    [x_evs3[db], xn_evs3[-1]], inc=16, sem="ost"))
            p3_yst_guard = [ysum_evs[-1]]
            p3_free_x16 = [ev_mm3]
            p3_free_xn16 = [x_evs3[-1]]
            p3_free_psc = [x_evs3[-1]]
            p3_free_bx = [ost_evs[-1]] * NDB
            p3_ln_tail = [xn_evs3[-1]]
            prev_reuse = []

        for semk, eng in [("sync", "sync"), ("gpsimd", "gpsimd"), ("yst", "sync"),
                          ("ysw0", "gpsimd"), ("ag", "gpsimd"),
                          ("slot0", "sync"), ("slot1", "sync"), ("pub", "sync"),
                          ("gath", "sync"), ("spill", "sync"), ("ost", "sync"),
                          ("spx", "sync"), ("spn", "sync"), ("spd", "sync")]:
            lastv = pg.count[semk]
            if lastv:
                pg.add(eng, lambda e, semk=semk, lastv=lastv: e.wait_ge(
                    pg.sems[semk], lastv) and None, [], inc=0)

        block = E(nc.Block())
        pg.emit(block)
    return nc


def _prep_inputs(x_full, inp):
    B, Lb, _ = x_full.shape
    T = B * Lb
    x_fm = np.ascontiguousarray(x_full.reshape(T, DM).T)
    A_dm_n = inp["L0_attn_A"].reshape(DM, NST).astype(np.float32)
    W_in = inp["L0_attn_W_in"].astype(np.float32)
    b_in = inp["L0_attn_b_in"].astype(np.float32)
    W_dt1 = inp["L0_attn_W_dt1"].astype(np.float32)
    W_dt2 = inp["L0_attn_W_dt2"].astype(np.float32)
    base_w = inp["L0_ffn_base_w"].astype(np.float32)
    gate_w = inp["L0_ffn_gate_w"].astype(np.float32)
    lora_A = inp["L0_ffn_lora_A"].astype(np.float32)
    lora_B = inp["L0_ffn_lora_B"].astype(np.float32)
    Dv = inp["L0_attn_D"].astype(np.float32)
    b_dt2 = inp["L0_attn_b_dt2"].astype(np.float32).reshape(-1)
    base_b = inp["L0_ffn_base_b"].astype(np.float32).reshape(-1)
    a_cat = lora_A.transpose(2, 0, 1).reshape(DM, ERK)
    b_cat = lora_B.transpose(0, 2, 1).reshape(ERK, DM)
    sel_one = np.concatenate(
        [np.eye(NST, dtype=np.float32)[:, n:n + 1] * np.ones((1, 128), np.float32)
         for n in range(NST)], axis=1)
    shared = {
        "ones_col": np.full((128, 1), 1.0 / DM, np.float32),
        "ones_row": np.ones((1, 128), np.float32),
        "ident": np.eye(128, dtype=np.float32),
        "expand16": np.kron(np.eye(NEX, dtype=np.float32), np.ones((1, 16), np.float32)) * 2.0,
        "sel16": np.concatenate([sel_one, np.zeros((NST, NST * 128), np.float32),
                                 sel_one], axis=0),
        "fconsts": np.tile(np.array([[1e-5, -0.5]], np.float32), (128, 1)),
        "b_dt1": inp["L0_attn_b_dt1"].reshape(1, -1).astype(np.float32),
    }
    maps = []
    for core in range(8):
        p = core % 4
        perm_n = np.empty(NST, np.int64)
        for a in range(4):
            for j in range(4):
                perm_n[4 * a + j] = 4 * ((a - 2 * p) % 4) + j
        m = dict(shared)
        m["x"] = np.ascontiguousarray(
            x_fm[:, core * TOK:(core + 1) * TOK]).reshape(NDB, 128, TOK)
        m["w_in"] = np.concatenate(
            [W_in[:, perm_n], W_in[:, NST + perm_n]], axis=1)
        m["b_in"] = np.concatenate([b_in[perm_n], b_in[NST + perm_n]]).reshape(1, -1)
        m["w_dt1"] = W_dt1
        m["w_dt2"] = W_dt2
        m["base_wT"] = base_w.T
        m["gate_wT"] = gate_w.T
        m["a_cat"] = a_cat
        m["b_cat"] = b_cat
        m["b_dt2"] = b_dt2.reshape(1, -1)
        m["base_b"] = base_b.reshape(1, -1)
        m["D_sb"] = Dv.reshape(NDB, 128).T
        a2 = np.empty((128, NSUB * NQ), np.float32)
        sel = np.zeros((128, NSUB * 8), np.float32)
        slot_tbl = np.zeros((1, NSUB + 1), np.int32)
        y_tbl = np.zeros((1, NSUB), np.int32)
        for R in range(NSUB):
            g = (R - 2 * p) % 32
            phys_db = g // NQ
            phys_n0 = NQ * (g % NQ)
            slot_tbl[0, R] = phys_db
            y_tbl[0, R] = g
            for j in range(NQ):
                a2[:, R * NQ + j] = A_dm_n[phys_db * 128:(phys_db + 1) * 128,
                                           phys_n0 + j]
            if p > 0:
                sel[:, R * 8 + (core - 1)] = 1.0
        m["a2_sb"] = np.ascontiguousarray(a2)
        m["seltab"] = np.ascontiguousarray(sel)
        m["slot_tbl"] = slot_tbl
        m["y_tbl"] = y_tbl
        maps.append({k: (np.ascontiguousarray(v) if v.dtype == np.int32
                         else np.ascontiguousarray(v, np.float32))
                     for k, v in m.items()})
    return maps, None


def _unperm(core):
    return np.arange(DM, dtype=np.int64)


def kernel(**inputs):
    x_full = np.asarray(inputs["L0_attn_x"], np.float32)
    B, Lb, _ = x_full.shape
    maps, _ = _prep_inputs(x_full, {k: np.asarray(v, np.float32) for k, v in inputs.items()})
    nc = build_program()
    res = run_bass_kernel_spmd(nc, maps, core_ids=list(range(8)))
    T = B * Lb
    out_fm = np.empty((DM, T), np.float32)
    for core in range(8):
        perm_d = _unperm(core)
        out_fm[perm_d, core * TOK:(core + 1) * TOK] = \
            res.results[core]["out"].reshape(DM, TOK)
    return np.ascontiguousarray(out_fm.T).reshape(B, Lb, DM)


# version 14
# speedup vs baseline: 1.1832x; 1.0044x over previous
import contextlib
import numpy as np
import concourse.bass as bass
import concourse.mybir as mybir
from concourse.bass_utils import run_bass_kernel_spmd

F32 = mybir.dt.float32
BF16 = mybir.dt.bfloat16
I32 = mybir.dt.int32
AF = mybir.ActivationFunctionType
OP = mybir.AluOpType
AX = mybir.AxisListType

DM = 1024
NDB = 8
NST = 16
RK = 64
NEX = 8
NGT = 10
ERK = 128
Lc = 512
TOK = 4096
NCH = TOK // Lc
NQ = 4
NSUB = NQ * NDB + 6
NPC = Lc

P1_CUT = 8
P1_RUN = NCH
P2_RUN = NSUB
PHASE3 = True
AG_ON = True
DYN_ON = True
NPieces = TOK // NPC


class Prog:
    ENGINES = ("sync", "tensor", "scalar", "vector", "gpsimd")

    def __init__(self, nc):
        self.nc = nc
        self.ops = {e: [] for e in self.ENGINES}
        self.count = {}
        self.ecount = {e: 0 for e in self.ENGINES}
        self.sems = {}
        self.extra_sems = []

    def make_sems(self, E):
        for e in self.ENGINES:
            self.sems[e] = E(self.nc.semaphore(f"sem_{e}"))
            self.count[e] = 0
        for s in self.extra_sems:
            self.sems[s] = E(self.nc.semaphore(f"sem_{s}"))
            self.count[s] = 0

    def add(self, engine, fn, deps=(), inc=1, sem=None):
        sem = sem or engine
        self.count[sem] += inc
        self.ecount[engine] += 1
        ev = (sem, self.count[sem])
        self.ops[engine].append((fn, tuple(deps), inc, sem, self.ecount[engine]))
        if not hasattr(self, "evmap"):
            self.evmap = {}
        self.evmap[ev] = (engine, self.ecount[engine])
        return ev

    def eev(self, engine):
        return (engine, self.count[engine])

    def emit(self, block):
        sems = self.sems

        def run(eng_name):
            def body(eng):
                waited = {s: 0 for s in self.count}
                drained = 0
                dma_engine = eng_name == "sync"
                for fn, deps, inc, sem, eidx in self.ops[eng_name]:
                    need = {}
                    se_need = 0
                    for (de, dc) in deps:
                        if de == eng_name and not dma_engine:
                            se_need = max(se_need, self.evmap.get((de, dc), (de, 0))[1])
                            continue
                        need[de] = max(need.get(de, 0), dc)
                    for de, dc in need.items():
                        if waited[de] < dc:
                            eng.wait_ge(sems[de], dc)
                            waited[de] = dc
                    if se_need > drained:
                        eng.drain()
                        drained = self.ecount_at(eng_name, eidx)
                    instr = fn(eng)
                    if inc:
                        assert instr is not None
                        instr.then_inc(sems[sem], inc)
            return body

        block.sync(run("sync"))
        block.tensor(run("tensor"))
        block.scalar(run("scalar"))
        block.vector(run("vector"))
        block.gpsimd(run("gpsimd"))

    def ecount_at(self, eng, eidx):
        return eidx - 1


def build_program():
    nc = bass.Bass("TRN2", target_bir_lowering=False)

    dr = lambda n, s, k="ExternalInput", dt=F32: nc.dram_tensor(n, s, dt, kind=k).ap()
    x_h = dr("x", [NDB, 128, TOK])
    out_h = dr("out", [NDB, 128, TOK], "ExternalOutput")
    win_h = dr("w_in", [DM, 2 * NST])
    wdt1_h = dr("w_dt1", [DM, RK])
    wdt2_h = dr("w_dt2", [RK, DM])
    basew_h = dr("base_wT", [DM, DM])
    gate_h = dr("gate_wT", [DM, NGT])
    acat_h = dr("a_cat", [DM, ERK])
    bcat_h = dr("b_cat", [ERK, DM])
    a2_h = dr("a2_sb", [128, NSUB * NQ])
    d_h = dr("D_sb", [128, NDB])
    bin_h = dr("b_in", [1, 2 * NST])
    bdt1_h = dr("b_dt1", [1, RK])
    bdt2_h = dr("b_dt2", [1, DM])
    baseb_h = dr("base_b", [1, DM])
    onescol_h = dr("ones_col", [128, 1])
    onesrow_h = dr("ones_row", [1, 128])
    ident_h = dr("ident", [128, 128])
    expand_h = dr("expand16", [NEX, 128])
    sel16_h = dr("sel16", [48, NST * 128])
    fc_h = dr("fconsts", [128, 2])
    seltab_h = dr("seltab", [128, NSUB * 8])
    slottab_h = dr("slot_tbl", [1, NSUB + 1], dt=I32)
    ytab_h = dr("y_tbl", [1, NSUB], dt=I32)
    s_dtxn = dr("s_dtxn", [NDB + 1, 128, 2 * TOK], "Internal")
    s_xmid = dr("s_xmid", [NDB, 128, TOK], "Internal")
    s_y = dr("s_y", [NQ * NDB, 128, TOK], "Internal", dt=BF16)
    cc_in = dr("cc_in", [128, NQ], "Internal")
    cc_out = dr("cc_out", [8 * 128, NQ], "Internal")

    with contextlib.ExitStack() as stack:
        E = stack.enter_context
        sb = lambda n, s, dt=F32: E(nc.sbuf_tensor(n, s, dt))

        bb0 = sb("bb0", [128, 2 * TOK])
        bb1 = sb("bb1", [128, 2 * TOK])
        b_x = bb0[:, 0:TOK]
        b_xn = bb0[:, TOK:2 * TOK]
        b_dt = bb1[:, 0:TOK]
        b_y = bb1[:, TOK:2 * TOK]
        b_t1 = sb("b_t1", [128, TOK])
        b_t2 = sb("b_t2", [128, TOK])
        b_t3 = sb("b_t3", [128, TOK])
        b_t4 = sb("b_t4", [128, TOK], BF16)
        ybuf16 = sb("ybuf16", [128, TOK], BF16)
        xn16 = sb("xn16", [128, TOK], BF16)
        x16b = sb("x16b", [128, TOK], BF16)
        xsq16 = xn16
        x16 = b_t4
        bcpack = sb("bcpack", [32 + NST, TOK], BF16)
        bmf = bcpack[0:NST, :]
        cmf = bcpack[32:32 + NST, :]
        r_mean = sb("r_mean", [1, Lc])
        r_msq = sb("r_msq", [1, Lc])
        r_rstd = sb("r_rstd", [1, Lc])
        r_mean16 = sb("r_mean16", [1, Lc], BF16)
        r_rstd16 = sb("r_rstd16", [1, Lc], BF16)
        gm1 = sb("gm1", [128, 4])
        gnm1 = sb("gnm1", [128, 4])
        gm2 = sb("gm2", [128, 4])
        ge = sb("ge", [128, 64])
        ge2 = None
        goh1 = sb("goh1", [128, 64])
        goh1n = None
        ge2 = b_t2[:, 0:64]
        goh1n = b_t2[:, 64:128]
        gtw = sb("gtw", [128, 64])
        gden = sb("gden", [128, 4])
        giz = sb("giz", [128, 4])
        gdf = sb("gdf", [128, 4])
        ew8 = sb("ew8", [128, 32])
        logit_sb = None
        ewfm16 = sb("ewfm16", [NEX, Lc], BF16)
        ewexp16 = sb("ewexp16", [128, Lc], BF16)
        zw16 = sb("zw16", [128, Lc], BF16)
        p116 = sb("p116", [RK, Lc], BF16)
        win16 = sb("win16", [128, NDB * 2 * NST], BF16)
        wdt1_16 = sb("wdt1_16", [128, NDB * RK], BF16)
        wdt2_16 = sb("wdt2_16", [RK, DM], BF16)
        basew16 = sb("basew16", [128, NDB * DM], BF16)
        gate16 = sb("gate16", [128, NDB * NGT], BF16)
        acat16 = sb("acat16", [128, NDB * ERK], BF16)
        bcat16 = sb("bcat16", [128, DM], BF16)
        a_sb = sb("a_sb", [128, NSUB * NQ])
        ainv = sb("ainv", [128, NSUB * NQ])
        d_sb = sb("d_sb", [128, NDB])
        seltab = sb("seltab_s", [128, NSUB * 8], BF16)
        tabs = sb("tabs", [1, 2 * NSUB + 1], I32)
        stcols = sb("stcols", [128, NSUB * NQ])
        gathbuf = sb("gathbuf", [128, 2 * 8 * NQ])
        init4 = sb("init4", [128, NQ])
        nbias = 2 * NST + RK + DM + DM
        braws = b_t1[0:1, 0:nbias]
        blo_f = b_t3[0:1, 0:nbias]
        bhi = sb("bhi16", [1, nbias], BF16)
        blo = sb("blo16", [1, nbias], BF16)
        onescol16 = sb("onescol16", [128, 1], BF16)
        onesrow16 = sb("onesrow16", [1, 128], BF16)
        onesL16_t = sb("onesL16s", [1, Lc], BF16)
        onesL16 = onesL16_t[0:1, :]
        onesL32 = onesL16
        ident_sb = sb("ident_sb", [128, 128])
        expand16 = sb("expand16sb", [NEX, 128], BF16)
        sel16sb = sb("sel16sb", [48, NST * 128], BF16)
        fc_sb = sb("fc_sb", [128, 2])

        ps8 = E(nc.psum_tensor("ps8", [128, TOK], F32))
        ps_st = ps8[0:33, 0 * Lc:1 * Lc]
        ps_a = ps8[:, 1 * Lc:2 * Lc]
        ps_b = ps8[:, 2 * Lc:3 * Lc]
        ps_c = ps8[:, 3 * Lc:4 * Lc]
        ps_d = ps8[:, 4 * Lc:5 * Lc]

        wstage = b_t2[:, 0:2048]
        pg = Prog(nc)
        pg.extra_sems = ["slot0", "slot1", "yst", "pub", "gath", "spill", "ost",
                         "ysw0", "ag", "spx", "spn", "spd"]
        pg.make_sems(E)

        OB_IN, OB_DT1, OB_DT2, OB_BB = 0, 2 * NST, 2 * NST + RK, 2 * NST + RK + DM

        def ld_dma(dst, src, deps=()):
            return pg.add("sync", lambda e, d=dst, s=src: e.dma_start(out=d, in_=s),
                          deps, inc=16)

        def vcast(dst, src, deps):
            return pg.add("vector", lambda e, d=dst, s=src: e.tensor_copy(d, s), deps)

        ld_dma(a_sb[:], a2_h)
        ld_dma(tabs[0:1, 0:NSUB + 1], slottab_h)
        ld_dma(tabs[0:1, NSUB + 1:2 * NSUB + 1], ytab_h)
        ld_dma(d_sb[:], d_h)
        ld_dma(ident_sb[:], ident_h)
        ld_dma(fc_sb[:], fc_h)
        ld_dma(braws[0:1, OB_IN:OB_IN + 2 * NST], bin_h)
        ld_dma(braws[0:1, OB_DT1:OB_DT1 + RK], bdt1_h)
        ld_dma(braws[0:1, OB_DT2:OB_DT2 + DM], bdt2_h)
        evA = ld_dma(braws[0:1, OB_BB:OB_BB + DM], baseb_h)
        pg.add("vector", lambda e: e.reciprocal(ainv[:], a_sb[:]), [evA])
        pg.add("vector", lambda e: e.memset(onesL16, 1.0), [])
        pg.add("vector", lambda e: e.memset(ge[:], 0.0), [])
        pg.add("vector", lambda e: e.memset(goh1[:], 0.0), [])
        pg.add("vector", lambda e: e.memset(stcols[:], 0.0), [])
        pg.add("vector", lambda e: e.memset(gathbuf[:], 0.0), [])
        ev = vcast(bhi[0:1, :], braws, [evA])
        ev = pg.add("vector", lambda e: e.tensor_sub(blo_f, braws, bhi[0:1, :]), [ev])
        ev_blo = vcast(blo[0:1, :], blo_f, [ev])
        pc = [ev_blo]

        def staged(dst, src, p_rows, p_cols):
            ev_l = ld_dma(wstage[0:p_rows, 0:p_cols], src, [pc[0]])
            pc[0] = vcast(dst, wstage[0:p_rows, 0:p_cols], [ev_l])

        staged(seltab[:], seltab_h, 128, NSUB * 8)
        staged(expand16[:], expand_h, NEX, 128)
        staged(sel16sb[0:48, :], sel16_h, 48, NST * 128)
        staged(onescol16[:], onescol_h, 128, 1)
        staged(onesrow16[:], onesrow_h, 1, 128)
        for k in range(NDB):
            staged(win16[:, k * 2 * NST:(k + 1) * 2 * NST],
                   win_h[k * 128:(k + 1) * 128, :], 128, 2 * NST)
        for k in range(NDB):
            staged(wdt1_16[:, k * RK:(k + 1) * RK],
                   wdt1_h[k * 128:(k + 1) * 128, :], 128, RK)
        for k in range(NDB):
            staged(basew16[:, k * DM:(k + 1) * DM],
                   basew_h[k * 128:(k + 1) * 128, :], 128, DM)
        for k in range(NDB):
            staged(gate16[:, k * NGT:(k + 1) * NGT],
                   gate_h[k * 128:(k + 1) * 128, :], 128, NGT)
        for k in range(NDB):
            staged(acat16[:, k * ERK:(k + 1) * ERK],
                   acat_h[k * 128:(k + 1) * 128, :], 128, ERK)
        staged(wdt2_16[:], wdt2_h, RK, DM)
        staged(bcat16[:], bcat_h, 128, DM)
        setup_done = ("vector", pg.count["vector"])

        def dbs(buf, db):
            return buf[:, db * Lc:(db + 1) * Lc]

        def layernorm(src, dst, dst16, in_evs, ps_guard, x16_free=(), sq_free=(),
                      sqbuf=None, cast_free=(), xbuf=None):
            sq = sqbuf if sqbuf is not None else xsq16
            xb = xbuf if xbuf is not None else x16
            ev_c = pg.add("vector", lambda e, xb=xb: e.tensor_copy(xb[:], src[:]),
                          list(in_evs) + list(x16_free))
            ev_sq = pg.add("scalar", lambda e, sq=sq, xb=xb: e.activation(sq[:], xb[:], AF.Square),
                           [ev_c] + list(sq_free))

            def stats(eng):
                for k in range(NDB):
                    eng.matmul(ps_st[0:1, :], onescol16[:], dbs(xb, k),
                               start=(k == 0), stop=(k == NDB - 1))
                ii = None
                for k in range(NDB):
                    ii = eng.matmul(ps_st[32:33, :], onescol16[:], dbs(sq, k),
                                    start=(k == 0), stop=(k == NDB - 1))
                return ii
            ev_mm = pg.add("tensor", stats, [ev_c, ev_sq, setup_done])
            ev_r0a = pg.add("scalar", lambda e: e.copy(r_mean[:], ps_st[0:1, :]), [ev_mm])
            ev_r0 = pg.add("scalar", lambda e: e.copy(r_msq[:], ps_st[32:33, :]), [ev_r0a])
            ev_v1 = pg.add("vector", lambda e: e.tensor_mul(r_rstd[:], r_mean[:], r_mean[:]), [ev_r0])
            ev_v2 = pg.add("vector", lambda e: e.tensor_sub(r_msq[:], r_msq[:], r_rstd[:]), [ev_v1])
            ev_s1 = pg.add("scalar", lambda e: e.activation(r_msq[:], r_msq[:], AF.Ln, bias=fc_sb[0:1, 0:1]), [ev_v2])
            ev_s2 = pg.add("scalar", lambda e: e.activation(r_rstd[:], r_msq[:], AF.Exp, scale=fc_sb[0:1, 1:2]), [ev_s1])
            ev_m16 = pg.add("vector", lambda e: e.tensor_copy(r_mean16[:], r_mean[:]), [ev_r0])
            ev_m16b = pg.add("vector", lambda e: e.tensor_copy(r_rstd16[:], r_rstd[:]), [ev_s2])
            ev_rep = pg.add("tensor", lambda e: (
                e.matmul(ps_a[:, :], onesrow16[:], r_mean16[:], start=True, stop=True),
                e.matmul(ps_b[:, :], onesrow16[:], r_rstd16[:], start=True, stop=True))[-1],
                [ev_m16, ev_m16b] + list(ps_guard))
            evs = []
            for db in range(NDB):
                e1 = pg.add("vector", lambda e, db=db: e.tensor_sub(dbs(dst, db), dbs(src, db), ps_a[:, :]), [ev_rep])
                e2 = pg.add("vector", lambda e, db=db: e.tensor_mul(dbs(dst, db), dbs(dst, db), ps_b[:, :]), [e1])
                e3 = pg.add("vector", lambda e, db=db: e.tensor_copy(dbs(dst16, db), dbs(dst, db)),
                            [e2] + list(cast_free))
                evs.append(e3)
            return evs, ev_mm

        def k1_into(eng, ps, rows_n, col_off, is_last):
            eng.matmul(ps[0:rows_n, :], bhi[0:1, col_off:col_off + rows_n],
                       onesL16, start=False, stop=False)
            return eng.matmul(ps[0:rows_n, :], blo[0:1, col_off:col_off + rows_n],
                              onesL32, start=False, stop=is_last)

        def ffn(xn_evs, x_buf, x_in_evs, ps_guard, psc_free=()):
            def gate_mm(eng):
                ii = None
                for k in range(NDB):
                    ii = eng.matmul(ps_c[0:NGT, :], gate16[:, k * NGT:(k + 1) * NGT],
                                    dbs(xn16, k), start=(k == 0), stop=(k == NDB - 1))
                return ii
            ev_g = pg.add("tensor", gate_mm, list(xn_evs) + list(psc_free))
            ev_ge = pg.add("scalar", lambda e: e.copy(b_y[0:NGT, 0:Lc], ps_c[0:NGT, :]), [ev_g])
            nblk = Lc // 128

            def transp(eng):
                ii = None
                for j in range(nblk):
                    ii = eng.transpose(ps_d[:, j * 16:j * 16 + NGT],
                                       b_y[0:NGT, j * 128:(j + 1) * 128],
                                       ident_sb[0:NGT, 0:NGT])
                return ii
            ev_t = pg.add("tensor", transp, [ev_ge, ev_g])

            def blkcols(buf, j, w=NGT):
                return buf[:, j * 16:j * 16 + w]
            for j in range(nblk):
                ev = pg.add("vector", lambda e, j=j: e.reduce_max(gm1[:, j:j + 1], blkcols(ps_d, j), axis=AX.X), [ev_t])
            for j in range(nblk):
                ev = pg.add("vector", lambda e, j=j: e.reduce_max(gnm1[:, j:j + 1], blkcols(ps_d, j), axis=AX.X, negate=True), [ev_t])
            ev_mx = ev
            evs_e = []
            for j in range(nblk):
                evs_e.append(pg.add("scalar", lambda e, j=j: e.activation(
                    blkcols(ge, j), blkcols(ps_d, j), AF.Exp, bias=gnm1[:, j:j + 1]), [ev_mx]))
            for j in range(nblk):
                ev = pg.add("vector", lambda e, j=j: e.tensor_scalar(
                    blkcols(goh1, j), blkcols(ps_d, j), gm1[:, j:j + 1], None, OP.is_equal),
                    [ev_mx] + evs_e)
            ev = pg.add("vector", lambda e: e.tensor_scalar(
                goh1n[:], goh1[:], -1.0, 1.0, OP.mult, OP.add), [ev])
            ev = pg.add("vector", lambda e: e.tensor_mul(ge2[:], ge[:], goh1n[:]), [ev] + evs_e)
            ev_e2 = ev
            for j in range(nblk):
                ev = pg.add("vector", lambda e, j=j: e.reduce_max(gm2[:, j:j + 1], blkcols(ge2, j), axis=AX.X), [ev_e2])
            ev_sp = pg.add("vector", lambda e: e.tensor_copy(giz[:], gm2[:]), [ev])
            for j in range(nblk):
                ev = pg.add("vector", lambda e, j=j: e.tensor_scalar(
                    blkcols(ge2, j), blkcols(ge2, j), gm2[:, j:j + 1], None, OP.is_equal), [ev_sp])
            for j in range(nblk):
                ev = pg.add("vector", lambda e, j=j: e.scalar_tensor_tensor(
                    blkcols(gtw, j), blkcols(ge2, j), gm2[:, j:j + 1], blkcols(goh1, j),
                    OP.mult, OP.add), [ev])
            for j in range(nblk):
                ev = pg.add("vector", lambda e, j=j: e.reduce_sum(
                    gden[:, j:j + 1], blkcols(gtw, j, NEX), axis=AX.X), [ev])
            ev = pg.add("vector", lambda e: e.tensor_single_scalar(giz[:], gden[:], 0.0, OP.is_equal), [ev])
            ev = pg.add("vector", lambda e: e.tensor_add(gdf[:], gden[:], giz[:]), [ev])
            ev = pg.add("vector", lambda e: e.reciprocal(gm2[:], gdf[:]), [ev])
            ev_sp2 = pg.add("vector", lambda e: e.tensor_copy(gdf[:], gm2[:]), [ev])
            evw = []
            for j in range(nblk):
                evw.append(pg.add("vector", lambda e, j=j: e.tensor_scalar(
                    ew8[:, j * 8:(j + 1) * 8], blkcols(gtw, j, NEX), gm2[:, j:j + 1], None,
                    OP.mult), [ev_sp2]))

            def transb(eng):
                ii = None
                for j in range(nblk):
                    ii = eng.transpose(ps_d[0:NEX, j * 128:(j + 1) * 128],
                                       ew8[:, j * 8:(j + 1) * 8], ident_sb[:, :])
                return ii
            ev_tb = pg.add("tensor", transb, evw)
            ev_efm = pg.add("scalar", lambda e: e.copy(ewfm16[:], ps_d[0:NEX, :]), [ev_tb])
            ev_ex = pg.add("tensor", lambda e: e.matmul(
                ps_d[:, :], expand16[:], ewfm16[:], start=True, stop=True), [ev_efm])
            ev_ex16 = pg.add("scalar", lambda e: e.copy(ewexp16[:], ps_d[:, :]), [ev_ex])

            def zmm(eng):
                ii = None
                for k in range(NDB):
                    ii = eng.matmul(ps_c[:, :], acat16[:, k * ERK:(k + 1) * ERK],
                                    dbs(xn16, k), start=(k == 0), stop=(k == NDB - 1))
                return ii
            ev_z = pg.add("tensor", zmm, list(xn_evs) + [ev_ge])
            ev_zw = pg.add("vector", lambda e: e.tensor_mul(zw16[:], ps_c[:, :], ewexp16[:]),
                           [ev_z, ev_ex16])
            x_evs = []
            prev_ep = {0: None, 1: None}
            for m in range(NDB):
                ps = ps_a if (m % 2 == 0) else ps_b

                def bmm(eng, m=m, ps=ps):
                    for k in range(NDB):
                        eng.matmul(ps[:, :], basew16[:, k * DM + m * 128:k * DM + (m + 1) * 128],
                                   dbs(xn16, k), start=(k == 0), stop=False)
                    eng.matmul(ps[:, :], bcat16[:, m * 128:(m + 1) * 128], zw16[:],
                               start=False, stop=False)
                    return k1_into(eng, ps, 128, OB_BB + m * 128, True)
                deps = list(xn_evs) + [ev_zw]
                if prev_ep[m % 2] is not None:
                    deps.append(prev_ep[m % 2])
                if m < 2:
                    deps += list(ps_guard)
                ev_b = pg.add("tensor", bmm, deps)
                ev_ep = pg.add("vector", lambda e, m=m, ps=ps: e.scalar_tensor_tensor(
                    dbs(x_buf, m), ps[:, :], 0.5, dbs(x_buf, m),
                    OP.mult, OP.add), [ev_b] + list(x_in_evs))
                prev_ep[m % 2] = ev_ep
                x_evs.append(ev_ep)
            return x_evs

        prev_reuse = [setup_done]
        free_bx = [setup_done] * NDB
        free_x16 = [setup_done]
        free_xn16 = [setup_done]
        free_sq1 = [setup_done]
        free_x16b = [setup_done]
        free_psc = [setup_done]
        free_bdt = [setup_done]
        free_bxn = [setup_done]
        guard1 = [setup_done]
        for c in range(P1_RUN):
            c0 = c * Lc
            ev_ld1 = pg.add(
                "sync", lambda e, c0=c0: e.dma_start(
                    out=b_x[:, :],
                    in_=x_h[:, :, c0:c0 + Lc].rearrange("d p l -> p d l")),
                [free_bx[0]], inc=16)
            ld_evs = [ev_ld1] * NDB
            x_ev = ld_evs[-1]
            if P1_CUT < 2:
                prev_reuse = ld_evs
                continue
            xn_evs, ev_mm1 = layernorm(b_x, b_xn, xn16, ld_evs, guard1,
                                       x16_free=free_x16b, sq_free=free_sq1,
                                       sqbuf=ybuf16, cast_free=free_xn16, xbuf=x16b)
            if P1_CUT < 3:
                prev_reuse = xn_evs
                continue
            x_evs = ffn(xn_evs, b_x, ld_evs, [xn_evs[-1]], psc_free=free_psc)
            if P1_CUT < 4:
                prev_reuse = x_evs
                continue
            xn_evs2, ev_mm2 = layernorm(b_x, b_xn, xn16, [x_evs[-1]] + x_evs, x_evs)
            if P1_CUT < 5:
                prev_reuse = xn_evs2
                continue
            sp_evs = []
            sp_xmid = []
            for db in range(NDB):
                sp_xmid.append(pg.add(
                    "sync", lambda e, db=db, c0=c0: e.dma_start(
                        out=s_xmid[db, :, c0:c0 + Lc], in_=dbs(b_x, db)),
                    [x_evs[db], xn_evs2[-1]], inc=16, sem="spx"))
            sp_xn = []
            for db in range(NDB):
                sp_xn.append(pg.add(
                    "sync", lambda e, db=db, c0=c0: e.dma_start(
                        out=s_dtxn[db, :, TOK + c0:TOK + c0 + Lc], in_=dbs(b_xn, db)),
                    [xn_evs2[db]], inc=16, sem="spn"))
            sp_evs = sp_xmid + sp_xn

            def bcmm(eng):
                for k in range(NDB):
                    eng.matmul(ps_c[0:NST, :], win16[:, k * 2 * NST:k * 2 * NST + NST],
                               dbs(xn16, k), start=(k == 0), stop=False)
                eng.matmul(ps_c[0:NST, :], bhi[0:1, OB_IN:OB_IN + NST],
                           onesL16, start=False, stop=False)
                eng.matmul(ps_c[0:NST, :], blo[0:1, OB_IN:OB_IN + NST],
                           onesL32, start=False, stop=True)
                for k in range(NDB):
                    eng.matmul(ps_c[32:32 + NST, :],
                               win16[:, k * 2 * NST + NST:(k + 1) * 2 * NST],
                               dbs(xn16, k), start=(k == 0), stop=False)
                eng.matmul(ps_c[32:32 + NST, :], bhi[0:1, OB_IN + NST:OB_IN + 2 * NST],
                           onesL16, start=False, stop=False)
                return eng.matmul(ps_c[32:32 + NST, :], blo[0:1, OB_IN + NST:OB_IN + 2 * NST],
                                  onesL32, start=False, stop=True)
            if P1_CUT < 6:
                prev_reuse = sp_evs + [xn_evs2[-1]]
                continue
            ev_bc = pg.add("tensor", bcmm, list(xn_evs2))
            ev_bm = pg.add("scalar", lambda e, c0=c0: e.copy(bmf[:, c0:c0 + Lc], ps_c[0:NST, :]), [ev_bc])
            ev_cm = pg.add("scalar", lambda e, c0=c0: e.copy(cmf[:, c0:c0 + Lc], ps_c[32:32 + NST, :]), [ev_bc])

            def p1mm(eng):
                for k in range(NDB):
                    eng.matmul(ps_c[0:RK, :], wdt1_16[:, k * RK:(k + 1) * RK],
                               dbs(xn16, k), start=(k == 0), stop=False)
                return k1_into(eng, ps_c, RK, OB_DT1, True)
            if P1_CUT < 7:
                prev_reuse = sp_evs + [xn_evs2[-1], ev_bm, ev_cm]
                continue
            ev_p1 = pg.add("tensor", p1mm, list(xn_evs2) + [ev_bm, ev_cm])
            ev_p116 = pg.add("scalar", lambda e: e.copy(p116[:], ps_c[0:RK, :]), [ev_p1])
            exp_evs = []
            prev = ev_p116
            for db in range(NDB):
                def dtmm(eng, db=db):
                    eng.matmul(ps_c[:, :], wdt2_16[:, db * 128:(db + 1) * 128], p116[:],
                               start=True, stop=False)
                    return k1_into(eng, ps_c, 128, OB_DT2 + db * 128, True)
                ev_m = pg.add("tensor", dtmm, [ev_p116, prev])
                ev_e = pg.add("scalar", lambda e, db=db: e.activation(
                    dbs(b_y, db), ps_c[:, :], AF.Exp), [ev_m])
                prev = ev_e
                exp_evs.append(ev_e)
            if P1_CUT < 8:
                prev_reuse = sp_evs + [xn_evs2[-1], exp_evs[-1]]
                continue
            dt_sp = []
            cl_evs = []
            for db in range(NDB):
                ev_l = pg.add("scalar", lambda e, db=db: e.activation(
                    dbs(b_dt, db), dbs(b_y, db), AF.Ln, bias=1.0),
                    [exp_evs[db]] + free_bdt)
                ev_cl = pg.add("vector", lambda e, db=db: e.tensor_scalar(
                    dbs(b_dt, db), dbs(b_dt, db), 1e-7, 1e6, OP.max, OP.min), [ev_l])
                cl_evs.append(ev_cl)
                dt_sp.append(pg.add(
                    "sync", lambda e, db=db, c0=c0: e.dma_start(
                        out=s_dtxn[db, :, c0:c0 + Lc], in_=dbs(b_dt, db)),
                    [ev_cl], inc=16, sem="spd"))
            free_bx = [sp_xmid[-1]] * NDB
            free_x16 = [ev_mm2]
            free_xn16 = [ev_p1]
            free_sq1 = [ev_mm1]
            free_x16b = [ev_mm1]
            free_psc = [exp_evs[-1]]
            free_bdt = [dt_sp[-1]]
            free_bxn = sp_xn
            guard1 = [sp_xn[-1], xn_evs2[-1]]
            prev_reuse = sp_evs + dt_sp + [xn_evs2[-1], cl_evs[-1], exp_evs[-1]]
        p1_done_s = ("sync", pg.count["sync"])
        p1_done_sp = ("spd", pg.count["spd"])
        p1_done_spx = ("spx", pg.count["spx"])
        p1_done_spn = ("spn", pg.count["spn"])
        p1_done_v = ("vector", pg.count["vector"])
        p1_done_sc = ("scalar", pg.count["scalar"])
        p1_done_t = ("tensor", pg.count["tensor"])

        abuf = [b_t1, b_t2]
        bbxb = b_t4
        htb = b_t3

        setup_g = [p1_done_s, p1_done_v, p1_done_sc, p1_done_t, p1_done_sp,
                   p1_done_spx, p1_done_spn]
        bbp = [bb0, bb1]

        _dyn_ctr = [0]

        def dyn_load(eng, tab_idx, dst, scratch):
            _dyn_ctr[0] += 1
            i = _dyn_ctr[0]
            with eng.register(f"ds{i}") as slot, eng.register(f"do{i}") as off:
                eng.reg_load(slot, tabs[0:1, tab_idx:tab_idx + 1])
                eng.reg_mul(off, slot, 128 * 2 * TOK)
                srcap = bass.AP(tensor=scratch.tensor, offset=off,
                                ap=[[2 * TOK, 128], [1, 2 * TOK]])
                return eng.dma_start(out=dst, in_=srcap)

        def dyn_store(eng, tab_idx, src_buf, scratch):
            _dyn_ctr[0] += 1
            i = _dyn_ctr[0]
            with eng.register(f"ds{i}") as slot, eng.register(f"do{i}") as off:
                eng.reg_load(slot, tabs[0:1, tab_idx:tab_idx + 1])
                eng.reg_mul(off, slot, 128 * TOK)
                dstap = bass.AP(tensor=scratch.tensor, offset=off,
                                ap=[[TOK, 128], [1, TOK]])
                return eng.dma_start(out=dstap, in_=src_buf)

        def load_engine(T):
            return "sync" if T < 27 else "scalar"

        def store_engine(R):
            return "scalar" if R < 16 else "gpsimd"

        buf_loaded = {0: [], 1: []}
        buf_loaded[0] = [pg.add("sync", lambda e: e.dma_start(
            out=bbp[0][:], in_=s_dtxn[0, :, :]), setup_g, inc=16, sem="slot0")]
        buf_cons = {0: list(setup_g), 1: list(setup_g)}
        ab_free = {0: None, 1: None}
        t4_free = []
        xn16_free = {0: [], 1: []}
        ht_free = {0: [], 1: []}
        ps_free = {0: [p1_done_v, p1_done_sc], 1: [p1_done_v, p1_done_sc]}
        ystore_prev = None
        gather_load_evs = {}
        cmb_evs = {}
        ag_ev_prev = None

        for R in range(P2_RUN):
            par = R % 2
            npar = 1 - par
            dtq = bbp[par][:, 0:TOK]
            xnq = bbp[par][:, TOK:2 * TOK]
            if R + 1 < P2_RUN:
                T = R + 1
                if T <= 3 or not DYN_ON:
                    lds = pg.add(load_engine(T) if DYN_ON else "sync",
                                 lambda e, npar=npar: e.dma_start(
                                     out=bbp[npar][:], in_=s_dtxn[0, :, :]),
                                 buf_cons[npar], inc=16, sem=f"slot{npar}")
                else:
                    lds = pg.add(load_engine(T),
                                 lambda e, T=T, npar=npar: dyn_load(
                                     e, T, bbp[npar][:], s_dtxn),
                                 buf_cons[npar], inc=16, sem=f"slot{npar}")
                buf_loaded[npar] = [lds]
                buf_cons[npar] = []
            if R < 2:
                ev_cmb = pg.add("vector", lambda e: e.memset(init4[:], 0.0), [])
            else:
                glev = gather_load_evs[R - 2]
                goff = ((R - 2) % 2) * 8 * NQ
                ev_cmb = pg.add("vector", lambda e: e.memset(init4[:], 0.0),
                                [glev])
                for k in range(8):
                    scol = R * 8 + k
                    ev_cmb = pg.add("vector", lambda e, k=k, scol=scol, goff=goff:
                                    e.scalar_tensor_tensor(
                        init4[:], gathbuf[:, goff + k * NQ:goff + (k + 1) * NQ],
                        seltab[:, scol:scol + 1],
                        init4[:], OP.mult, OP.add), [ev_cmb, glev])
            ev_cmb = pg.add("vector", lambda e: e.tensor_copy(giz[:], gm2[:]), [ev_cmb])
            cmb_evs[R] = ev_cmb

            st_evs = []
            last_y = ystore_prev
            last_ev_y = None
            HW2 = TOK // 2
            for j in range(NQ):
                gcol = R * NQ + j
                rho = NQ * (R % NQ) + j
                ab = abuf[j % 2]
                ab_deps = buf_loaded[par] + [ev_cmb]
                if ab_free[j % 2] is not None:
                    ab_deps.append(ab_free[j % 2])
                ev_ab = pg.add("scalar", lambda e, ab=ab, gcol=gcol, dtq=dtq: e.activation(
                    ab[:], dtq, AF.Exp, scale=a_sb[:, gcol:gcol + 1]), ab_deps)
                buf_cons[par].append(ev_ab)

                def mmh(eng, rho, base, h):
                    ii = None
                    for pc in range(4):
                        c0 = h * HW2 + pc * NPC
                        ii = eng.matmul(ps8[:, (h * 4 + pc) * NPC:(h * 4 + pc + 1) * NPC],
                                        sel16sb[base:base + NST, rho * 128:(rho + 1) * 128],
                                        (bmf if base == 0 else cmf)[:, c0:c0 + NPC],
                                        start=True, stop=True)
                    return ii

                def H(buf, h):
                    return buf[:, h * HW2:(h + 1) * HW2]

                psH = [ps8[:, 0:HW2], ps8[:, HW2:TOK]]
                ev_mmB = {}
                ev_bm16 = {}
                for h in (0, 1):
                    ev_mmB[h] = pg.add("tensor", lambda e, rho=rho, h=h: mmh(
                        e, rho, 0, h), ps_free[h] + [setup_done])
                    ev_bm16[h] = pg.add("scalar", lambda e, h=h: e.copy(
                        H(xn16, h), psH[h]), [ev_mmB[h]] + xn16_free[h])
                ev_t1 = pg.add("vector", lambda e, ab=ab, xnq=xnq: e.scalar_tensor_tensor(
                    bbxb[:], ab[:], -1.0, xnq, OP.add, OP.mult),
                    [ev_ab] + buf_loaded[par] + t4_free)
                buf_cons[par].append(ev_t1)
                ev_bx0 = pg.add("vector", lambda e: e.tensor_mul(
                    H(bbxb, 0), H(bbxb, 0), H(xn16, 0)), [ev_t1, ev_bm16[0]])
                xn16_free[0] = [ev_bx0]
                ev_h0 = pg.add("vector", lambda e, ab=ab, j=j: e.tensor_tensor_scan(
                    H(htb, 0), H(ab, 0), H(bbxb, 0), init4[:, j:j + 1],
                    OP.mult, OP.add), [ev_bx0, ev_cmb] + ht_free[0])
                ev_bx1 = pg.add("vector", lambda e: e.tensor_mul(
                    H(bbxb, 1), H(bbxb, 1), H(xn16, 1)), [ev_t1, ev_bm16[1]])
                xn16_free[1] = [ev_bx1]
                ev_mmC = {}
                ev_mmC[0] = pg.add("tensor", lambda e, rho=rho: mmh(e, rho, 32, 0),
                                   [ev_bm16[0]])
                if j == 0:
                    cy_deps0 = ([last_y] if last_y is not None else [])
                    ev_cy0 = pg.add("vector", lambda e, gcol=gcol: e.scalar_tensor_tensor(
                        H(ybuf16, 0), H(htb, 0), ainv[:, gcol:gcol + 1], psH[0],
                        OP.mult, OP.mult), [ev_h0, ev_mmC[0]] + cy_deps0)
                    ev_y0 = ev_cy0
                else:
                    ev_cy0 = pg.add("vector", lambda e, gcol=gcol: e.scalar_tensor_tensor(
                        H(bbxb, 0), H(htb, 0), ainv[:, gcol:gcol + 1], psH[0],
                        OP.mult, OP.mult), [ev_h0, ev_mmC[0]])
                    ev_y0 = pg.add("gpsimd", lambda e: e.tensor_add(
                        H(ybuf16, 0), H(ybuf16, 0), H(bbxb, 0)), [ev_cy0])
                ev_h1 = pg.add("vector", lambda e, ab=ab: e.tensor_tensor_scan(
                    H(htb, 1), H(ab, 1), H(bbxb, 1), htb[:, HW2 - 1:HW2],
                    OP.mult, OP.add), [ev_bx1, ev_h0] + ht_free[1])
                ab_free[j % 2] = ev_h1
                ev_st = pg.add("gpsimd", lambda e, gcol=gcol: e.tensor_copy(
                    stcols[:, gcol:gcol + 1], htb[:, TOK - 1:TOK]), [ev_h1])
                st_evs.append(ev_st)
                ev_mmC[1] = pg.add("tensor", lambda e, rho=rho: mmh(e, rho, 32, 1),
                                   [ev_bm16[1]])
                if j == 0:
                    ev_cy1 = pg.add("vector", lambda e, gcol=gcol: e.scalar_tensor_tensor(
                        H(ybuf16, 1), H(htb, 1), ainv[:, gcol:gcol + 1], psH[1],
                        OP.mult, OP.mult), [ev_h1, ev_mmC[1]] + cy_deps0)
                    ev_y1 = ev_cy1
                    t4_free = [ev_h1]
                else:
                    ev_cy1 = pg.add("vector", lambda e, gcol=gcol: e.scalar_tensor_tensor(
                        H(bbxb, 1), H(htb, 1), ainv[:, gcol:gcol + 1], psH[1],
                        OP.mult, OP.mult), [ev_h1, ev_mmC[1]])
                    ev_y1 = pg.add("gpsimd", lambda e: e.tensor_add(
                        H(ybuf16, 1), H(ybuf16, 1), H(bbxb, 1)), [ev_cy1])
                    t4_free = [ev_y0, ev_y1]
                ht_free = {0: [ev_cy0], 1: [ev_cy1, ev_st]}
                ps_free = {0: [ev_cy0], 1: [ev_cy1]}
                last_ev_y = ev_y1
            if R <= 3 or not DYN_ON:
                ystore_prev = pg.add("sync", lambda e, R=R: e.dma_start(
                    out=s_y[R % 32, :, :], in_=ybuf16[:]), [last_ev_y], inc=16,
                    sem="yst")
            else:
                eng_nm = store_engine(R)
                sem_nm = "ysw0" if eng_nm == "gpsimd" else "yst"
                ystore_prev = pg.add(eng_nm, lambda e, R=R: dyn_store(
                    e, NSUB + 1 + R, ybuf16[:], s_y), [last_ev_y],
                    inc=16, sem=sem_nm)
            if AG_ON:
                pub_deps = list(st_evs)
                if ag_ev_prev is not None:
                    pub_deps.append(ag_ev_prev)
                ev_pub = pg.add("sync", lambda e, R=R: e.dma_start(
                    out=cc_in, in_=stcols[:, R * NQ:(R + 1) * NQ]), pub_deps, inc=16,
                    sem="pub")
                ag_deps = [ev_pub]
                if R - 1 in gather_load_evs:
                    ag_deps.append(gather_load_evs[R - 1])
                ev_ag = pg.add("gpsimd", lambda e: e.collective_compute(
                    "AllGather", OP.bypass, replica_groups=[list(range(8))],
                    ins=[cc_in], outs=[cc_out]), ag_deps, sem="ag")
                goff_w = (R % 2) * 8 * NQ
                gl_deps = [ev_ag]
                if R in cmb_evs:
                    pass
                if R - 2 >= 0 and R in cmb_evs:
                    pass
                gl_deps.append(cmb_evs[R])
                gather_load_evs[R] = pg.add("sync", lambda e, goff_w=goff_w: e.dma_start(
                    out=gathbuf[:, goff_w:goff_w + 8 * NQ].rearrange(
                        "p (k q) -> p k q", k=8),
                    in_=cc_out.rearrange("(k p) q -> p k q", p=128)), gl_deps, inc=16,
                    sem="gath")
                ag_ev_prev = ev_ag
            else:
                gather_load_evs[R] = st_evs[-1]

        p2_done = [("vector", pg.count["vector"]), ("gpsimd", pg.count["gpsimd"]),
                   ("sync", pg.count["sync"]), ("scalar", pg.count["scalar"]),
                   ("tensor", pg.count["tensor"]), ("yst", pg.count["yst"]),
                   ("ysw0", pg.count["ysw0"]), ("ag", pg.count["ag"]),
                   ("slot0", pg.count["slot0"]), ("slot1", pg.count["slot1"]),
                   ("pub", pg.count["pub"]), ("gath", pg.count["gath"])]

        prev_reuse = p2_done
        p3_free_x16 = p2_done
        p3_free_xn16 = p2_done
        p3_free_psc = p2_done
        p3_free_bx = p2_done
        p3_ln_tail = p2_done
        for c in range(NCH if PHASE3 else 0):
            c0 = c * Lc
            ld_deps = (list(p2_done) if c == 0 else
                       [p3_free_bx[0], p3_ln_tail[0]])
            ev_ld = pg.add(
                "sync", lambda e, c0=c0: e.dma_start(
                    out=b_x[:, :],
                    in_=s_xmid[:, :, c0:c0 + Lc].rearrange("d p l -> p d l")),
                ld_deps, inc=16)
            ld_evs = [ev_ld] * NDB
            ysum_evs = []
            for db in range(NDB):
                q_evs = [pg.add(
                    "sync", lambda e, db=db, c0=c0: e.dma_start(
                        out=ybuf16[:, 0:NQ * Lc],
                        in_=s_y[db * NQ:(db + 1) * NQ, :, c0:c0 + Lc].rearrange(
                            "q p l -> p q l")),
                    (prev_reuse if c == 0 else p3_yst_guard) + ysum_evs[-2:],
                    inc=16)]
                if db == 0:
                    ev_xn = pg.add(
                        "sync", lambda e, c0=c0: e.dma_start(
                            out=b_t3[:, :],
                            in_=s_dtxn[0:NDB, :, TOK + c0:TOK + c0 + Lc].rearrange(
                                "d p l -> p d l")),
                        (prev_reuse if c == 0 else [p3_ln_tail[0]]), inc=16)
                ev_a1 = pg.add("vector", lambda e: e.tensor_add(
                    ybuf16[:, 4 * Lc:5 * Lc], ybuf16[:, 0:Lc], ybuf16[:, Lc:2 * Lc]),
                    q_evs + [ev_xn] + ysum_evs[-2:])
                ev_a2 = pg.add("vector", lambda e: e.tensor_add(
                    ybuf16[:, 5 * Lc:6 * Lc], ybuf16[:, 2 * Lc:3 * Lc],
                    ybuf16[:, 3 * Lc:4 * Lc]), q_evs)
                ev_s = pg.add("vector", lambda e, db=db: e.tensor_add(
                    dbs(b_y, db), ybuf16[:, 4 * Lc:5 * Lc], ybuf16[:, 5 * Lc:6 * Lc]),
                    [ev_a1, ev_a2])
                ev_s = pg.add("vector", lambda e, db=db: e.scalar_tensor_tensor(
                    dbs(b_y, db), dbs(b_t3, db), d_sb[:, db:db + 1], dbs(b_y, db),
                    OP.mult, OP.add), [ev_s, ev_xn])
                ev_s = pg.add("vector", lambda e, db=db: e.tensor_add(
                    dbs(b_x, db), dbs(b_x, db), dbs(b_y, db)), [ev_s, ld_evs[db]])
                ysum_evs.append(ev_s)
            xn_evs3, ev_mm3 = layernorm(b_x, b_xn, xn16, ysum_evs, ysum_evs,
                                        x16_free=p3_free_x16, sq_free=p3_free_xn16)
            x_evs3 = ffn(xn_evs3, b_x, ysum_evs, [xn_evs3[-1]],
                         psc_free=p3_free_psc)
            ost_evs = []
            for db in range(NDB):
                ost_evs.append(pg.add(
                    "sync", lambda e, db=db, c0=c0: e.dma_start(
                        out=out_h[db, :, c0:c0 + Lc], in_=dbs(b_x, db)),
                    [x_evs3[db], xn_evs3[-1]], inc=16, sem="ost"))
            p3_yst_guard = [ysum_evs[-1]]
            p3_free_x16 = [ev_mm3]
            p3_free_xn16 = [x_evs3[-1]]
            p3_free_psc = [x_evs3[-1]]
            p3_free_bx = [ost_evs[-1]] * NDB
            p3_ln_tail = [xn_evs3[-1]]
            prev_reuse = []

        for semk, eng in [("sync", "sync"), ("gpsimd", "gpsimd"), ("yst", "sync"),
                          ("ysw0", "gpsimd"), ("ag", "gpsimd"),
                          ("slot0", "sync"), ("slot1", "sync"), ("pub", "sync"),
                          ("gath", "sync"), ("spill", "sync"), ("ost", "sync"),
                          ("spx", "sync"), ("spn", "sync"), ("spd", "sync")]:
            lastv = pg.count[semk]
            if lastv:
                pg.add(eng, lambda e, semk=semk, lastv=lastv: e.wait_ge(
                    pg.sems[semk], lastv) and None, [], inc=0)

        block = E(nc.Block())
        pg.emit(block)
    return nc


def _prep_inputs(x_full, inp):
    B, Lb, _ = x_full.shape
    T = B * Lb
    x_fm = np.ascontiguousarray(x_full.reshape(T, DM).T)
    A_dm_n = inp["L0_attn_A"].reshape(DM, NST).astype(np.float32)
    W_in = inp["L0_attn_W_in"].astype(np.float32)
    b_in = inp["L0_attn_b_in"].astype(np.float32)
    W_dt1 = inp["L0_attn_W_dt1"].astype(np.float32)
    W_dt2 = inp["L0_attn_W_dt2"].astype(np.float32)
    base_w = inp["L0_ffn_base_w"].astype(np.float32)
    gate_w = inp["L0_ffn_gate_w"].astype(np.float32)
    lora_A = inp["L0_ffn_lora_A"].astype(np.float32)
    lora_B = inp["L0_ffn_lora_B"].astype(np.float32)
    Dv = inp["L0_attn_D"].astype(np.float32)
    b_dt2 = inp["L0_attn_b_dt2"].astype(np.float32).reshape(-1)
    base_b = inp["L0_ffn_base_b"].astype(np.float32).reshape(-1)
    a_cat = lora_A.transpose(2, 0, 1).reshape(DM, ERK)
    b_cat = lora_B.transpose(0, 2, 1).reshape(ERK, DM)
    sel_one = np.concatenate(
        [np.eye(NST, dtype=np.float32)[:, n:n + 1] * np.ones((1, 128), np.float32)
         for n in range(NST)], axis=1)
    shared = {
        "ones_col": np.full((128, 1), 1.0 / DM, np.float32),
        "ones_row": np.ones((1, 128), np.float32),
        "ident": np.eye(128, dtype=np.float32),
        "expand16": np.kron(np.eye(NEX, dtype=np.float32), np.ones((1, 16), np.float32)) * 2.0,
        "sel16": np.concatenate([sel_one, np.zeros((NST, NST * 128), np.float32),
                                 sel_one], axis=0),
        "fconsts": np.tile(np.array([[1e-5, -0.5]], np.float32), (128, 1)),
        "b_dt1": inp["L0_attn_b_dt1"].reshape(1, -1).astype(np.float32),
    }
    maps = []
    for core in range(8):
        p = core % 4
        perm_n = np.empty(NST, np.int64)
        for a in range(4):
            for j in range(4):
                perm_n[4 * a + j] = 4 * ((a - 2 * p) % 4) + j
        m = dict(shared)
        m["x"] = np.ascontiguousarray(
            x_fm[:, core * TOK:(core + 1) * TOK]).reshape(NDB, 128, TOK)
        m["w_in"] = np.concatenate(
            [W_in[:, perm_n], W_in[:, NST + perm_n]], axis=1)
        m["b_in"] = np.concatenate([b_in[perm_n], b_in[NST + perm_n]]).reshape(1, -1)
        m["w_dt1"] = W_dt1
        m["w_dt2"] = W_dt2
        m["base_wT"] = base_w.T
        m["gate_wT"] = gate_w.T
        m["a_cat"] = a_cat
        m["b_cat"] = b_cat
        m["b_dt2"] = b_dt2.reshape(1, -1)
        m["base_b"] = base_b.reshape(1, -1)
        m["D_sb"] = Dv.reshape(NDB, 128).T
        a2 = np.empty((128, NSUB * NQ), np.float32)
        sel = np.zeros((128, NSUB * 8), np.float32)
        slot_tbl = np.zeros((1, NSUB + 1), np.int32)
        y_tbl = np.zeros((1, NSUB), np.int32)
        for R in range(NSUB):
            g = (R - 2 * p) % 32
            phys_db = g // NQ
            phys_n0 = NQ * (g % NQ)
            slot_tbl[0, R] = phys_db
            y_tbl[0, R] = g
            for j in range(NQ):
                a2[:, R * NQ + j] = A_dm_n[phys_db * 128:(phys_db + 1) * 128,
                                           phys_n0 + j]
            if p > 0:
                sel[:, R * 8 + (core - 1)] = 1.0
        m["a2_sb"] = np.ascontiguousarray(a2)
        m["seltab"] = np.ascontiguousarray(sel)
        m["slot_tbl"] = slot_tbl
        m["y_tbl"] = y_tbl
        maps.append({k: (np.ascontiguousarray(v) if v.dtype == np.int32
                         else np.ascontiguousarray(v, np.float32))
                     for k, v in m.items()})
    return maps, None


def _unperm(core):
    return np.arange(DM, dtype=np.int64)


def kernel(**inputs):
    x_full = np.asarray(inputs["L0_attn_x"], np.float32)
    B, Lb, _ = x_full.shape
    maps, _ = _prep_inputs(x_full, {k: np.asarray(v, np.float32) for k, v in inputs.items()})
    nc = build_program()
    res = run_bass_kernel_spmd(nc, maps, core_ids=list(range(8)))
    T = B * Lb
    out_fm = np.empty((DM, T), np.float32)
    for core in range(8):
        perm_d = _unperm(core)
        out_fm[perm_d, core * TOK:(core + 1) * TOK] = \
            res.results[core]["out"].reshape(DM, TOK)
    return np.ascontiguousarray(out_fm.T).reshape(B, Lb, DM)
